# version 05
# baseline (speedup 1.0000x reference)
import math
import numpy as np
import ml_dtypes
import concourse.bass as bass
import concourse.mybir as mybir
from concourse.bass_utils import run_bass_kernel_spmd
from contextlib import ExitStack

F32 = mybir.dt.float32
BF16 = mybir.dt.bfloat16
I32 = mybir.dt.int32
AF = mybir.ActivationFunctionType
ALU = mybir.AluOpType
AX = mybir.AxisListType

NCORES = 8
S = 8192
T = S // NCORES
D = 2048
DC = 512
DR = 1024
NBLK = 8
KCH = 16
H = 16
SCALER = 1.0 / math.sqrt(192.0)
C_EXP = 126 << 23
EPS = 1e-5

ENGINES = ("sync", "gpsimd", "vector", "scalar", "tensor")


class Sched:

    def __init__(self, ctr, sems):
        self.q = {e: [] for e in ENGINES}
        self.ctr = ctr
        self.sems = sems

    def wait(self, eng, sname, val):
        if val > 0:
            self.q[eng].append(lambda e, s=self.sems[sname], v=val: e.wait_ge(s, v))

    def op(self, eng, fn, inc=None, n=1):
        if inc is None:
            self.q[eng].append(fn)
            return self.ctr.get("_", 0)
        self.ctr[inc] = self.ctr.get(inc, 0) + n
        cnt = self.ctr[inc]
        sem = self.sems[inc]

        def run(e, fn=fn, sem=sem, n=n):
            fn(e).then_inc(sem, n)

        self.q[eng].append(run)
        return cnt

    def val(self, name):
        return self.ctr.get(name, 0)

    def flush(self, blk):
        for ename in ENGINES:
            ops = self.q[ename]
            if not ops:
                continue

            def body(e, ops=ops):
                for f in ops:
                    f(e)

            getattr(blk, ename)(body)
        self.q = {e: [] for e in ENGINES}


def build_program(debug=False):
    nc = bass.Bass("TRN2", target_bir_lowering=False)

    xT_d = nc.dram_tensor("xT", [D, T], F32, kind="ExternalInput")
    out_d = nc.dram_tensor("out", [T, D], F32, kind="ExternalOutput")
    aw = {}
    for name, shp in (("Wc", [D, DC]), ("Wcq", [DC, D]), ("Wqr", [DC, DR]),
                      ("Wckv", [D, DC]), ("Wck", [DC, D]), ("Wkr", [D, DR]),
                      ("Wv", [DC, D]), ("Wo", [D, D])):
        aw[name] = nc.dram_tensor(name, shp, F32, kind="ExternalInput")
    bias_d = nc.dram_tensor("biases", [128, 88], F32, kind="ExternalInput")
    rope_d = nc.dram_tensor("rope_cs", [128, 2 * T], F32, kind="ExternalInput")
    rotm_d = nc.dram_tensor("rotm", [128, 128], F32, kind="ExternalInput")
    eyeb_d = nc.dram_tensor("eye_bf", [128, 128], BF16, kind="ExternalInput")
    eyef_d = nc.dram_tensor("eye_f32", [128, 128], F32, kind="ExternalInput")
    pk_d = {m: nc.dram_tensor(f"{m}_pk", [2048, 1024], I32, kind="ExternalInput")
            for m in ("g", "u", "d")}
    sc_d = {m: nc.dram_tensor(f"{m}_sc", [2048, 64], F32, kind="ExternalInput")
            for m in ("g", "u", "d")}
    w_in = {m: nc.dram_tensor(f"{m}_in", [16, 128, 16, 128], BF16) for m in ("g", "u", "d")}
    w_out = {m: nc.dram_tensor(f"{m}_out", [NBLK, 16, 128, 16, 128], BF16) for m in ("g", "u", "d")}

    dbg = {}
    if debug:
        dbg["x1T"] = nc.dram_tensor("dbg_x1T", [D, T], F32, kind="ExternalOutput")
        dbg["x2T"] = nc.dram_tensor("dbg_x2T", [D, T], F32, kind="ExternalOutput")
        dbg["yT"] = nc.dram_tensor("dbg_yT", [D, T], BF16, kind="ExternalOutput")
        dbg["wg"] = nc.dram_tensor("dbg_wg", [16, 128, 16, 128], BF16, kind="ExternalOutput")
        dbg["wd"] = nc.dram_tensor("dbg_wd", [16, 128, 16, 128], BF16, kind="ExternalOutput")
        dbg["q"] = nc.dram_tensor("dbg_q", [128, 3072], BF16, kind="ExternalOutput")
        dbg["k"] = nc.dram_tensor("dbg_k", [128, 3072], BF16, kind="ExternalOutput")
        dbg["v"] = nc.dram_tensor("dbg_v", [128, 2048], BF16, kind="ExternalOutput")
        dbg["sc"] = nc.dram_tensor("dbg_sc", [128, 256], F32, kind="ExternalOutput")
        dbg["es"] = nc.dram_tensor("dbg_es", [128, 256], F32, kind="ExternalOutput")
        dbg["at"] = nc.dram_tensor("dbg_at", [128, 256], BF16, kind="ExternalOutput")
        dbg["ot"] = nc.dram_tensor("dbg_ot", [128, 2048], BF16, kind="ExternalOutput")
        dbg["rs"] = nc.dram_tensor("dbg_rs", [128, 16], F32, kind="ExternalOutput")
        dbg["rsb"] = nc.dram_tensor("dbg_rsb", [128, 16], F32, kind="ExternalOutput")
        dbg["nm"] = nc.dram_tensor("dbg_nm", [128, 16], F32, kind="ExternalOutput")

    ctr = {}

    with ExitStack() as top:
        def sb(name, shape, dtype):
            return top.enter_context(nc.sbuf_tensor(name, shape, dtype))

        xt = sb("xt", [128, KCH, T], F32)
        y = sb("y", [128, KCH, T], BF16)
        sh = sb("sh", [128, 16384], BF16)
        biases = sb("biases_sb", [128, 88], F32)
        rotm = sb("rotm_sb", [128, 128], F32)
        eyeb = sb("eyeb_sb", [128, 128], BF16)
        eyef = sb("eyef_sb", [128, 128], F32)
        ones_c = sb("ones_c", [128, 1], F32)
        ones_r = sb("ones_r", [1, 128], F32)
        rows = sb("rows", [1, 3, 512], F32)

        shF = sh.bitcast(F32)
        inter = sh.rearrange("p (c t) -> p c t", c=KCH)

        P = [top.enter_context(nc.psum_tensor(f"P{i}", [128, 512], F32)) for i in range(6)]
        PT = [top.enter_context(nc.psum_tensor(f"PT{i}", [128, 1024], BF16)) for i in range(2)]
        PTf = [t.bitcast(F32) for t in PT]

        sems = {k: top.enter_context(nc.semaphore("s_" + k))
                for k in ("ld", "gd", "cc", "pe", "ve", "ac", "pre",
                          "pk0", "pk1", "st0", "st1",
                          "wg0", "wg1", "wu0", "wu1", "wd0", "wd1",
                          "wa0", "wa1", "os0", "os1")}

        bias_cols = {}
        off = 0
        for bn, w in (("bc", 4), ("bcq", 16), ("bqr", 8), ("bckv", 4),
                      ("bck", 16), ("bkr", 8), ("bv", 16), ("bo", 16)):
            bias_cols[bn] = off
            off += w

        def bias_ap(bn, chunk):
            c = bias_cols[bn] + chunk
            return biases[:, c:c + 1]

        pre_total = [0]

        def emit_ln_half(sch, t0, y_off, sq_sb, ve_guard=0, pe_guard=0, tw=512):
            q_pe = {}
            sch.wait("tensor", "pre", pre_total[0])
            sch.wait("scalar", "pre", pre_total[0])
            sch.wait("vector", "pre", pre_total[0])
            if ve_guard:
                sch.wait("tensor", "ve", ve_guard)
            if pe_guard:
                sch.wait("scalar", "pe", pe_guard)
            for ch in range(KCH):
                slot = ch % 2
                if ch >= 2:
                    sch.wait("scalar", "pe", q_pe[ch - 2])
                c_ac = sch.op("scalar",
                              lambda e, ch=ch, slot=slot: e.activation(
                                  sq_sb[:, slot, 0:tw], xt[:, ch, t0:t0 + tw], AF.Square),
                              inc="ac")
                sch.wait("tensor", "ac", c_ac)
                sch.op("tensor",
                       lambda e, ch=ch: e.matmul(P[4][0:1, 0:tw], ones_c[:], xt[:, ch, t0:t0 + tw],
                                                 start=(ch == 0), stop=(ch == KCH - 1)))
                q_pe[ch] = sch.op("tensor",
                                  lambda e, ch=ch, slot=slot: e.matmul(
                                      P[5][0:1, 0:tw], ones_c[:], sq_sb[:, slot, 0:tw],
                                      start=(ch == 0), stop=(ch == KCH - 1)),
                                  inc="pe")
            stats_done = q_pe[KCH - 1]
            sch.wait("vector", "pe", stats_done)
            sch.op("vector", lambda e: e.tensor_scalar(rows[:, 0, 0:tw], P[4][0:1, 0:tw], 1.0 / D, None, ALU.mult))
            sch.op("vector", lambda e: e.tensor_scalar(rows[:, 1, 0:tw], P[5][0:1, 0:tw], 1.0 / D, None, ALU.mult))
            sch.op("vector", lambda e: e.tensor_tensor(rows[:, 2, 0:tw], rows[:, 0, 0:tw], rows[:, 0, 0:tw], ALU.mult))
            sch.op("vector", lambda e: e.tensor_tensor(rows[:, 2, 0:tw], rows[:, 1, 0:tw], rows[:, 2, 0:tw], ALU.subtract))
            c_ve = sch.op("vector", lambda e: e.tensor_scalar(rows[:, 2, 0:tw], rows[:, 2, 0:tw], EPS, None, ALU.add),
                          inc="ve")
            sch.wait("scalar", "ve", c_ve)
            c_ac = sch.op("scalar", lambda e: e.activation(rows[:, 1, 0:tw], rows[:, 2, 0:tw], AF.Sqrt),
                          inc="ac")
            sch.wait("vector", "ac", c_ac)
            c_ve = sch.op("vector", lambda e: e.reciprocal(rows[:, 2, 0:tw], rows[:, 1, 0:tw]), inc="ve")
            sch.wait("tensor", "ve", c_ve)
            sch.op("tensor", lambda e: e.matmul(P[0][:, 0:tw], ones_r[:], rows[:, 0, 0:tw], start=True, stop=True))
            bc_done = sch.op("tensor", lambda e: e.matmul(P[1][:, 0:tw], ones_r[:], rows[:, 2, 0:tw], start=True, stop=True),
                             inc="pe")
            sch.wait("vector", "pe", bc_done)
            tmp = shF[:, 3072:3072 + tw]
            for ch in range(KCH):
                sch.op("vector", lambda e, ch=ch, tmp=tmp: e.tensor_tensor(
                    tmp, xt[:, ch, t0:t0 + tw], P[0][:, 0:tw], ALU.subtract))
                c_n = sch.op("vector", lambda e, ch=ch, tmp=tmp: e.tensor_tensor(
                    y[:, ch, y_off:y_off + tw], tmp, P[1][:, 0:tw], ALU.mult),
                    inc="ve" if ch == KCH - 1 else None)
            return c_n, stats_done

        def emit_ffn_app(sch, sq_sb, wr, g1, first, ln_done=None, dq_hook=None):
            if ln_done is None:
                n0, sd0 = emit_ln_half(sch, 0, 0, sq_sb)
                ln_done, _ = emit_ln_half(sch, 512, 512, sq_sb, ve_guard=n0, pe_guard=sd0)

            if first:
                sch.wait("sync", "cc", 2)
            dn_q = "gpsimd" if first else "sync"

            wload = {}
            pe_grp = {}
            gelu_at = {}
            inter_at = {}
            evac_at = {}
            down_pe_done = {}
            first_down = True

            for nb in range(NBLK):
                for mc in range(16):
                    gidx = nb * 16 + mc
                    if dq_hook is not None:
                        dq_hook(gidx)
                    slot = gidx % 2
                    for m in ("g", "u"):
                        if gidx >= 2:
                            sch.wait("sync", "pe", pe_grp[gidx - 2])
                        wload[(m, gidx)] = sch.op("sync", lambda e, m=m, nb=nb, mc=mc, slot=slot: e.dma_start(
                            out=wr[m][:, slot], in_=w_out[m][nb, mc]), inc="w%s%d" % (m, slot), n=16)
                    for tt in (0, 1):
                        sch.wait("tensor", "wg%d" % slot, wload[("g", gidx)])
                        sch.wait("tensor", "wu%d" % slot, wload[("u", gidx)])
                        if gidx == 0 and tt == 0:
                            sch.wait("tensor", "ve", ln_done)
                        if gidx >= 1:
                            sch.wait("tensor", "ac", gelu_at[(gidx - 1, tt)])
                            sch.wait("tensor", "ve", inter_at[(gidx - 1, tt)])
                        for k in range(KCH):
                            sch.op("tensor", lambda e, slot=slot, k=k, tt=tt: e.matmul(
                                P[tt][:, :], wr["g"][:, slot, k, :], y[:, k, tt * 512:(tt + 1) * 512],
                                start=(k == 0), stop=(k == KCH - 1)))
                        grp_pe = 0
                        for k in range(KCH):
                            grp_pe = sch.op("tensor", lambda e, slot=slot, k=k, tt=tt: e.matmul(
                                P[2 + tt][:, :], wr["u"][:, slot, k, :], y[:, k, tt * 512:(tt + 1) * 512],
                                start=(k == 0), stop=(k == KCH - 1)),
                                inc="pe" if k == KCH - 1 else None)
                        if tt == 1:
                            pe_grp[gidx] = grp_pe
                        sch.wait("scalar", "pe", grp_pe)
                        gelu_at[(gidx, tt)] = sch.op("scalar", lambda e, tt=tt: e.activation(
                            g1[:, tt, :], P[tt][:, :], AF.Gelu_apprx_tanh), inc="ac")
                        sch.wait("vector", "ac", gelu_at[(gidx, tt)])
                        if nb >= 1 and mc == 0 and tt == 0:
                            sch.wait("vector", "pe", down_pe_done[nb - 1])
                        inter_at[(gidx, tt)] = sch.op("vector", lambda e, mc=mc, tt=tt: e.tensor_tensor(
                            inter[:, mc, tt * 512:(tt + 1) * 512], g1[:, tt, :], P[2 + tt][:, :], ALU.mult),
                            inc="ve")
                for hc in range(16):
                    hgidx = nb * 16 + hc
                    slot = hgidx % 2
                    if first and first_down:
                        sch.wait(dn_q, "cc", 3)
                        first_down = False
                    if hgidx >= 2:
                        sch.wait(dn_q, "pe", pe_grp[("d", hgidx - 2)])
                    wload[("d", hgidx)] = sch.op(dn_q, lambda e, nb=nb, hc=hc, slot=slot: e.dma_start(
                        out=wr["d"][:, slot], in_=w_out["d"][nb, hc]), inc="wd%d" % slot, n=16)
                    for tt in (0, 1):
                        g = hgidx * 2 + tt
                        bank = 4 + (g % 2)
                        sch.wait("tensor", "wd%d" % slot, wload[("d", hgidx)])
                        if hc == 0 and tt == 0:
                            sch.wait("tensor", "ve", inter_at[(nb * 16 + 15, 1)])
                        if g >= 2:
                            sch.wait("tensor", "ve", evac_at[g - 2])
                        grp_pe = 0
                        for k in range(KCH):
                            grp_pe = sch.op("tensor", lambda e, slot=slot, k=k, tt=tt, bank=bank: e.matmul(
                                P[bank][:, :], wr["d"][:, slot, k, :], inter[:, k, tt * 512:(tt + 1) * 512],
                                start=(k == 0), stop=(k == KCH - 1)),
                                inc="pe" if k == KCH - 1 else None)
                        if tt == 1:
                            pe_grp[("d", hgidx)] = grp_pe
                        sch.wait("vector", "pe", grp_pe)
                        evac_at[g] = sch.op("vector", lambda e, hc=hc, tt=tt, bank=bank: e.scalar_tensor_tensor(
                            xt[:, hc, tt * 512:(tt + 1) * 512], P[bank][:, :], 0.5,
                            xt[:, hc, tt * 512:(tt + 1) * 512], ALU.mult, ALU.add),
                            inc="ve")
                down_pe_done[nb] = pe_grp[("d", nb * 16 + 15)]

        with ExitStack() as c0:
            pk_t = c0.enter_context(nc.sbuf_tensor("pk", [128, 2, 1024], I32))
            sc_t = c0.enter_context(nc.sbuf_tensor("sct", [128, 2, 64], F32))
            v0l = c0.enter_context(nc.sbuf_tensor("v0l", [128, 1024], I32))
            sl_t = c0.enter_context(nc.sbuf_tensor("sl", [128, 1024], I32))
            v8_t = c0.enter_context(nc.sbuf_tensor("v8", [128, 1024], F32))
            nat = c0.enter_context(nc.sbuf_tensor("nat", [128, 2, 2048], BF16))
            stage = c0.enter_context(nc.sbuf_tensor("stage", [128, 2, 16, 128], BF16))
            sq_sb = c0.enter_context(nc.sbuf_tensor("sq1", [128, 2, 512], F32))
            wr = {m: c0.enter_context(nc.sbuf_tensor(f"ring1{m}", [128, 2, 16, 128], BF16)) for m in ("g", "u", "d")}
            g1 = c0.enter_context(nc.sbuf_tensor("g1a", [128, 2, 512], F32))

            with nc.Block() as blk:
                sch = Sched(ctr, sems)
                sch.op("sync", lambda e: e.dma_start(out=xt[:], in_=xT_d.rearrange("(c p) t -> p c t", p=128)),
                       inc="pre", n=16)
                sch.op("sync", lambda e: e.dma_start(out=biases[:], in_=bias_d[:]), inc="pre", n=16)
                sch.op("sync", lambda e: e.dma_start(out=rotm[:], in_=rotm_d[:]), inc="pre", n=16)
                sch.op("sync", lambda e: e.dma_start(out=eyeb[:], in_=eyeb_d[:]), inc="pre", n=16)
                eye_ld = sch.op("sync", lambda e: e.dma_start(out=eyef[:], in_=eyef_d[:]), inc="pre", n=16)
                PRE_ALL = sch.val("pre")
                pre_total[0] = PRE_ALL
                sch.op("vector", lambda e: e.memset(ones_c[:], 1.0))
                sch.op("vector", lambda e: e.memset(ones_r[:], 1.0), inc="ve")

                n0_, sd0_ = emit_ln_half(sch, 0, 0, sq_sb)
                ffn1_ln, _ = emit_ln_half(sch, 512, 512, sq_sb, ve_guard=n0_, pe_guard=sd0_)

                dq_done = {}
                tp_done = {}
                pt_evac = {}
                stage_dma = {}

                def emit_dq_chunk(m, oc, gl, qeng):
                    slot = gl % 2
                    if gl >= 2:
                        sch.wait(qeng, "ve", dq_done[gl - 2])
                    pksem = "pk%d" % slot
                    sch.op(qeng, lambda e, m=m, oc=oc, slot=slot: e.dma_start(
                        out=pk_t[:, slot, :],
                        in_=pk_d[m].rearrange("(c p) j -> c p j", p=128)[oc]), inc=pksem, n=16)
                    c_ld = sch.op(qeng, lambda e, m=m, oc=oc, slot=slot: e.dma_start(
                        out=sc_t[:, slot, :],
                        in_=sc_d[m].rearrange("(c p) j -> c p j", p=128)[oc]), inc=pksem, n=16)
                    sch.wait("vector", pksem, c_ld)
                    if gl >= 2:
                        sch.wait("vector", "pe", tp_done[gl - 2])
                    pkap = pk_t[:, slot, :]
                    sc_b = sc_t.rearrange("p s (b one) -> p s b one", one=1)[:, slot].broadcast_to([128, 64, 16])
                    c_dq = 0
                    for par in (0, 1):
                        if par == 0:
                            sch.op("vector", lambda e, pkap=pkap: e.tensor_scalar(v0l[:], pkap, 7, 22, ALU.bitwise_and, ALU.logical_shift_left))
                            sch.op("vector", lambda e, pkap=pkap: e.tensor_scalar(sl_t[:], pkap, 8, 28, ALU.bitwise_and, ALU.logical_shift_left))
                        else:
                            sch.op("vector", lambda e, pkap=pkap: e.tensor_scalar(v0l[:], pkap, 4, None, ALU.logical_shift_right))
                            sch.op("vector", lambda e: e.tensor_scalar(v0l[:], v0l[:], 7, 22, ALU.bitwise_and, ALU.logical_shift_left))
                            sch.op("vector", lambda e, pkap=pkap: e.tensor_scalar(sl_t[:], pkap, 128, 24, ALU.bitwise_and, ALU.logical_shift_left))
                        sch.op("vector", lambda e: e.tensor_scalar(v0l[:], v0l[:], C_EXP, None, ALU.add))
                        v0f = v0l.bitcast(F32)
                        sch.op("vector", lambda e, v0f=v0f: e.tensor_scalar(v8_t[:], v0f[:], 1.0, 0.0, ALU.subtract, ALU.min))
                        sch.op("vector", lambda e, v0f=v0f: e.tensor_add(v8_t[:], v0f[:], v8_t[:]))
                        sch.op("vector", lambda e: e.tensor_tensor(v8_t.bitcast(I32)[:], v8_t.bitcast(I32)[:], sl_t[:], ALU.bitwise_xor))
                        natv = nat.rearrange("p s (b g two) -> p s b g two", two=2, g=16)[:, slot, :, :, par]
                        c_dq = sch.op("vector", lambda e, natv=natv, sc_b=sc_b: e.tensor_tensor(
                            natv, v8_t.rearrange("p (b g) -> p b g", g=16), sc_b, ALU.mult),
                            inc="ve" if par == 1 else None)
                    dq_done[gl] = c_dq
                    sch.wait("tensor", "ve", c_dq)
                    if gl == 0:
                        sch.wait("tensor", "pre", PRE_ALL)
                    for grp in range(2):
                        bank = grp
                        if gl > 0:
                            sch.wait("tensor", "ac", pt_evac[(gl - 1, grp)])
                        c_tp = 0
                        for kk in range(8):
                            k = grp * 8 + kk
                            c_tp = sch.op("tensor", lambda e, slot=slot, k=k, kk=kk, bank=bank: e.transpose(
                                PT[bank][:, kk * 128:(kk + 1) * 128],
                                nat[:, slot, k * 128:(k + 1) * 128], eyeb[:]),
                                inc="pe" if kk == 7 else None)
                        sch.wait("scalar", "pe", c_tp)
                        if gl >= 2:
                            sch.wait("scalar", "st%d" % slot, stage_dma[gl - 2])
                        pt_evac[(gl, grp)] = sch.op("scalar", lambda e, slot=slot, grp=grp, bank=bank: e.copy(
                            stage[:, slot, grp * 8:(grp + 1) * 8, :],
                            PT[bank].rearrange("p (g q) -> p g q", q=128)),
                            inc="ac")
                        if grp == 1:
                            tp_done[gl] = c_tp
                    sch.wait("scalar", "ac", pt_evac[(gl, 1)])
                    stage_dma[gl] = sch.op("scalar", lambda e, m=m, oc=oc, slot=slot: e.dma_start(
                        out=w_in[m][oc], in_=stage[:, slot]), inc="st%d" % slot, n=16)

                for mi, m in enumerate(("g", "u")):
                    for oc in range(16):
                        emit_dq_chunk(m, oc, mi * 16 + oc, "sync")
                    sch.wait("gpsimd", "st0", sch.val("st0"))
                    sch.wait("gpsimd", "st1", sch.val("st1"))
                    sch.op("gpsimd", lambda e, m=m: e.collective_compute(
                        "AllGather", ALU.bypass,
                        replica_groups=[list(range(NCORES))],
                        ins=[w_in[m].rearrange("a p k q -> (a p k) q")],
                        outs=[w_out[m].rearrange("n a p k q -> (n a p k) q")]),
                        inc="cc")

                def dq_hook(gidx):
                    if gidx < 16:
                        emit_dq_chunk("d", gidx, 32 + gidx, "scalar")
                        if gidx == 15:
                            sch.wait("gpsimd", "st0", sch.val("st0"))
                            sch.wait("gpsimd", "st1", sch.val("st1"))
                            sch.op("gpsimd", lambda e: e.collective_compute(
                                "AllGather", ALU.bypass,
                                replica_groups=[list(range(NCORES))],
                                ins=[w_in["d"].rearrange("a p k q -> (a p k) q")],
                                outs=[w_out["d"].rearrange("n a p k q -> (n a p k) q")]),
                                inc="cc")

                emit_ffn_app(sch, sq_sb, wr, g1, first=True, ln_done=ffn1_ln, dq_hook=dq_hook)
                sch.flush(blk)
        cc_all = ctr.get("cc", 0)


        if debug:
            with nc.Block() as blk:
                sch = Sched(ctr, sems)
                sch.op("sync", lambda e: e.dma_start(
                    out=dbg["x1T"].rearrange("(c p) t -> p c t", p=128), in_=xt[:]), inc="ld", n=16)
                sch.op("sync", lambda e: e.dma_start(
                    out=dbg["yT"].rearrange("(c p) t -> p c t", p=128), in_=y[:]), inc="ld", n=16)
                sch.op("sync", lambda e: e.dma_start(out=dbg["wg"][:], in_=w_out["g"][3]), inc="ld", n=16)
                c = sch.op("sync", lambda e: e.dma_start(out=dbg["wd"][:], in_=w_out["d"][3]), inc="ld", n=16)
                sch.wait("sync", "ld", c)
                sch.flush(blk)

        TQ = 256
        with ExitStack() as c2:
            sq_sb = c2.enter_context(nc.sbuf_tensor("sq2", [128, 2, TQ], F32))
            wra = c2.enter_context(nc.sbuf_tensor("wra", [128, 2, 16, 128], BF16))
            cqb = c2.enter_context(nc.sbuf_tensor("cqb", [128, 4, TQ], BF16))
            ckvb = c2.enter_context(nc.sbuf_tensor("ckvb", [128, 4, TQ], BF16))
            qcb = c2.enter_context(nc.sbuf_tensor("qcb", [128, 16, TQ], BF16))
            kcb = c2.enter_context(nc.sbuf_tensor("kcb", [128, 16, TQ], BF16))
            qrb = c2.enter_context(nc.sbuf_tensor("qrb", [128, 8, TQ], BF16))
            krb = c2.enter_context(nc.sbuf_tensor("krb", [128, 8, TQ], BF16))
            qr0 = c2.enter_context(nc.sbuf_tensor("qr0", [128, TQ], F32))
            rtm1 = c2.enter_context(nc.sbuf_tensor("rtm1", [128, TQ], F32))
            rtm2 = c2.enter_context(nc.sbuf_tensor("rtm2", [128, TQ], F32))
            sc256 = c2.enter_context(nc.sbuf_tensor("sc256", [128, 256], F32))
            es256 = c2.enter_context(nc.sbuf_tensor("es256", [128, 256], F32))
            nm16 = c2.enter_context(nc.sbuf_tensor("nm16", [128, 16], F32))
            rs16 = c2.enter_context(nc.sbuf_tensor("rs16", [128, 16], F32))
            rs16b = c2.enter_context(nc.sbuf_tensor("rs16b", [128, 16], F32))
            rs16c = c2.enter_context(nc.sbuf_tensor("rs16c", [128, 16], F32))
            at256 = c2.enter_context(nc.sbuf_tensor("at256", [128, 256], BF16))
            ao = c2.enter_context(nc.sbuf_tensor("ao", [128, 16, TQ], BF16))
            rope = c2.enter_context(nc.sbuf_tensor("rope_sb", [128, 2 * T], F32))
            vb = y.rearrange("p c (h t) -> p c h t", h=4)

            q_t = sh[:, 6144:9216]
            k_t = sh[:, 9216:12288]
            v_t = sh[:, 12288:14336]
            o_t = sh[:, 14336:16384]
            q192 = q_t.rearrange("p (i d) -> p i d", d=192)
            k192 = k_t.rearrange("p (i d) -> p i d", d=192)

            proj_specs = [
                ("cq", "Wc", "bc", "y", 16, 4, None),
                ("ckv", "Wckv", "bckv", "y", 16, 4, None),
                ("qc", "Wcq", "bcq", "cq", 4, 16, None),
                ("qr", "Wqr", "bqr", "cq", 4, 8, None),
                ("kc", "Wck", "bck", "ckv", 4, 16, None),
                ("kr", "Wkr", "bkr", "y", 16, 8, None),
                ("v", "Wv", "bv", "ckv", 4, 16, None),
            ]

            for tq in range(4):
                t0 = tq * TQ
                with nc.Block() as blk:
                    sch = Sched(ctr, sems)
                    if tq == 0:
                        rope_ld = sch.op("sync", lambda e: e.dma_start(out=rope[:], in_=rope_d[:]), inc="pre", n=16)
                        sch.wait("vector", "pre", rope_ld)
                    ln_done, _ = emit_ln_half(sch, t0, 0, sq_sb, tw=TQ)

                    dsts = {"cq": cqb, "ckv": ckvb, "qc": qcb, "kc": kcb,
                            "v": None, "qr": qrb, "kr": krb}

                    def vslice(m):
                        return vb[:, m, 1, :]

                    pbank_i = 0
                    bank_free = {}
                    dst_ready = {}
                    wcons = {}

                    def rhs_of(src, k):
                        if src == "y":
                            return y[:, k, 0:TQ]
                        if src == "cq":
                            return cqb[:, k, :]
                        return ckvb[:, k, :]

                    for (pname, wname, bname, ksrc, kch, mch, _unused) in proj_specs:
                        for m in range(mch):
                            slot = pbank_i % 2
                            if pbank_i >= 2:
                                sch.wait("gpsimd", "pe", wcons[pbank_i - 2])
                            c_w = sch.op("gpsimd", lambda e, wname=wname, kch=kch, m=m, slot=slot: e.dma_start(
                                out=wra[:, slot, 0:kch, :],
                                in_=aw[wname].rearrange("(k p) n -> p k n", p=128)[:, :, m * 128:(m + 1) * 128]),
                                inc="wa%d" % slot, n=16)
                            bank = 2 + (pbank_i % 4)
                            sch.wait("tensor", "wa%d" % slot, c_w)
                            if pname == "cq" and m == 0:
                                sch.wait("tensor", "ve", ln_done)
                            if bank in bank_free:
                                sch.wait("tensor", "ve", bank_free[bank])
                            if pname in ("qc", "qr") and m == 0:
                                sch.wait("tensor", "ve", dst_ready[("cq", 3)])
                            if pname in ("kc", "v") and m == 0:
                                sch.wait("tensor", "ve", dst_ready[("ckv", 3)])
                            grp = 0
                            for k in range(kch):
                                grp = sch.op("tensor", lambda e, slot=slot, k=k, ksrc=ksrc, bank=bank: e.matmul(
                                    P[bank][:, 0:TQ], wra[:, slot, k, :], rhs_of(ksrc, k),
                                    start=(k == 0), stop=(k == kch - 1)),
                                    inc="pe" if k == kch - 1 else None)
                            wcons[pbank_i] = grp
                            pbank_i += 1
                            sch.wait("vector", "pe", grp)
                            if pname in ("qr", "kr"):
                                c1v = sch.op("vector", lambda e, bank=bank, bname=bname, m=m: e.tensor_scalar(
                                    qr0[:], P[bank][:, 0:TQ], bias_ap(bname, m), None, ALU.add), inc="ve")
                                bank_free[bank] = c1v
                                sch.wait("tensor", "ve", c1v)
                                if ("rotfree",) in dst_ready:
                                    sch.wait("tensor", "ve", dst_ready[("rotfree",)])
                                c_rot = sch.op("tensor", lambda e: e.matmul(
                                    PTf[0][:, 0:TQ], rotm[:], qr0[:], start=True, stop=True), inc="pe")
                                sch.wait("vector", "pe", c_rot)
                                cosap = rope[:, t0:t0 + TQ]
                                sinap = rope[:, T + t0:T + t0 + TQ]
                                sch.op("vector", lambda e, cosap=cosap: e.tensor_tensor(rtm1[:], qr0[:], cosap, ALU.mult))
                                sch.op("vector", lambda e, sinap=sinap: e.tensor_tensor(rtm2[:], PTf[0][:, 0:TQ], sinap, ALU.mult))
                                dstap = (qrb if pname == "qr" else krb)[:, m, :]
                                c2v = sch.op("vector", lambda e, dstap=dstap: e.tensor_add(dstap, rtm1[:], rtm2[:]), inc="ve")
                                dst_ready[("rotfree",)] = c2v
                                dst_ready[(pname, m)] = c2v
                            else:
                                dstap = vslice(m) if pname == "v" else dsts[pname][:, m, :]
                                c1v = sch.op("vector", lambda e, dstap=dstap, bank=bank, bname=bname, m=m: e.tensor_scalar(
                                    dstap, P[bank][:, 0:TQ], bias_ap(bname, m), None, ALU.add), inc="ve")
                                bank_free[bank] = c1v
                                dst_ready[(pname, m)] = c1v
                    proj_all = sch.val("ve")

                    tp_ev = {}
                    prev_av = 0
                    ao_marks = (0, 0)
                    for st in range(TQ // 128):
                        s0 = st * 128

                        def tgroup(srcs, dst_builder, gi, split4=False, st=st, prev_av=prev_av):
                            bank = gi % 2
                            vm, am = tp_ev.get(bank, (0, 0))
                            sch.wait("tensor", "ve", vm)
                            sch.wait("tensor", "ac", am)
                            c_tp = 0
                            for idx, (srcap,) in enumerate(srcs):
                                c_tp = sch.op("tensor", lambda e, srcap=srcap, idx=idx, bank=bank: e.transpose(
                                    PT[bank][:, idx * 128:(idx + 1) * 128], srcap, eyeb[:]),
                                    inc="pe" if idx == 7 else None)
                            eng = "vector" if gi % 3 != 2 else "scalar"
                            sch.wait(eng, "pe", c_tp)
                            if eng == "scalar" and prev_av:
                                sch.wait(eng, "ve", prev_av)
                            if split4:
                                src_ap = PT[bank].rearrange("p (g two d) -> p g two d", two=2, d=64)
                            else:
                                src_ap = PT[bank].rearrange("p (g q) -> p g q", q=128)
                            if eng == "vector":
                                c = sch.op(eng, lambda e, b=dst_builder, s=src_ap: e.tensor_copy(b(), s), inc="ve")
                                tp_ev[bank] = (c, am)
                            else:
                                c = sch.op(eng, lambda e, b=dst_builder, s=src_ap: e.copy(b(), s), inc="ac")
                                tp_ev[bank] = (vm, c)
                            return tp_ev[bank]

                        sch.wait("tensor", "ve", proj_all)
                        gi = 0
                        marks = []
                        for half in range(2):
                            marks.append(tgroup([(qcb[:, half * 8 + j, s0:s0 + 128],) for j in range(8)],
                                                lambda half=half: q192[:, half * 8:(half + 1) * 8, 0:128], gi))
                            gi += 1
                        marks.append(tgroup([(qrb[:, j, s0:s0 + 128],) for j in range(8)],
                                            lambda: q192[:, :, 128:192].rearrange("p (a two) d -> p a two d", two=2),
                                            gi, split4=True))
                        gi += 1
                        for half in range(2):
                            marks.append(tgroup([(kcb[:, half * 8 + j, s0:s0 + 128],) for j in range(8)],
                                                lambda half=half: k192[:, half * 8:(half + 1) * 8, 0:128], gi))
                            gi += 1
                        marks.append(tgroup([(krb[:, j, s0:s0 + 128],) for j in range(8)],
                                            lambda: k192[:, :, 128:192].rearrange("p (a two) d -> p a two d", two=2),
                                            gi, split4=True))
                        gi += 1
                        for half in range(2):
                            marks.append(tgroup([(vb[:, half * 8 + j, 1, s0:s0 + 128],) for j in range(8)],
                                                lambda half=half: v_t.rearrange("p (j d) -> p j d", d=128)[:, half * 8:(half + 1) * 8, :], gi))
                            gi += 1
                        for vm, am in marks:
                            sch.wait("vector", "ve", vm)
                            sch.wait("vector", "ac", am)
                        prod = shF[:, 0:3072].rearrange("p (j d) -> p j d", d=192)
                        c_sc = 0
                        for i in range(H):
                            qi = q192[:, i:i + 1, :].broadcast_to([128, H, 192])
                            sch.op("vector", lambda e, qi=qi, prod=prod: e.tensor_tensor(prod, qi, k192, ALU.mult))
                            c_sc = sch.op("vector", lambda e, i=i, prod=prod: e.tensor_reduce(
                                sc256[:, i * 16:(i + 1) * 16], prod, AX.X, ALU.add),
                                inc="ve" if i == H - 1 else None)
                        sch.wait("vector", "ve", c_sc)
                        sc3 = sc256.rearrange("p (i j) -> p i j", j=16)
                        c_r1 = sch.op("vector", lambda e, sc3=sc3: e.tensor_reduce(nm16[:], sc3, AX.X, ALU.max, negate=True), inc="ve")
                        sch.wait("vector", "ve", c_r1)
                        c_sm = sch.op("vector", lambda e, sc3=sc3: e.tensor_tensor(
                            sc3, sc3,
                            nm16.rearrange("p (i one) -> p i one", one=1).broadcast_to([128, 16, 16]),
                            ALU.add), inc="ve")
                        sch.wait("scalar", "ve", c_sm)
                        c_exp = sch.op("scalar", lambda e: e.activation(es256[:], sc256[:], AF.Exp, scale=SCALER), inc="ac")
                        sch.wait("vector", "ac", c_exp)
                        es3 = es256.rearrange("p (i j) -> p i j", j=16)
                        c_r2 = sch.op("vector", lambda e, es3=es3: e.tensor_reduce(rs16[:], es3, AX.X, ALU.add), inc="ve")
                        sch.wait("vector", "ve", c_r2)
                        c_r3 = sch.op("vector", lambda e: e.reciprocal(rs16b[:], rs16[:]), inc="ve")
                        sch.wait("vector", "ve", c_r3)
                        at3 = at256.rearrange("p (i j) -> p i j", j=16)
                        sch.op("vector", lambda e, es3=es3, at3=at3: e.tensor_tensor(
                            at3, es3,
                            rs16b.rearrange("p (i one) -> p i one", one=1).broadcast_to([128, 16, 16]),
                            ALU.mult))
                        vv = v_t.rearrange("p (j d) -> p d j", d=128)
                        prod2 = shF[:, 0:2048].rearrange("p (d j) -> p d j", j=16)
                        c_av = 0
                        def av_reduce(e, i, prod2):
                            with nc.allow_low_precision(reason="attn@v: fp32 accum, bf16 matmul operand out"):
                                return e.tensor_reduce(o_t[:, i * 128:(i + 1) * 128], prod2, AX.X, ALU.add)

                        for i in range(H):
                            ai = at3[:, i:i + 1, :].broadcast_to([128, 128, 16])
                            sch.op("vector", lambda e, ai=ai, vv=vv, prod2=prod2: e.tensor_tensor(prod2, ai, vv, ALU.mult))
                            c_av = sch.op("vector", lambda e, i=i, prod2=prod2: av_reduce(e, i, prod2),
                                inc="ve" if i == H - 1 else None)
                        prev_av = c_av
                        if debug and tq == 0 and st == 0:
                            sch.wait("sync", "ve", c_av)
                            for nm, src in (("q", q_t), ("k", k_t), ("v", v_t), ("o", None),
                                            ("sc", sc256), ("es", es256), ("at", at256)):
                                if nm == "o":
                                    continue
                                dk = {"q": "q", "k": "k", "v": "v", "sc": "sc", "es": "es", "at": "at"}[nm]
                                sch.op("sync", lambda e, dk=dk, src=src: e.dma_start(out=dbg[dk][:], in_=src[:]), inc="ld", n=16)
                            sch.op("sync", lambda e: e.dma_start(out=dbg["ot"][:], in_=o_t[:]), inc="ld", n=16)
                            sch.op("sync", lambda e: e.dma_start(out=dbg["rs"][:], in_=rs16[:]), inc="ld", n=16)
                            sch.op("sync", lambda e: e.dma_start(out=dbg["rsb"][:], in_=rs16b[:]), inc="ld", n=16)
                            sch.op("sync", lambda e: e.dma_start(out=dbg["nm"][:], in_=nm16[:]), inc="ld", n=16)
                            tap_tot = sch.val("ld")
                            for eng in ("vector", "scalar", "tensor"):
                                sch.wait(eng, "ld", tap_tot)
                        sch.wait("tensor", "ve", c_av)
                        for half in range(2):
                            bank = half
                            vm, am = tp_ev.get(bank, (0, 0))
                            sch.wait("tensor", "ve", vm)
                            sch.wait("tensor", "ac", am)
                            c_tp = 0
                            for idx in range(8):
                                mch = half * 8 + idx
                                c_tp = sch.op("tensor", lambda e, mch=mch, idx=idx, bank=bank: e.transpose(
                                    PT[bank][:, idx * 128:(idx + 1) * 128],
                                    o_t[:, mch * 128:(mch + 1) * 128], eyeb[:]),
                                    inc="pe" if idx == 7 else None)
                            eng = "vector" if half == 0 else "scalar"
                            sch.wait(eng, "pe", c_tp)
                            if eng == "scalar":
                                sch.wait(eng, "ve", c_av)
                            dstb = (lambda half=half, s0=s0: ao[:, half * 8:(half + 1) * 8, s0:s0 + 128])
                            src_ap = PT[bank].rearrange("p (g q) -> p g q", q=128)
                            if eng == "vector":
                                c = sch.op(eng, lambda e, dstb=dstb, s=src_ap: e.tensor_copy(dstb(), s), inc="ve")
                                tp_ev[bank] = (c, am)
                            else:
                                c = sch.op(eng, lambda e, dstb=dstb, s=src_ap: e.copy(dstb(), s), inc="ac")
                                tp_ev[bank] = (vm, c)
                        ao_marks = (tp_ev[0][0], tp_ev[1][1])

                    sch.wait("tensor", "ve", ao_marks[0])
                    sch.wait("tensor", "ac", ao_marks[1])
                    wo_pe = {}
                    wo_bank = {}
                    for m in range(16):
                        slot = m % 2
                        if m >= 2:
                            sch.wait("gpsimd", "pe", wo_pe[m - 2])
                        else:
                            sch.wait("gpsimd", "pe", wcons[pbank_i - 2 + m])
                        c_w = sch.op("gpsimd", lambda e, m=m, slot=slot: e.dma_start(
                            out=wra[:, slot, :, :],
                            in_=aw["Wo"].rearrange("(k p) n -> p k n", p=128)[:, :, m * 128:(m + 1) * 128]),
                            inc="wa%d" % slot, n=16)
                        bank = 2 + (m % 2)
                        sch.wait("tensor", "wa%d" % slot, c_w)
                        if bank in wo_bank:
                            sch.wait("tensor", "ve", wo_bank[bank])
                        grp = 0
                        for k in range(KCH):
                            grp = sch.op("tensor", lambda e, slot=slot, k=k, bank=bank: e.matmul(
                                P[bank][:, 0:TQ], wra[:, slot, k, :], ao[:, k, :],
                                start=(k == 0), stop=(k == KCH - 1)),
                                inc="pe" if k == KCH - 1 else None)
                        wo_pe[m] = grp
                        sch.wait("vector", "pe", grp)
                        c_r = sch.op("vector", lambda e, m=m, bank=bank, t0=t0: e.scalar_tensor_tensor(
                            xt[:, m, t0:t0 + TQ], P[bank][:, 0:TQ], bias_ap("bo", m),
                            xt[:, m, t0:t0 + TQ], ALU.add, ALU.add), inc="ve")
                        wo_bank[bank] = c_r
                    sch.flush(blk)

        if debug:
            with nc.Block() as blk:
                sch = Sched(ctr, sems)
                c = sch.op("sync", lambda e: e.dma_start(
                    out=dbg["x2T"].rearrange("(c p) t -> p c t", p=128), in_=xt[:]), inc="ld", n=16)
                sch.wait("sync", "ld", c)
                sch.flush(blk)

        with ExitStack() as c3:
            sq_sb = c3.enter_context(nc.sbuf_tensor("sq3", [128, 2, 512], F32))
            wr = {m: c3.enter_context(nc.sbuf_tensor(f"ring3{m}", [128, 2, 16, 128], BF16)) for m in ("g", "u", "d")}
            g1 = c3.enter_context(nc.sbuf_tensor("g1b", [128, 2, 512], F32))
            with nc.Block() as blk:
                sch = Sched(ctr, sems)
                emit_ffn_app(sch, sq_sb, wr, g1, first=False)
                sch.flush(blk)

        with ExitStack() as c4:
            ost = c4.enter_context(nc.sbuf_tensor("ost", [128, 2, D], F32))
            with nc.Block() as blk:
                sch = Sched(ctr, sems)
                ost_dma = {}
                pt_free = {}
                for st8 in range(8):
                    slot = st8 % 2
                    last_ev = 0
                    for grp4 in range(4):
                        bank = grp4 % 2
                        if bank in pt_free:
                            sch.wait("tensor", "ac", pt_free[bank])
                        c_tp = 0
                        for idx in range(4):
                            mch = grp4 * 4 + idx
                            c_tp = sch.op("tensor", lambda e, mch=mch, idx=idx, st8=st8, bank=bank: e.transpose(
                                PTf[bank][:, idx * 128:(idx + 1) * 128],
                                xt[:, mch, st8 * 128:(st8 + 1) * 128], eyef[:]),
                                inc="pe" if idx == 3 else None)
                        sch.wait("scalar", "pe", c_tp)
                        if st8 >= 2:
                            sch.wait("scalar", "os%d" % slot, ost_dma[st8 - 2])
                        last_ev = sch.op("scalar", lambda e, grp4=grp4, slot=slot, bank=bank: e.copy(
                            ost[:, slot, grp4 * 512:(grp4 + 1) * 512], PTf[bank][:, :]), inc="ac")
                        pt_free[bank] = last_ev
                    sch.wait("sync", "ac", last_ev)
                    ost_dma[st8] = sch.op("sync", lambda e, st8=st8, slot=slot: e.dma_start(
                        out=out_d[st8 * 128:(st8 + 1) * 128, :], in_=ost[:, slot]), inc="os%d" % slot, n=16)
                sch.wait("sync", "os0", sch.val("os0"))
                sch.wait("sync", "os1", sch.val("os1"))
                sch.flush(blk)

    return nc


_CACHE = {}


def _host_constants():
    j = np.arange(32, dtype=np.float32)
    inv_freq = (1.0 / (10000.0 ** (j / np.float32(32.0)))).astype(np.float32)
    dh = np.arange(128) % 64
    fr = inv_freq[dh % 32]
    R = np.zeros((128, 128), np.float32)
    for h2 in range(2):
        b = h2 * 64
        for dd in range(32):
            R[b + dd + 32, b + dd] = -1.0
            R[b + dd, b + dd + 32] = 1.0
    return fr, R


def kernel(**inputs):
    inp = {k: np.asarray(v) for k, v in inputs.items()}
    h_t = np.asarray(inp["L0_attn_h_t"], np.float32).reshape(S, D)

    fr, R = _host_constants()
    eye_bf = np.eye(128, dtype=ml_dtypes.bfloat16)
    eye_f = np.eye(128, dtype=np.float32)

    def pack_bias(v, chunks):
        return np.ascontiguousarray(np.asarray(v, np.float32).reshape(chunks, 128).T)

    bias_host = np.ascontiguousarray(np.concatenate([
        pack_bias(inp["L0_attn_bc"], 4),
        pack_bias(inp["L0_attn_bcq"], 16),
        pack_bias(inp["L0_attn_bqr"], 8),
        pack_bias(inp["L0_attn_bckv"], 4),
        pack_bias(inp["L0_attn_bck"], 16),
        pack_bias(inp["L0_attn_bkr"], 8),
        pack_bias(inp["L0_attn_bv"], 16),
        pack_bias(inp["L0_attn_bo"], 16),
    ], axis=1))

    if "nc" not in _CACHE:
        _CACHE["nc"] = build_program(debug=False)
    nc = _CACHE["nc"]

    in_maps = []
    for c in range(NCORES):
        t0 = c * T
        tpos = np.arange(t0, t0 + T, dtype=np.float32)
        ang = tpos[None, :] * fr[:, None]
        rope_cs = np.ascontiguousarray(
            np.concatenate([np.cos(ang), np.sin(ang)], axis=1).astype(np.float32))
        m = {
            "xT": np.ascontiguousarray(h_t[t0:t0 + T, :].T),
            "biases": bias_host,
            "rope_cs": rope_cs,
            "rotm": R,
            "eye_bf": eye_bf,
            "eye_f32": eye_f,
            "Wc": np.asarray(inp["L0_attn_Wc"], np.float32),
            "Wcq": np.asarray(inp["L0_attn_Wcq"], np.float32),
            "Wqr": np.asarray(inp["L0_attn_Wqr"], np.float32),
            "Wckv": np.asarray(inp["L0_attn_Wckv"], np.float32),
            "Wck": np.asarray(inp["L0_attn_Wck"], np.float32),
            "Wkr": np.asarray(inp["L0_attn_Wkr"], np.float32),
            "Wv": np.asarray(inp["L0_attn_Wv"], np.float32),
            "Wo": np.asarray(inp["L0_attn_Wo"], np.float32),
            "g_pk": np.ascontiguousarray(inp["L0_ffn_gate_W_packed"][c * 2048:(c + 1) * 2048, :]).astype(np.int32),
            "g_sc": np.ascontiguousarray(inp["L0_ffn_gate_scales"][c * 2048:(c + 1) * 2048, :]).astype(np.float32),
            "u_pk": np.ascontiguousarray(inp["L0_ffn_up_W_packed"][c * 2048:(c + 1) * 2048, :]).astype(np.int32),
            "u_sc": np.ascontiguousarray(inp["L0_ffn_up_scales"][c * 2048:(c + 1) * 2048, :]).astype(np.float32),
            "d_pk": np.ascontiguousarray(inp["L0_ffn_down_W_packed"][:, c * 1024:(c + 1) * 1024]).astype(np.int32),
            "d_sc": np.ascontiguousarray(inp["L0_ffn_down_scales"][:, c * 64:(c + 1) * 64]).astype(np.float32),
        }
        in_maps.append(m)

    res = run_bass_kernel_spmd(nc, in_maps, core_ids=list(range(NCORES)))
    out = np.concatenate([np.asarray(res.results[c]["out"]) for c in range(NCORES)], axis=0)
    return out.reshape(1, S, D).astype(np.float32)


# version 06
# speedup vs baseline: 1.0358x; 1.0358x over previous
import math
import numpy as np
import ml_dtypes
import concourse.bass as bass
import concourse.mybir as mybir
from concourse.bass_utils import run_bass_kernel_spmd
from contextlib import ExitStack

F32 = mybir.dt.float32
BF16 = mybir.dt.bfloat16
I32 = mybir.dt.int32
AF = mybir.ActivationFunctionType
ALU = mybir.AluOpType
AX = mybir.AxisListType

NCORES = 8
S = 8192
T = S // NCORES
D = 2048
DC = 512
DR = 1024
NBLK = 8
KCH = 16
H = 16
SCALER = 1.0 / math.sqrt(192.0)
C_EXP = 126 << 23
EPS = 1e-5

ENGINES = ("sync", "gpsimd", "vector", "scalar", "tensor")


class Sched:

    def __init__(self, ctr, sems):
        self.q = {e: [] for e in ENGINES}
        self.ctr = ctr
        self.sems = sems

    def wait(self, eng, sname, val):
        if val > 0:
            self.q[eng].append(lambda e, s=self.sems[sname], v=val: e.wait_ge(s, v))

    def op(self, eng, fn, inc=None, n=1):
        if inc is None:
            self.q[eng].append(fn)
            return self.ctr.get("_", 0)
        self.ctr[inc] = self.ctr.get(inc, 0) + n
        cnt = self.ctr[inc]
        sem = self.sems[inc]

        def run(e, fn=fn, sem=sem, n=n):
            fn(e).then_inc(sem, n)

        self.q[eng].append(run)
        return cnt

    def val(self, name):
        return self.ctr.get(name, 0)

    def flush(self, blk):
        for ename in ENGINES:
            ops = self.q[ename]
            if not ops:
                continue

            def body(e, ops=ops):
                for f in ops:
                    f(e)

            getattr(blk, ename)(body)
        self.q = {e: [] for e in ENGINES}


def build_program(debug=False):
    nc = bass.Bass("TRN2", target_bir_lowering=False)

    xT_d = nc.dram_tensor("xT", [D, T], F32, kind="ExternalInput")
    out_d = nc.dram_tensor("out", [T, D], F32, kind="ExternalOutput")
    aw = {}
    for name, shp in (("Wc", [D, DC]), ("Wcq", [DC, D]), ("Wqr", [DC, DR]),
                      ("Wckv", [D, DC]), ("Wck", [DC, D]), ("Wkr", [D, DR]),
                      ("Wv", [DC, D]), ("Wo", [D, D])):
        aw[name] = nc.dram_tensor(name, shp, F32, kind="ExternalInput")
    bias_d = nc.dram_tensor("biases", [128, 88], F32, kind="ExternalInput")
    rope_d = nc.dram_tensor("rope_cs", [128, 2 * T], F32, kind="ExternalInput")
    rotm_d = nc.dram_tensor("rotm", [128, 128], F32, kind="ExternalInput")
    eyeb_d = nc.dram_tensor("eye_bf", [128, 128], BF16, kind="ExternalInput")
    eyef_d = nc.dram_tensor("eye_f32", [128, 128], F32, kind="ExternalInput")
    pk_d = {m: nc.dram_tensor(f"{m}_pk", [2048, 1024], I32, kind="ExternalInput")
            for m in ("g", "u", "d")}
    sc_d = {m: nc.dram_tensor(f"{m}_sc", [2048, 64], F32, kind="ExternalInput")
            for m in ("g", "u", "d")}
    w_in = {m: nc.dram_tensor(f"{m}_in", [16, 128, 16, 128], BF16) for m in ("g", "u", "d")}
    w_out = {m: nc.dram_tensor(f"{m}_out", [4, NBLK, 4, 128, 16, 128], BF16) for m in ("g", "u", "d")}

    dbg = {}
    if debug:
        dbg["x1T"] = nc.dram_tensor("dbg_x1T", [D, T], F32, kind="ExternalOutput")
        dbg["x2T"] = nc.dram_tensor("dbg_x2T", [D, T], F32, kind="ExternalOutput")
        dbg["yT"] = nc.dram_tensor("dbg_yT", [D, T], BF16, kind="ExternalOutput")
        dbg["wg"] = nc.dram_tensor("dbg_wg", [16, 128, 16, 128], BF16, kind="ExternalOutput")
        dbg["wd"] = nc.dram_tensor("dbg_wd", [16, 128, 16, 128], BF16, kind="ExternalOutput")
        dbg["q"] = nc.dram_tensor("dbg_q", [128, 3072], BF16, kind="ExternalOutput")
        dbg["k"] = nc.dram_tensor("dbg_k", [128, 3072], BF16, kind="ExternalOutput")
        dbg["v"] = nc.dram_tensor("dbg_v", [128, 2048], BF16, kind="ExternalOutput")
        dbg["sc"] = nc.dram_tensor("dbg_sc", [128, 256], F32, kind="ExternalOutput")
        dbg["es"] = nc.dram_tensor("dbg_es", [128, 256], F32, kind="ExternalOutput")
        dbg["at"] = nc.dram_tensor("dbg_at", [128, 256], BF16, kind="ExternalOutput")
        dbg["ot"] = nc.dram_tensor("dbg_ot", [128, 2048], BF16, kind="ExternalOutput")
        dbg["rs"] = nc.dram_tensor("dbg_rs", [128, 16], F32, kind="ExternalOutput")
        dbg["rsb"] = nc.dram_tensor("dbg_rsb", [128, 16], F32, kind="ExternalOutput")
        dbg["nm"] = nc.dram_tensor("dbg_nm", [128, 16], F32, kind="ExternalOutput")

    ctr = {}

    with ExitStack() as top:
        def sb(name, shape, dtype):
            return top.enter_context(nc.sbuf_tensor(name, shape, dtype))

        xt = sb("xt", [128, KCH, T], F32)
        y = sb("y", [128, KCH, T], BF16)
        sh = sb("sh", [128, 16384], BF16)
        biases = sb("biases_sb", [128, 88], F32)
        rotm = sb("rotm_sb", [128, 128], F32)
        eyeb = sb("eyeb_sb", [128, 128], BF16)
        eyef = sb("eyef_sb", [128, 128], F32)
        ones_c = sb("ones_c", [128, 1], F32)
        ones_r = sb("ones_r", [1, 128], F32)
        rows = sb("rows", [1, 3, 512], F32)

        shF = sh.bitcast(F32)
        inter = sh.rearrange("p (c t) -> p c t", c=KCH)

        P = [top.enter_context(nc.psum_tensor(f"P{i}", [128, 512], F32)) for i in range(6)]
        PT = [top.enter_context(nc.psum_tensor(f"PT{i}", [128, 1024], BF16)) for i in range(2)]
        PTf = [t.bitcast(F32) for t in PT]

        sems = {k: top.enter_context(nc.semaphore("s_" + k))
                for k in ("ld", "gd", "cc", "pe", "ve", "ac", "pre",
                          "pk0", "pk1", "st0", "st1",
                          "wg0", "wg1", "wu0", "wu1", "wd0", "wd1",
                          "wa0", "wa1", "os0", "os1")}

        bias_cols = {}
        off = 0
        for bn, w in (("bc", 4), ("bcq", 16), ("bqr", 8), ("bckv", 4),
                      ("bck", 16), ("bkr", 8), ("bv", 16), ("bo", 16)):
            bias_cols[bn] = off
            off += w

        def bias_ap(bn, chunk):
            c = bias_cols[bn] + chunk
            return biases[:, c:c + 1]

        pre_total = [0]
        cc_at = {("g", 0): 1, ("u", 0): 2}
        _n = 2
        for _q in range(1, 4):
            _n += 1; cc_at[("g", _q)] = _n
            _n += 1; cc_at[("u", _q)] = _n
        for _q in range(4):
            _n += 1; cc_at[("d", _q)] = _n

        def emit_ln_half(sch, t0, y_off, sq_sb, ve_guard=0, pe_guard=0, tw=512):
            q_pe = {}
            sch.wait("tensor", "pre", pre_total[0])
            sch.wait("scalar", "pre", pre_total[0])
            sch.wait("vector", "pre", pre_total[0])
            if ve_guard:
                sch.wait("tensor", "ve", ve_guard)
            if pe_guard:
                sch.wait("scalar", "pe", pe_guard)
            for ch in range(KCH):
                slot = ch % 2
                if ch >= 2:
                    sch.wait("scalar", "pe", q_pe[ch - 2])
                c_ac = sch.op("scalar",
                              lambda e, ch=ch, slot=slot: e.activation(
                                  sq_sb[:, slot, 0:tw], xt[:, ch, t0:t0 + tw], AF.Square),
                              inc="ac")
                sch.wait("tensor", "ac", c_ac)
                sch.op("tensor",
                       lambda e, ch=ch: e.matmul(P[4][0:1, 0:tw], ones_c[:], xt[:, ch, t0:t0 + tw],
                                                 start=(ch == 0), stop=(ch == KCH - 1)))
                q_pe[ch] = sch.op("tensor",
                                  lambda e, ch=ch, slot=slot: e.matmul(
                                      P[5][0:1, 0:tw], ones_c[:], sq_sb[:, slot, 0:tw],
                                      start=(ch == 0), stop=(ch == KCH - 1)),
                                  inc="pe")
            stats_done = q_pe[KCH - 1]
            sch.wait("vector", "pe", stats_done)
            sch.op("vector", lambda e: e.tensor_scalar(rows[:, 0, 0:tw], P[4][0:1, 0:tw], 1.0 / D, None, ALU.mult))
            sch.op("vector", lambda e: e.tensor_scalar(rows[:, 1, 0:tw], P[5][0:1, 0:tw], 1.0 / D, None, ALU.mult))
            sch.op("vector", lambda e: e.tensor_tensor(rows[:, 2, 0:tw], rows[:, 0, 0:tw], rows[:, 0, 0:tw], ALU.mult))
            sch.op("vector", lambda e: e.tensor_tensor(rows[:, 2, 0:tw], rows[:, 1, 0:tw], rows[:, 2, 0:tw], ALU.subtract))
            c_ve = sch.op("vector", lambda e: e.tensor_scalar(rows[:, 2, 0:tw], rows[:, 2, 0:tw], EPS, None, ALU.add),
                          inc="ve")
            sch.wait("scalar", "ve", c_ve)
            c_ac = sch.op("scalar", lambda e: e.activation(rows[:, 1, 0:tw], rows[:, 2, 0:tw], AF.Sqrt),
                          inc="ac")
            sch.wait("vector", "ac", c_ac)
            c_ve = sch.op("vector", lambda e: e.reciprocal(rows[:, 2, 0:tw], rows[:, 1, 0:tw]), inc="ve")
            sch.wait("tensor", "ve", c_ve)
            sch.op("tensor", lambda e: e.matmul(P[0][:, 0:tw], ones_r[:], rows[:, 0, 0:tw], start=True, stop=True))
            bc_done = sch.op("tensor", lambda e: e.matmul(P[1][:, 0:tw], ones_r[:], rows[:, 2, 0:tw], start=True, stop=True),
                             inc="pe")
            sch.wait("vector", "pe", bc_done)
            tmp = shF[:, 3072:3072 + tw]
            for ch in range(KCH):
                sch.op("vector", lambda e, ch=ch, tmp=tmp: e.tensor_tensor(
                    tmp, xt[:, ch, t0:t0 + tw], P[0][:, 0:tw], ALU.subtract))
                c_n = sch.op("vector", lambda e, ch=ch, tmp=tmp: e.tensor_tensor(
                    y[:, ch, y_off:y_off + tw], tmp, P[1][:, 0:tw], ALU.mult),
                    inc="ve" if ch == KCH - 1 else None)
            return c_n, stats_done

        def emit_ffn_app(sch, sq_sb, wr, g1, first, ln_done=None, dq_hook=None):
            if ln_done is None:
                n0, sd0 = emit_ln_half(sch, 0, 0, sq_sb)
                ln_done, _ = emit_ln_half(sch, 512, 512, sq_sb, ve_guard=n0, pe_guard=sd0)

            dn_q = "gpsimd" if first else "sync"

            wload = {}
            pe_grp = {}
            gelu_at = {}
            inter_at = {}
            evac_at = {}
            down_pe_done = {}
            first_down = True

            for nb in range(NBLK):
                for mc in range(16):
                    gidx = nb * 16 + mc
                    if dq_hook is not None:
                        dq_hook(gidx)
                    slot = gidx % 2
                    if first and nb == 0:
                        sch.wait("sync", "cc", cc_at[("u", mc // 4)])
                    for m in ("g", "u"):
                        if gidx >= 2:
                            sch.wait("sync", "pe", pe_grp[gidx - 2])
                        wload[(m, gidx)] = sch.op("sync", lambda e, m=m, nb=nb, mc=mc, slot=slot: e.dma_start(
                            out=wr[m][:, slot], in_=w_out[m][mc // 4, nb, mc % 4]), inc="w%s%d" % (m, slot), n=16)
                    for tt in (0, 1):
                        sch.wait("tensor", "wg%d" % slot, wload[("g", gidx)])
                        sch.wait("tensor", "wu%d" % slot, wload[("u", gidx)])
                        if gidx == 0 and tt == 0:
                            sch.wait("tensor", "ve", ln_done)
                        if gidx >= 1:
                            sch.wait("tensor", "ac", gelu_at[(gidx - 1, tt)])
                            sch.wait("tensor", "ve", inter_at[(gidx - 1, tt)])
                        for k in range(KCH):
                            sch.op("tensor", lambda e, slot=slot, k=k, tt=tt: e.matmul(
                                P[tt][:, :], wr["g"][:, slot, k, :], y[:, k, tt * 512:(tt + 1) * 512],
                                start=(k == 0), stop=(k == KCH - 1)))
                        grp_pe = 0
                        for k in range(KCH):
                            grp_pe = sch.op("tensor", lambda e, slot=slot, k=k, tt=tt: e.matmul(
                                P[2 + tt][:, :], wr["u"][:, slot, k, :], y[:, k, tt * 512:(tt + 1) * 512],
                                start=(k == 0), stop=(k == KCH - 1)),
                                inc="pe" if k == KCH - 1 else None)
                        if tt == 1:
                            pe_grp[gidx] = grp_pe
                        sch.wait("scalar", "pe", grp_pe)
                        gelu_at[(gidx, tt)] = sch.op("scalar", lambda e, tt=tt: e.activation(
                            g1[:, tt, :], P[tt][:, :], AF.Gelu_apprx_tanh), inc="ac")
                        sch.wait("vector", "ac", gelu_at[(gidx, tt)])
                        if nb >= 1 and mc == 0 and tt == 0:
                            sch.wait("vector", "pe", down_pe_done[nb - 1])
                        inter_at[(gidx, tt)] = sch.op("vector", lambda e, mc=mc, tt=tt: e.tensor_tensor(
                            inter[:, mc, tt * 512:(tt + 1) * 512], g1[:, tt, :], P[2 + tt][:, :], ALU.mult),
                            inc="ve")
                for hc in range(16):
                    hgidx = nb * 16 + hc
                    slot = hgidx % 2
                    if first and nb == 0:
                        sch.wait(dn_q, "cc", cc_at[("d", hc // 4)])
                    if hgidx >= 2:
                        sch.wait(dn_q, "pe", pe_grp[("d", hgidx - 2)])
                    wload[("d", hgidx)] = sch.op(dn_q, lambda e, nb=nb, hc=hc, slot=slot: e.dma_start(
                        out=wr["d"][:, slot], in_=w_out["d"][hc // 4, nb, hc % 4]), inc="wd%d" % slot, n=16)
                    for tt in (0, 1):
                        g = hgidx * 2 + tt
                        bank = 4 + (g % 2)
                        sch.wait("tensor", "wd%d" % slot, wload[("d", hgidx)])
                        if hc == 0 and tt == 0:
                            sch.wait("tensor", "ve", inter_at[(nb * 16 + 15, 1)])
                        if g >= 2:
                            sch.wait("tensor", "ve", evac_at[g - 2])
                        grp_pe = 0
                        for k in range(KCH):
                            grp_pe = sch.op("tensor", lambda e, slot=slot, k=k, tt=tt, bank=bank: e.matmul(
                                P[bank][:, :], wr["d"][:, slot, k, :], inter[:, k, tt * 512:(tt + 1) * 512],
                                start=(k == 0), stop=(k == KCH - 1)),
                                inc="pe" if k == KCH - 1 else None)
                        if tt == 1:
                            pe_grp[("d", hgidx)] = grp_pe
                        sch.wait("vector", "pe", grp_pe)
                        evac_at[g] = sch.op("vector", lambda e, hc=hc, tt=tt, bank=bank: e.scalar_tensor_tensor(
                            xt[:, hc, tt * 512:(tt + 1) * 512], P[bank][:, :], 0.5,
                            xt[:, hc, tt * 512:(tt + 1) * 512], ALU.mult, ALU.add),
                            inc="ve")
                down_pe_done[nb] = pe_grp[("d", nb * 16 + 15)]

        with ExitStack() as c0:
            pk_t = c0.enter_context(nc.sbuf_tensor("pk", [128, 2, 1024], I32))
            sc_t = c0.enter_context(nc.sbuf_tensor("sct", [128, 2, 64], F32))
            v0l = c0.enter_context(nc.sbuf_tensor("v0l", [128, 1024], I32))
            sl_t = c0.enter_context(nc.sbuf_tensor("sl", [128, 1024], I32))
            v8_t = c0.enter_context(nc.sbuf_tensor("v8", [128, 1024], F32))
            nat = c0.enter_context(nc.sbuf_tensor("nat", [128, 2, 2048], BF16))
            stage = c0.enter_context(nc.sbuf_tensor("stage", [128, 2, 16, 128], BF16))
            sq_sb = c0.enter_context(nc.sbuf_tensor("sq1", [128, 2, 512], F32))
            wr = {m: c0.enter_context(nc.sbuf_tensor(f"ring1{m}", [128, 2, 16, 128], BF16)) for m in ("g", "u", "d")}
            g1 = c0.enter_context(nc.sbuf_tensor("g1a", [128, 2, 512], F32))

            with nc.Block() as blk:
                sch = Sched(ctr, sems)
                sch.op("sync", lambda e: e.dma_start(out=xt[:], in_=xT_d.rearrange("(c p) t -> p c t", p=128)),
                       inc="pre", n=16)
                sch.op("sync", lambda e: e.dma_start(out=biases[:], in_=bias_d[:]), inc="pre", n=16)
                sch.op("sync", lambda e: e.dma_start(out=rotm[:], in_=rotm_d[:]), inc="pre", n=16)
                sch.op("sync", lambda e: e.dma_start(out=eyeb[:], in_=eyeb_d[:]), inc="pre", n=16)
                eye_ld = sch.op("sync", lambda e: e.dma_start(out=eyef[:], in_=eyef_d[:]), inc="pre", n=16)
                PRE_ALL = sch.val("pre")
                pre_total[0] = PRE_ALL
                sch.op("vector", lambda e: e.memset(ones_c[:], 1.0))
                sch.op("vector", lambda e: e.memset(ones_r[:], 1.0), inc="ve")

                n0_, sd0_ = emit_ln_half(sch, 0, 0, sq_sb)
                ffn1_ln, _ = emit_ln_half(sch, 512, 512, sq_sb, ve_guard=n0_, pe_guard=sd0_)

                dq_done = {}
                tp_done = {}
                pt_evac = {}
                stage_dma = {}

                dq_hist = []

                def emit_dq_chunk(m, oc, gl, qeng):
                    dq_hist.append(gl)
                    slot = (len(dq_hist) - 1) % 2
                    if len(dq_hist) >= 3:
                        sch.wait(qeng, "ve", dq_done[dq_hist[-3]])
                    pksem = "pk%d" % slot
                    sch.op(qeng, lambda e, m=m, oc=oc, slot=slot: e.dma_start(
                        out=pk_t[:, slot, :],
                        in_=pk_d[m].rearrange("(c p) j -> c p j", p=128)[oc]), inc=pksem, n=16)
                    c_ld = sch.op(qeng, lambda e, m=m, oc=oc, slot=slot: e.dma_start(
                        out=sc_t[:, slot, :],
                        in_=sc_d[m].rearrange("(c p) j -> c p j", p=128)[oc]), inc=pksem, n=16)
                    sch.wait("vector", pksem, c_ld)
                    if len(dq_hist) >= 3:
                        sch.wait("vector", "pe", tp_done[dq_hist[-3]])
                    pkap = pk_t[:, slot, :]
                    sc_b = sc_t.rearrange("p s (b one) -> p s b one", one=1)[:, slot].broadcast_to([128, 64, 16])
                    c_dq = 0
                    for par in (0, 1):
                        if par == 0:
                            sch.op("vector", lambda e, pkap=pkap: e.tensor_scalar(v0l[:], pkap, 7, 22, ALU.bitwise_and, ALU.logical_shift_left))
                            sch.op("vector", lambda e, pkap=pkap: e.tensor_scalar(sl_t[:], pkap, 8, 28, ALU.bitwise_and, ALU.logical_shift_left))
                        else:
                            sch.op("vector", lambda e, pkap=pkap: e.tensor_scalar(v0l[:], pkap, 4, None, ALU.logical_shift_right))
                            sch.op("vector", lambda e: e.tensor_scalar(v0l[:], v0l[:], 7, 22, ALU.bitwise_and, ALU.logical_shift_left))
                            sch.op("vector", lambda e, pkap=pkap: e.tensor_scalar(sl_t[:], pkap, 128, 24, ALU.bitwise_and, ALU.logical_shift_left))
                        sch.op("vector", lambda e: e.tensor_scalar(v0l[:], v0l[:], C_EXP, None, ALU.add))
                        v0f = v0l.bitcast(F32)
                        sch.op("vector", lambda e, v0f=v0f: e.tensor_scalar(v8_t[:], v0f[:], 1.0, 0.0, ALU.subtract, ALU.min))
                        sch.op("vector", lambda e, v0f=v0f: e.tensor_add(v8_t[:], v0f[:], v8_t[:]))
                        sch.op("vector", lambda e: e.tensor_tensor(v8_t.bitcast(I32)[:], v8_t.bitcast(I32)[:], sl_t[:], ALU.bitwise_xor))
                        natv = nat.rearrange("p s (b g two) -> p s b g two", two=2, g=16)[:, slot, :, :, par]
                        c_dq = sch.op("vector", lambda e, natv=natv, sc_b=sc_b: e.tensor_tensor(
                            natv, v8_t.rearrange("p (b g) -> p b g", g=16), sc_b, ALU.mult),
                            inc="ve" if par == 1 else None)
                    dq_done[gl] = c_dq
                    sch.wait("tensor", "ve", c_dq)
                    if len(dq_hist) == 1:
                        sch.wait("tensor", "pre", PRE_ALL)
                    for grp in range(2):
                        bank = grp
                        if len(dq_hist) >= 2:
                            sch.wait("tensor", "ac", pt_evac[(dq_hist[-2], grp)])
                        c_tp = 0
                        for kk in range(8):
                            k = grp * 8 + kk
                            c_tp = sch.op("tensor", lambda e, slot=slot, k=k, kk=kk, bank=bank: e.transpose(
                                PT[bank][:, kk * 128:(kk + 1) * 128],
                                nat[:, slot, k * 128:(k + 1) * 128], eyeb[:]),
                                inc="pe" if kk == 7 else None)
                        sch.wait("scalar", "pe", c_tp)
                        if len(dq_hist) >= 3:
                            sch.wait("scalar", "st%d" % slot, stage_dma[dq_hist[-3]])
                        pt_evac[(gl, grp)] = sch.op("scalar", lambda e, slot=slot, grp=grp, bank=bank: e.copy(
                            stage[:, slot, grp * 8:(grp + 1) * 8, :],
                            PT[bank].rearrange("p (g q) -> p g q", q=128)),
                            inc="ac")
                        if grp == 1:
                            tp_done[gl] = c_tp
                    sch.wait("scalar", "ac", pt_evac[(gl, 1)])
                    stage_dma[gl] = sch.op("scalar", lambda e, m=m, oc=oc, slot=slot: e.dma_start(
                        out=w_in[m][oc], in_=stage[:, slot]), inc="st%d" % slot, n=16)

                def emit_cc(m, q):
                    sch.wait("gpsimd", "st0", sch.val("st0"))
                    sch.wait("gpsimd", "st1", sch.val("st1"))
                    rows = 4 * 128 * 16
                    c = sch.op("gpsimd", lambda e, m=m, q=q, rows=rows: e.collective_compute(
                        "AllGather", ALU.bypass,
                        replica_groups=[list(range(NCORES))],
                        ins=[w_in[m].rearrange("a p k q -> (a p k) q")[q * rows:(q + 1) * rows, :]],
                        outs=[w_out[m][q].rearrange("n a p k q -> (n a p k) q")]),
                        inc="cc")
                    assert c == cc_at[(m, q)], (m, q, c, cc_at[(m, q)])

                for m in ("g", "u"):
                    for oc in range(4):
                        emit_dq_chunk(m, oc, (0 if m == "g" else 16) + oc, "sync")
                    emit_cc(m, 0)

                rest = []
                for q in range(1, 4):
                    rest += [("g", oc) for oc in range(4 * q, 4 * q + 4)]
                    rest += [("u", oc) for oc in range(4 * q, 4 * q + 4)]
                rest += [("d", oc) for oc in range(16)]

                CPH = 3

                def dq_hook(gidx):
                    for j in range(CPH * gidx, min(CPH * (gidx + 1), len(rest))):
                        m, oc = rest[j]
                        base = {"g": 0, "u": 16, "d": 32}[m]
                        emit_dq_chunk(m, oc, base + oc, "scalar")
                        if oc % 4 == 3:
                            emit_cc(m, oc // 4)

                emit_ffn_app(sch, sq_sb, wr, g1, first=True, ln_done=ffn1_ln, dq_hook=dq_hook)
                sch.flush(blk)
        cc_all = ctr.get("cc", 0)


        if debug:
            with nc.Block() as blk:
                sch = Sched(ctr, sems)
                sch.op("sync", lambda e: e.dma_start(
                    out=dbg["x1T"].rearrange("(c p) t -> p c t", p=128), in_=xt[:]), inc="ld", n=16)
                sch.op("sync", lambda e: e.dma_start(
                    out=dbg["yT"].rearrange("(c p) t -> p c t", p=128), in_=y[:]), inc="ld", n=16)
                sch.op("sync", lambda e: e.dma_start(out=dbg["wg"][:], in_=w_out["g"][3]), inc="ld", n=16)
                c = sch.op("sync", lambda e: e.dma_start(out=dbg["wd"][:], in_=w_out["d"][3]), inc="ld", n=16)
                sch.wait("sync", "ld", c)
                sch.flush(blk)

        TQ = 256
        with ExitStack() as c2:
            sq_sb = c2.enter_context(nc.sbuf_tensor("sq2", [128, 2, TQ], F32))
            wra = c2.enter_context(nc.sbuf_tensor("wra", [128, 2, 16, 128], BF16))
            cqb = c2.enter_context(nc.sbuf_tensor("cqb", [128, 4, TQ], BF16))
            ckvb = c2.enter_context(nc.sbuf_tensor("ckvb", [128, 4, TQ], BF16))
            qcb = c2.enter_context(nc.sbuf_tensor("qcb", [128, 16, TQ], BF16))
            kcb = c2.enter_context(nc.sbuf_tensor("kcb", [128, 16, TQ], BF16))
            qrb = c2.enter_context(nc.sbuf_tensor("qrb", [128, 8, TQ], BF16))
            krb = c2.enter_context(nc.sbuf_tensor("krb", [128, 8, TQ], BF16))
            qr0 = c2.enter_context(nc.sbuf_tensor("qr0", [128, TQ], F32))
            rtm1 = c2.enter_context(nc.sbuf_tensor("rtm1", [128, TQ], F32))
            rtm2 = c2.enter_context(nc.sbuf_tensor("rtm2", [128, TQ], F32))
            sc256 = c2.enter_context(nc.sbuf_tensor("sc256", [128, 256], F32))
            es256 = c2.enter_context(nc.sbuf_tensor("es256", [128, 256], F32))
            nm16 = c2.enter_context(nc.sbuf_tensor("nm16", [128, 16], F32))
            rs16 = c2.enter_context(nc.sbuf_tensor("rs16", [128, 16], F32))
            rs16b = c2.enter_context(nc.sbuf_tensor("rs16b", [128, 16], F32))
            rs16c = c2.enter_context(nc.sbuf_tensor("rs16c", [128, 16], F32))
            at256 = c2.enter_context(nc.sbuf_tensor("at256", [128, 256], BF16))
            ao = c2.enter_context(nc.sbuf_tensor("ao", [128, 16, TQ], BF16))
            rope = c2.enter_context(nc.sbuf_tensor("rope_sb", [128, 2 * T], F32))
            vb = y.rearrange("p c (h t) -> p c h t", h=4)

            q_t = sh[:, 6144:9216]
            k_t = sh[:, 9216:12288]
            v_t = sh[:, 12288:14336]
            o_t = sh[:, 14336:16384]
            q192 = q_t.rearrange("p (i d) -> p i d", d=192)
            k192 = k_t.rearrange("p (i d) -> p i d", d=192)

            proj_specs = [
                ("cq", "Wc", "bc", "y", 16, 4, None),
                ("ckv", "Wckv", "bckv", "y", 16, 4, None),
                ("qc", "Wcq", "bcq", "cq", 4, 16, None),
                ("qr", "Wqr", "bqr", "cq", 4, 8, None),
                ("kc", "Wck", "bck", "ckv", 4, 16, None),
                ("kr", "Wkr", "bkr", "y", 16, 8, None),
                ("v", "Wv", "bv", "ckv", 4, 16, None),
            ]

            for tq in range(4):
                t0 = tq * TQ
                with nc.Block() as blk:
                    sch = Sched(ctr, sems)
                    if tq == 0:
                        rope_ld = sch.op("sync", lambda e: e.dma_start(out=rope[:], in_=rope_d[:]), inc="pre", n=16)
                        sch.wait("vector", "pre", rope_ld)
                    ln_done, _ = emit_ln_half(sch, t0, 0, sq_sb, tw=TQ)

                    dsts = {"cq": cqb, "ckv": ckvb, "qc": qcb, "kc": kcb,
                            "v": None, "qr": qrb, "kr": krb}

                    def vslice(m):
                        return vb[:, m, 1, :]

                    pbank_i = 0
                    bank_free = {}
                    dst_ready = {}
                    wcons = {}

                    def rhs_of(src, k):
                        if src == "y":
                            return y[:, k, 0:TQ]
                        if src == "cq":
                            return cqb[:, k, :]
                        return ckvb[:, k, :]

                    for (pname, wname, bname, ksrc, kch, mch, _unused) in proj_specs:
                        for m in range(mch):
                            slot = pbank_i % 2
                            if pbank_i >= 2:
                                sch.wait("gpsimd", "pe", wcons[pbank_i - 2])
                            c_w = sch.op("gpsimd", lambda e, wname=wname, kch=kch, m=m, slot=slot: e.dma_start(
                                out=wra[:, slot, 0:kch, :],
                                in_=aw[wname].rearrange("(k p) n -> p k n", p=128)[:, :, m * 128:(m + 1) * 128]),
                                inc="wa%d" % slot, n=16)
                            bank = 2 + (pbank_i % 4)
                            sch.wait("tensor", "wa%d" % slot, c_w)
                            if pname == "cq" and m == 0:
                                sch.wait("tensor", "ve", ln_done)
                            if bank in bank_free:
                                sch.wait("tensor", "ve", bank_free[bank])
                            if pname in ("qc", "qr") and m == 0:
                                sch.wait("tensor", "ve", dst_ready[("cq", 3)])
                            if pname in ("kc", "v") and m == 0:
                                sch.wait("tensor", "ve", dst_ready[("ckv", 3)])
                            grp = 0
                            for k in range(kch):
                                grp = sch.op("tensor", lambda e, slot=slot, k=k, ksrc=ksrc, bank=bank: e.matmul(
                                    P[bank][:, 0:TQ], wra[:, slot, k, :], rhs_of(ksrc, k),
                                    start=(k == 0), stop=(k == kch - 1)),
                                    inc="pe" if k == kch - 1 else None)
                            wcons[pbank_i] = grp
                            pbank_i += 1
                            sch.wait("vector", "pe", grp)
                            if pname in ("qr", "kr"):
                                c1v = sch.op("vector", lambda e, bank=bank, bname=bname, m=m: e.tensor_scalar(
                                    qr0[:], P[bank][:, 0:TQ], bias_ap(bname, m), None, ALU.add), inc="ve")
                                bank_free[bank] = c1v
                                sch.wait("tensor", "ve", c1v)
                                if ("rotfree",) in dst_ready:
                                    sch.wait("tensor", "ve", dst_ready[("rotfree",)])
                                c_rot = sch.op("tensor", lambda e: e.matmul(
                                    PTf[0][:, 0:TQ], rotm[:], qr0[:], start=True, stop=True), inc="pe")
                                sch.wait("vector", "pe", c_rot)
                                cosap = rope[:, t0:t0 + TQ]
                                sinap = rope[:, T + t0:T + t0 + TQ]
                                sch.op("vector", lambda e, cosap=cosap: e.tensor_tensor(rtm1[:], qr0[:], cosap, ALU.mult))
                                sch.op("vector", lambda e, sinap=sinap: e.tensor_tensor(rtm2[:], PTf[0][:, 0:TQ], sinap, ALU.mult))
                                dstap = (qrb if pname == "qr" else krb)[:, m, :]
                                c2v = sch.op("vector", lambda e, dstap=dstap: e.tensor_add(dstap, rtm1[:], rtm2[:]), inc="ve")
                                dst_ready[("rotfree",)] = c2v
                                dst_ready[(pname, m)] = c2v
                            else:
                                dstap = vslice(m) if pname == "v" else dsts[pname][:, m, :]
                                c1v = sch.op("vector", lambda e, dstap=dstap, bank=bank, bname=bname, m=m: e.tensor_scalar(
                                    dstap, P[bank][:, 0:TQ], bias_ap(bname, m), None, ALU.add), inc="ve")
                                bank_free[bank] = c1v
                                dst_ready[(pname, m)] = c1v
                    proj_all = sch.val("ve")

                    tp_ev = {}
                    prev_av = 0
                    ao_marks = (0, 0)
                    for st in range(TQ // 128):
                        s0 = st * 128

                        def tgroup(srcs, dst_builder, gi, split4=False, st=st, prev_av=prev_av):
                            bank = gi % 2
                            vm, am = tp_ev.get(bank, (0, 0))
                            sch.wait("tensor", "ve", vm)
                            sch.wait("tensor", "ac", am)
                            c_tp = 0
                            for idx, (srcap,) in enumerate(srcs):
                                c_tp = sch.op("tensor", lambda e, srcap=srcap, idx=idx, bank=bank: e.transpose(
                                    PT[bank][:, idx * 128:(idx + 1) * 128], srcap, eyeb[:]),
                                    inc="pe" if idx == 7 else None)
                            eng = "vector" if gi % 3 != 2 else "scalar"
                            sch.wait(eng, "pe", c_tp)
                            if eng == "scalar" and prev_av:
                                sch.wait(eng, "ve", prev_av)
                            if split4:
                                src_ap = PT[bank].rearrange("p (g two d) -> p g two d", two=2, d=64)
                            else:
                                src_ap = PT[bank].rearrange("p (g q) -> p g q", q=128)
                            if eng == "vector":
                                c = sch.op(eng, lambda e, b=dst_builder, s=src_ap: e.tensor_copy(b(), s), inc="ve")
                                tp_ev[bank] = (c, am)
                            else:
                                c = sch.op(eng, lambda e, b=dst_builder, s=src_ap: e.copy(b(), s), inc="ac")
                                tp_ev[bank] = (vm, c)
                            return tp_ev[bank]

                        sch.wait("tensor", "ve", proj_all)
                        gi = 0
                        marks = []
                        for half in range(2):
                            marks.append(tgroup([(qcb[:, half * 8 + j, s0:s0 + 128],) for j in range(8)],
                                                lambda half=half: q192[:, half * 8:(half + 1) * 8, 0:128], gi))
                            gi += 1
                        marks.append(tgroup([(qrb[:, j, s0:s0 + 128],) for j in range(8)],
                                            lambda: q192[:, :, 128:192].rearrange("p (a two) d -> p a two d", two=2),
                                            gi, split4=True))
                        gi += 1
                        for half in range(2):
                            marks.append(tgroup([(kcb[:, half * 8 + j, s0:s0 + 128],) for j in range(8)],
                                                lambda half=half: k192[:, half * 8:(half + 1) * 8, 0:128], gi))
                            gi += 1
                        marks.append(tgroup([(krb[:, j, s0:s0 + 128],) for j in range(8)],
                                            lambda: k192[:, :, 128:192].rearrange("p (a two) d -> p a two d", two=2),
                                            gi, split4=True))
                        gi += 1
                        for half in range(2):
                            marks.append(tgroup([(vb[:, half * 8 + j, 1, s0:s0 + 128],) for j in range(8)],
                                                lambda half=half: v_t.rearrange("p (j d) -> p j d", d=128)[:, half * 8:(half + 1) * 8, :], gi))
                            gi += 1
                        for vm, am in marks:
                            sch.wait("vector", "ve", vm)
                            sch.wait("vector", "ac", am)
                        prod = shF[:, 0:3072].rearrange("p (j d) -> p j d", d=192)
                        c_sc = 0
                        for i in range(H):
                            qi = q192[:, i:i + 1, :].broadcast_to([128, H, 192])
                            sch.op("vector", lambda e, qi=qi, prod=prod: e.tensor_tensor(prod, qi, k192, ALU.mult))
                            c_sc = sch.op("vector", lambda e, i=i, prod=prod: e.tensor_reduce(
                                sc256[:, i * 16:(i + 1) * 16], prod, AX.X, ALU.add),
                                inc="ve" if i == H - 1 else None)
                        sch.wait("vector", "ve", c_sc)
                        sc3 = sc256.rearrange("p (i j) -> p i j", j=16)
                        c_r1 = sch.op("vector", lambda e, sc3=sc3: e.tensor_reduce(nm16[:], sc3, AX.X, ALU.max, negate=True), inc="ve")
                        sch.wait("vector", "ve", c_r1)
                        c_sm = sch.op("vector", lambda e, sc3=sc3: e.tensor_tensor(
                            sc3, sc3,
                            nm16.rearrange("p (i one) -> p i one", one=1).broadcast_to([128, 16, 16]),
                            ALU.add), inc="ve")
                        sch.wait("scalar", "ve", c_sm)
                        c_exp = sch.op("scalar", lambda e: e.activation(es256[:], sc256[:], AF.Exp, scale=SCALER), inc="ac")
                        sch.wait("vector", "ac", c_exp)
                        es3 = es256.rearrange("p (i j) -> p i j", j=16)
                        c_r2 = sch.op("vector", lambda e, es3=es3: e.tensor_reduce(rs16[:], es3, AX.X, ALU.add), inc="ve")
                        sch.wait("vector", "ve", c_r2)
                        c_r3 = sch.op("vector", lambda e: e.reciprocal(rs16b[:], rs16[:]), inc="ve")
                        sch.wait("vector", "ve", c_r3)
                        at3 = at256.rearrange("p (i j) -> p i j", j=16)
                        sch.op("vector", lambda e, es3=es3, at3=at3: e.tensor_tensor(
                            at3, es3,
                            rs16b.rearrange("p (i one) -> p i one", one=1).broadcast_to([128, 16, 16]),
                            ALU.mult))
                        vv = v_t.rearrange("p (j d) -> p d j", d=128)
                        prod2 = shF[:, 0:2048].rearrange("p (d j) -> p d j", j=16)
                        c_av = 0
                        def av_reduce(e, i, prod2):
                            with nc.allow_low_precision(reason="attn@v: fp32 accum, bf16 matmul operand out"):
                                return e.tensor_reduce(o_t[:, i * 128:(i + 1) * 128], prod2, AX.X, ALU.add)

                        for i in range(H):
                            ai = at3[:, i:i + 1, :].broadcast_to([128, 128, 16])
                            sch.op("vector", lambda e, ai=ai, vv=vv, prod2=prod2: e.tensor_tensor(prod2, ai, vv, ALU.mult))
                            c_av = sch.op("vector", lambda e, i=i, prod2=prod2: av_reduce(e, i, prod2),
                                inc="ve" if i == H - 1 else None)
                        prev_av = c_av
                        if debug and tq == 0 and st == 0:
                            sch.wait("sync", "ve", c_av)
                            for nm, src in (("q", q_t), ("k", k_t), ("v", v_t), ("o", None),
                                            ("sc", sc256), ("es", es256), ("at", at256)):
                                if nm == "o":
                                    continue
                                dk = {"q": "q", "k": "k", "v": "v", "sc": "sc", "es": "es", "at": "at"}[nm]
                                sch.op("sync", lambda e, dk=dk, src=src: e.dma_start(out=dbg[dk][:], in_=src[:]), inc="ld", n=16)
                            sch.op("sync", lambda e: e.dma_start(out=dbg["ot"][:], in_=o_t[:]), inc="ld", n=16)
                            sch.op("sync", lambda e: e.dma_start(out=dbg["rs"][:], in_=rs16[:]), inc="ld", n=16)
                            sch.op("sync", lambda e: e.dma_start(out=dbg["rsb"][:], in_=rs16b[:]), inc="ld", n=16)
                            sch.op("sync", lambda e: e.dma_start(out=dbg["nm"][:], in_=nm16[:]), inc="ld", n=16)
                            tap_tot = sch.val("ld")
                            for eng in ("vector", "scalar", "tensor"):
                                sch.wait(eng, "ld", tap_tot)
                        sch.wait("tensor", "ve", c_av)
                        for half in range(2):
                            bank = half
                            vm, am = tp_ev.get(bank, (0, 0))
                            sch.wait("tensor", "ve", vm)
                            sch.wait("tensor", "ac", am)
                            c_tp = 0
                            for idx in range(8):
                                mch = half * 8 + idx
                                c_tp = sch.op("tensor", lambda e, mch=mch, idx=idx, bank=bank: e.transpose(
                                    PT[bank][:, idx * 128:(idx + 1) * 128],
                                    o_t[:, mch * 128:(mch + 1) * 128], eyeb[:]),
                                    inc="pe" if idx == 7 else None)
                            eng = "vector" if half == 0 else "scalar"
                            sch.wait(eng, "pe", c_tp)
                            if eng == "scalar":
                                sch.wait(eng, "ve", c_av)
                            dstb = (lambda half=half, s0=s0: ao[:, half * 8:(half + 1) * 8, s0:s0 + 128])
                            src_ap = PT[bank].rearrange("p (g q) -> p g q", q=128)
                            if eng == "vector":
                                c = sch.op(eng, lambda e, dstb=dstb, s=src_ap: e.tensor_copy(dstb(), s), inc="ve")
                                tp_ev[bank] = (c, am)
                            else:
                                c = sch.op(eng, lambda e, dstb=dstb, s=src_ap: e.copy(dstb(), s), inc="ac")
                                tp_ev[bank] = (vm, c)
                        ao_marks = (tp_ev[0][0], tp_ev[1][1])

                    sch.wait("tensor", "ve", ao_marks[0])
                    sch.wait("tensor", "ac", ao_marks[1])
                    wo_pe = {}
                    wo_bank = {}
                    for m in range(16):
                        slot = m % 2
                        if m >= 2:
                            sch.wait("gpsimd", "pe", wo_pe[m - 2])
                        else:
                            sch.wait("gpsimd", "pe", wcons[pbank_i - 2 + m])
                        c_w = sch.op("gpsimd", lambda e, m=m, slot=slot: e.dma_start(
                            out=wra[:, slot, :, :],
                            in_=aw["Wo"].rearrange("(k p) n -> p k n", p=128)[:, :, m * 128:(m + 1) * 128]),
                            inc="wa%d" % slot, n=16)
                        bank = 2 + (m % 2)
                        sch.wait("tensor", "wa%d" % slot, c_w)
                        if bank in wo_bank:
                            sch.wait("tensor", "ve", wo_bank[bank])
                        grp = 0
                        for k in range(KCH):
                            grp = sch.op("tensor", lambda e, slot=slot, k=k, bank=bank: e.matmul(
                                P[bank][:, 0:TQ], wra[:, slot, k, :], ao[:, k, :],
                                start=(k == 0), stop=(k == KCH - 1)),
                                inc="pe" if k == KCH - 1 else None)
                        wo_pe[m] = grp
                        sch.wait("vector", "pe", grp)
                        c_r = sch.op("vector", lambda e, m=m, bank=bank, t0=t0: e.scalar_tensor_tensor(
                            xt[:, m, t0:t0 + TQ], P[bank][:, 0:TQ], bias_ap("bo", m),
                            xt[:, m, t0:t0 + TQ], ALU.add, ALU.add), inc="ve")
                        wo_bank[bank] = c_r
                    sch.flush(blk)

        if debug:
            with nc.Block() as blk:
                sch = Sched(ctr, sems)
                c = sch.op("sync", lambda e: e.dma_start(
                    out=dbg["x2T"].rearrange("(c p) t -> p c t", p=128), in_=xt[:]), inc="ld", n=16)
                sch.wait("sync", "ld", c)
                sch.flush(blk)

        with ExitStack() as c3:
            sq_sb = c3.enter_context(nc.sbuf_tensor("sq3", [128, 2, 512], F32))
            wr = {m: c3.enter_context(nc.sbuf_tensor(f"ring3{m}", [128, 2, 16, 128], BF16)) for m in ("g", "u", "d")}
            g1 = c3.enter_context(nc.sbuf_tensor("g1b", [128, 2, 512], F32))
            with nc.Block() as blk:
                sch = Sched(ctr, sems)
                emit_ffn_app(sch, sq_sb, wr, g1, first=False)
                sch.flush(blk)

        with ExitStack() as c4:
            ost = c4.enter_context(nc.sbuf_tensor("ost", [128, 2, D], F32))
            with nc.Block() as blk:
                sch = Sched(ctr, sems)
                ost_dma = {}
                pt_free = {}
                for st8 in range(8):
                    slot = st8 % 2
                    last_ev = 0
                    for grp4 in range(4):
                        bank = grp4 % 2
                        if bank in pt_free:
                            sch.wait("tensor", "ac", pt_free[bank])
                        c_tp = 0
                        for idx in range(4):
                            mch = grp4 * 4 + idx
                            c_tp = sch.op("tensor", lambda e, mch=mch, idx=idx, st8=st8, bank=bank: e.transpose(
                                PTf[bank][:, idx * 128:(idx + 1) * 128],
                                xt[:, mch, st8 * 128:(st8 + 1) * 128], eyef[:]),
                                inc="pe" if idx == 3 else None)
                        sch.wait("scalar", "pe", c_tp)
                        if st8 >= 2:
                            sch.wait("scalar", "os%d" % slot, ost_dma[st8 - 2])
                        last_ev = sch.op("scalar", lambda e, grp4=grp4, slot=slot, bank=bank: e.copy(
                            ost[:, slot, grp4 * 512:(grp4 + 1) * 512], PTf[bank][:, :]), inc="ac")
                        pt_free[bank] = last_ev
                    sch.wait("sync", "ac", last_ev)
                    ost_dma[st8] = sch.op("sync", lambda e, st8=st8, slot=slot: e.dma_start(
                        out=out_d[st8 * 128:(st8 + 1) * 128, :], in_=ost[:, slot]), inc="os%d" % slot, n=16)
                sch.wait("sync", "os0", sch.val("os0"))
                sch.wait("sync", "os1", sch.val("os1"))
                sch.flush(blk)

    return nc


_CACHE = {}


def _host_constants():
    j = np.arange(32, dtype=np.float32)
    inv_freq = (1.0 / (10000.0 ** (j / np.float32(32.0)))).astype(np.float32)
    dh = np.arange(128) % 64
    fr = inv_freq[dh % 32]
    R = np.zeros((128, 128), np.float32)
    for h2 in range(2):
        b = h2 * 64
        for dd in range(32):
            R[b + dd + 32, b + dd] = -1.0
            R[b + dd, b + dd + 32] = 1.0
    return fr, R


def kernel(**inputs):
    inp = {k: np.asarray(v) for k, v in inputs.items()}
    h_t = np.asarray(inp["L0_attn_h_t"], np.float32).reshape(S, D)

    fr, R = _host_constants()
    eye_bf = np.eye(128, dtype=ml_dtypes.bfloat16)
    eye_f = np.eye(128, dtype=np.float32)

    def pack_bias(v, chunks):
        return np.ascontiguousarray(np.asarray(v, np.float32).reshape(chunks, 128).T)

    bias_host = np.ascontiguousarray(np.concatenate([
        pack_bias(inp["L0_attn_bc"], 4),
        pack_bias(inp["L0_attn_bcq"], 16),
        pack_bias(inp["L0_attn_bqr"], 8),
        pack_bias(inp["L0_attn_bckv"], 4),
        pack_bias(inp["L0_attn_bck"], 16),
        pack_bias(inp["L0_attn_bkr"], 8),
        pack_bias(inp["L0_attn_bv"], 16),
        pack_bias(inp["L0_attn_bo"], 16),
    ], axis=1))

    if "nc" not in _CACHE:
        _CACHE["nc"] = build_program(debug=False)
    nc = _CACHE["nc"]

    in_maps = []
    for c in range(NCORES):
        t0 = c * T
        tpos = np.arange(t0, t0 + T, dtype=np.float32)
        ang = tpos[None, :] * fr[:, None]
        rope_cs = np.ascontiguousarray(
            np.concatenate([np.cos(ang), np.sin(ang)], axis=1).astype(np.float32))
        m = {
            "xT": np.ascontiguousarray(h_t[t0:t0 + T, :].T),
            "biases": bias_host,
            "rope_cs": rope_cs,
            "rotm": R,
            "eye_bf": eye_bf,
            "eye_f32": eye_f,
            "Wc": np.asarray(inp["L0_attn_Wc"], np.float32),
            "Wcq": np.asarray(inp["L0_attn_Wcq"], np.float32),
            "Wqr": np.asarray(inp["L0_attn_Wqr"], np.float32),
            "Wckv": np.asarray(inp["L0_attn_Wckv"], np.float32),
            "Wck": np.asarray(inp["L0_attn_Wck"], np.float32),
            "Wkr": np.asarray(inp["L0_attn_Wkr"], np.float32),
            "Wv": np.asarray(inp["L0_attn_Wv"], np.float32),
            "Wo": np.asarray(inp["L0_attn_Wo"], np.float32),
            "g_pk": np.ascontiguousarray(inp["L0_ffn_gate_W_packed"][c * 2048:(c + 1) * 2048, :]).astype(np.int32),
            "g_sc": np.ascontiguousarray(inp["L0_ffn_gate_scales"][c * 2048:(c + 1) * 2048, :]).astype(np.float32),
            "u_pk": np.ascontiguousarray(inp["L0_ffn_up_W_packed"][c * 2048:(c + 1) * 2048, :]).astype(np.int32),
            "u_sc": np.ascontiguousarray(inp["L0_ffn_up_scales"][c * 2048:(c + 1) * 2048, :]).astype(np.float32),
            "d_pk": np.ascontiguousarray(inp["L0_ffn_down_W_packed"][:, c * 1024:(c + 1) * 1024]).astype(np.int32),
            "d_sc": np.ascontiguousarray(inp["L0_ffn_down_scales"][:, c * 64:(c + 1) * 64]).astype(np.float32),
        }
        in_maps.append(m)

    res = run_bass_kernel_spmd(nc, in_maps, core_ids=list(range(NCORES)))
    out = np.concatenate([np.asarray(res.results[c]["out"]) for c in range(NCORES)], axis=0)
    return out.reshape(1, S, D).astype(np.float32)


# version 07
# speedup vs baseline: 1.0384x; 1.0025x over previous
import math
import numpy as np
import ml_dtypes
import concourse.bass as bass
import concourse.mybir as mybir
from concourse.bass_utils import run_bass_kernel_spmd
from contextlib import ExitStack

F32 = mybir.dt.float32
BF16 = mybir.dt.bfloat16
I32 = mybir.dt.int32
AF = mybir.ActivationFunctionType
ALU = mybir.AluOpType
AX = mybir.AxisListType

NCORES = 8
S = 8192
T = S // NCORES
D = 2048
DC = 512
DR = 1024
NBLK = 8
KCH = 16
H = 16
SCALER = 1.0 / math.sqrt(192.0)
C_EXP = 126 << 23
EPS = 1e-5

ENGINES = ("sync", "gpsimd", "vector", "scalar", "tensor")


class Sched:

    def __init__(self, ctr, sems):
        self.q = {e: [] for e in ENGINES}
        self.ctr = ctr
        self.sems = sems

    def wait(self, eng, sname, val):
        if val > 0:
            self.q[eng].append(lambda e, s=self.sems[sname], v=val: e.wait_ge(s, v))

    def op(self, eng, fn, inc=None, n=1):
        if inc is None:
            self.q[eng].append(fn)
            return self.ctr.get("_", 0)
        self.ctr[inc] = self.ctr.get(inc, 0) + n
        cnt = self.ctr[inc]
        sem = self.sems[inc]

        def run(e, fn=fn, sem=sem, n=n):
            fn(e).then_inc(sem, n)

        self.q[eng].append(run)
        return cnt

    def val(self, name):
        return self.ctr.get(name, 0)

    def flush(self, blk):
        for ename in ENGINES:
            ops = self.q[ename]
            if not ops:
                continue

            def body(e, ops=ops):
                for f in ops:
                    f(e)

            getattr(blk, ename)(body)
        self.q = {e: [] for e in ENGINES}


def build_program(debug=False):
    nc = bass.Bass("TRN2", target_bir_lowering=False)

    xT_d = nc.dram_tensor("xT", [D, T], F32, kind="ExternalInput")
    out_d = nc.dram_tensor("out", [T, D], F32, kind="ExternalOutput")
    aw = {}
    for name, shp in (("Wc", [D, DC]), ("Wcq", [DC, D]), ("Wqr", [DC, DR]),
                      ("Wckv", [D, DC]), ("Wck", [DC, D]), ("Wkr", [D, DR]),
                      ("Wv", [DC, D]), ("Wo", [D, D])):
        aw[name] = nc.dram_tensor(name, shp, F32, kind="ExternalInput")
    bias_d = nc.dram_tensor("biases", [128, 88], F32, kind="ExternalInput")
    rope_d = nc.dram_tensor("rope_cs", [128, 2 * T], BF16, kind="ExternalInput")
    rotm_d = nc.dram_tensor("rotm", [128, 128], F32, kind="ExternalInput")
    eyeb_d = nc.dram_tensor("eye_bf", [128, 128], BF16, kind="ExternalInput")
    eyef_d = nc.dram_tensor("eye_f32", [128, 128], F32, kind="ExternalInput")
    pk_d = {m: nc.dram_tensor(f"{m}_pk", [2048, 1024], I32, kind="ExternalInput")
            for m in ("g", "u", "d")}
    sc_d = {m: nc.dram_tensor(f"{m}_sc", [2048, 64], F32, kind="ExternalInput")
            for m in ("g", "u", "d")}
    w_in = {m: nc.dram_tensor(f"{m}_in", [16, 128, 16, 128], BF16) for m in ("g", "u", "d")}
    w_out = {m: nc.dram_tensor(f"{m}_out", [4, NBLK, 4, 128, 16, 128], BF16) for m in ("g", "u", "d")}

    dbg = {}
    if debug:
        dbg["x1T"] = nc.dram_tensor("dbg_x1T", [D, T], F32, kind="ExternalOutput")
        dbg["x2T"] = nc.dram_tensor("dbg_x2T", [D, T], F32, kind="ExternalOutput")
        dbg["yT"] = nc.dram_tensor("dbg_yT", [D, T], BF16, kind="ExternalOutput")
        dbg["wg"] = nc.dram_tensor("dbg_wg", [16, 128, 16, 128], BF16, kind="ExternalOutput")
        dbg["wd"] = nc.dram_tensor("dbg_wd", [16, 128, 16, 128], BF16, kind="ExternalOutput")
        dbg["q"] = nc.dram_tensor("dbg_q", [128, 3072], BF16, kind="ExternalOutput")
        dbg["k"] = nc.dram_tensor("dbg_k", [128, 3072], BF16, kind="ExternalOutput")
        dbg["v"] = nc.dram_tensor("dbg_v", [128, 2048], BF16, kind="ExternalOutput")
        dbg["sc"] = nc.dram_tensor("dbg_sc", [128, 256], F32, kind="ExternalOutput")
        dbg["es"] = nc.dram_tensor("dbg_es", [128, 256], F32, kind="ExternalOutput")
        dbg["at"] = nc.dram_tensor("dbg_at", [128, 256], BF16, kind="ExternalOutput")
        dbg["ot"] = nc.dram_tensor("dbg_ot", [128, 2048], BF16, kind="ExternalOutput")
        dbg["rs"] = nc.dram_tensor("dbg_rs", [128, 16], F32, kind="ExternalOutput")
        dbg["rsb"] = nc.dram_tensor("dbg_rsb", [128, 16], F32, kind="ExternalOutput")
        dbg["nm"] = nc.dram_tensor("dbg_nm", [128, 16], F32, kind="ExternalOutput")

    ctr = {}

    with ExitStack() as top:
        def sb(name, shape, dtype):
            return top.enter_context(nc.sbuf_tensor(name, shape, dtype))

        xt = sb("xt", [128, KCH, T], F32)
        y = sb("y", [128, KCH, T], BF16)
        sh = sb("sh", [128, 16384], BF16)
        biases = sb("biases_sb", [128, 88], F32)
        rotm = sb("rotm_sb", [128, 128], F32)
        eyeb = sb("eyeb_sb", [128, 128], BF16)
        ones_c = sb("ones_c", [128, 1], F32)
        ones_r = sb("ones_r", [1, 128], F32)
        rows = sb("rows", [1, 3, 512], F32)

        shF = sh.bitcast(F32)
        inter = sh.rearrange("p (c t) -> p c t", c=KCH)

        P = [top.enter_context(nc.psum_tensor(f"P{i}", [128, 512], F32)) for i in range(6)]
        PT = [top.enter_context(nc.psum_tensor(f"PT{i}", [128, 1024], BF16)) for i in range(2)]
        PTf = [t.bitcast(F32) for t in PT]

        sems = {k: top.enter_context(nc.semaphore("s_" + k))
                for k in ("ld", "gd", "cc", "pe", "ve", "ac", "pre",
                          "pk0", "pk1", "st0", "st1",
                          "wg0", "wg1", "wu0", "wu1", "wd0", "wd1",
                          "wa0", "wa1", "os0", "os1")}

        bias_cols = {}
        off = 0
        for bn, w in (("bc", 4), ("bcq", 16), ("bqr", 8), ("bckv", 4),
                      ("bck", 16), ("bkr", 8), ("bv", 16), ("bo", 16)):
            bias_cols[bn] = off
            off += w

        def bias_ap(bn, chunk):
            c = bias_cols[bn] + chunk
            return biases[:, c:c + 1]

        pre_total = [0]
        cc_at = {("g", 0): 1, ("u", 0): 2}
        _n = 2
        for _q in range(1, 4):
            _n += 1; cc_at[("g", _q)] = _n
            _n += 1; cc_at[("u", _q)] = _n
        for _q in range(4):
            _n += 1; cc_at[("d", _q)] = _n

        def emit_ln_half(sch, t0, y_off, sq_sb, ve_guard=0, pe_guard=0, tw=512,
                         sq_ap=None, tmp_off=3072):
            q_pe = {}
            sch.wait("tensor", "pre", pre_total[0])
            sch.wait("scalar", "pre", pre_total[0])
            sch.wait("vector", "pre", pre_total[0])
            if ve_guard:
                sch.wait("tensor", "ve", ve_guard)
            if pe_guard:
                sch.wait("scalar", "pe", pe_guard)
            for ch in range(KCH):
                slot = ch % 2
                if ch >= 2:
                    sch.wait("scalar", "pe", q_pe[ch - 2])
                sqv = sq_ap(slot) if sq_ap is not None else sq_sb[:, slot, 0:tw]
                c_ac = sch.op("scalar",
                              lambda e, ch=ch, sqv=sqv: e.activation(
                                  sqv, xt[:, ch, t0:t0 + tw], AF.Square),
                              inc="ac")
                sch.wait("tensor", "ac", c_ac)
                sch.op("tensor",
                       lambda e, ch=ch: e.matmul(P[4][0:1, 0:tw], ones_c[:], xt[:, ch, t0:t0 + tw],
                                                 start=(ch == 0), stop=(ch == KCH - 1)))
                q_pe[ch] = sch.op("tensor",
                                  lambda e, ch=ch, sqv=sqv: e.matmul(
                                      P[5][0:1, 0:tw], ones_c[:], sqv,
                                      start=(ch == 0), stop=(ch == KCH - 1)),
                                  inc="pe")
            stats_done = q_pe[KCH - 1]
            sch.wait("vector", "pe", stats_done)
            sch.op("vector", lambda e: e.tensor_scalar(rows[:, 0, 0:tw], P[4][0:1, 0:tw], 1.0 / D, None, ALU.mult))
            sch.op("vector", lambda e: e.tensor_scalar(rows[:, 1, 0:tw], P[5][0:1, 0:tw], 1.0 / D, None, ALU.mult))
            sch.op("vector", lambda e: e.tensor_tensor(rows[:, 2, 0:tw], rows[:, 0, 0:tw], rows[:, 0, 0:tw], ALU.mult))
            sch.op("vector", lambda e: e.tensor_tensor(rows[:, 2, 0:tw], rows[:, 1, 0:tw], rows[:, 2, 0:tw], ALU.subtract))
            c_ve = sch.op("vector", lambda e: e.tensor_scalar(rows[:, 2, 0:tw], rows[:, 2, 0:tw], EPS, None, ALU.add),
                          inc="ve")
            sch.wait("scalar", "ve", c_ve)
            c_ac = sch.op("scalar", lambda e: e.activation(rows[:, 1, 0:tw], rows[:, 2, 0:tw], AF.Sqrt),
                          inc="ac")
            sch.wait("vector", "ac", c_ac)
            c_ve = sch.op("vector", lambda e: e.reciprocal(rows[:, 2, 0:tw], rows[:, 1, 0:tw]), inc="ve")
            sch.wait("tensor", "ve", c_ve)
            sch.op("tensor", lambda e: e.matmul(P[0][:, 0:tw], ones_r[:], rows[:, 0, 0:tw], start=True, stop=True))
            bc_done = sch.op("tensor", lambda e: e.matmul(P[1][:, 0:tw], ones_r[:], rows[:, 2, 0:tw], start=True, stop=True),
                             inc="pe")
            sch.wait("vector", "pe", bc_done)
            tmp = shF[:, tmp_off:tmp_off + tw]
            for ch in range(KCH):
                sch.op("vector", lambda e, ch=ch, tmp=tmp: e.tensor_tensor(
                    tmp, xt[:, ch, t0:t0 + tw], P[0][:, 0:tw], ALU.subtract))
                c_n = sch.op("vector", lambda e, ch=ch, tmp=tmp: e.tensor_tensor(
                    y[:, ch, y_off:y_off + tw], tmp, P[1][:, 0:tw], ALU.mult),
                    inc="ve" if ch == KCH - 1 else None)
            return c_n, stats_done

        def emit_ffn_app(sch, sq_sb, wr, g1, first, ln_done=None, dq_hook=None):
            if ln_done is None:
                n0, sd0 = emit_ln_half(sch, 0, 0, sq_sb)
                ln_done, _ = emit_ln_half(sch, 512, 512, sq_sb, ve_guard=n0, pe_guard=sd0)

            dn_q = "gpsimd" if first else "sync"

            wload = {}
            pe_grp = {}
            gelu_at = {}
            inter_at = {}
            evac_at = {}
            down_pe_done = {}
            first_down = True

            for nb in range(NBLK):
                for mc in range(16):
                    gidx = nb * 16 + mc
                    if dq_hook is not None:
                        dq_hook(gidx)
                    slot = gidx % 2
                    if first and nb == 0:
                        sch.wait("sync", "cc", cc_at[("u", mc // 4)])
                    for m in ("g", "u"):
                        if gidx >= 2:
                            sch.wait("sync", "pe", pe_grp[gidx - 2])
                        wload[(m, gidx)] = sch.op("sync", lambda e, m=m, nb=nb, mc=mc, slot=slot: e.dma_start(
                            out=wr[m][:, slot], in_=w_out[m][mc // 4, nb, mc % 4]), inc="w%s%d" % (m, slot), n=16)
                    for tt in (0, 1):
                        sch.wait("tensor", "wg%d" % slot, wload[("g", gidx)])
                        sch.wait("tensor", "wu%d" % slot, wload[("u", gidx)])
                        if gidx == 0 and tt == 0:
                            sch.wait("tensor", "ve", ln_done)
                        if gidx >= 1:
                            sch.wait("tensor", "ac", gelu_at[(gidx - 1, tt)])
                            sch.wait("tensor", "ve", inter_at[(gidx - 1, tt)])
                        for k in range(KCH):
                            sch.op("tensor", lambda e, slot=slot, k=k, tt=tt: e.matmul(
                                P[tt][:, :], wr["g"][:, slot, k, :], y[:, k, tt * 512:(tt + 1) * 512],
                                start=(k == 0), stop=(k == KCH - 1)))
                        grp_pe = 0
                        for k in range(KCH):
                            grp_pe = sch.op("tensor", lambda e, slot=slot, k=k, tt=tt: e.matmul(
                                P[2 + tt][:, :], wr["u"][:, slot, k, :], y[:, k, tt * 512:(tt + 1) * 512],
                                start=(k == 0), stop=(k == KCH - 1)),
                                inc="pe" if k == KCH - 1 else None)
                        if tt == 1:
                            pe_grp[gidx] = grp_pe
                        sch.wait("scalar", "pe", grp_pe)
                        gelu_at[(gidx, tt)] = sch.op("scalar", lambda e, tt=tt: e.activation(
                            g1[:, tt, :], P[tt][:, :], AF.Gelu_apprx_tanh), inc="ac")
                        sch.wait("vector", "ac", gelu_at[(gidx, tt)])
                        if nb >= 1 and mc == 0 and tt == 0:
                            sch.wait("vector", "pe", down_pe_done[nb - 1])
                        inter_at[(gidx, tt)] = sch.op("vector", lambda e, mc=mc, tt=tt: e.tensor_tensor(
                            inter[:, mc, tt * 512:(tt + 1) * 512], g1[:, tt, :], P[2 + tt][:, :], ALU.mult),
                            inc="ve")
                for hc in range(16):
                    hgidx = nb * 16 + hc
                    slot = hgidx % 2
                    if first and nb == 0:
                        sch.wait(dn_q, "cc", cc_at[("d", hc // 4)])
                    if hgidx >= 2:
                        sch.wait(dn_q, "pe", pe_grp[("d", hgidx - 2)])
                    wload[("d", hgidx)] = sch.op(dn_q, lambda e, nb=nb, hc=hc, slot=slot: e.dma_start(
                        out=wr["d"][:, slot], in_=w_out["d"][hc // 4, nb, hc % 4]), inc="wd%d" % slot, n=16)
                    for tt in (0, 1):
                        g = hgidx * 2 + tt
                        bank = 4 + (g % 2)
                        sch.wait("tensor", "wd%d" % slot, wload[("d", hgidx)])
                        if hc == 0 and tt == 0:
                            sch.wait("tensor", "ve", inter_at[(nb * 16 + 15, 1)])
                        if g >= 2:
                            sch.wait("tensor", "ve", evac_at[g - 2])
                        grp_pe = 0
                        for k in range(KCH):
                            grp_pe = sch.op("tensor", lambda e, slot=slot, k=k, tt=tt, bank=bank: e.matmul(
                                P[bank][:, :], wr["d"][:, slot, k, :], inter[:, k, tt * 512:(tt + 1) * 512],
                                start=(k == 0), stop=(k == KCH - 1)),
                                inc="pe" if k == KCH - 1 else None)
                        if tt == 1:
                            pe_grp[("d", hgidx)] = grp_pe
                        sch.wait("vector", "pe", grp_pe)
                        evac_at[g] = sch.op("vector", lambda e, hc=hc, tt=tt, bank=bank: e.scalar_tensor_tensor(
                            xt[:, hc, tt * 512:(tt + 1) * 512], P[bank][:, :], 0.5,
                            xt[:, hc, tt * 512:(tt + 1) * 512], ALU.mult, ALU.add),
                            inc="ve")
                down_pe_done[nb] = pe_grp[("d", nb * 16 + 15)]

        with ExitStack() as c0:
            pk_t = c0.enter_context(nc.sbuf_tensor("pk", [128, 2, 1024], I32))
            sc_t = c0.enter_context(nc.sbuf_tensor("sct", [128, 2, 64], F32))
            v0l = c0.enter_context(nc.sbuf_tensor("v0l", [128, 1024], I32))
            sl_t = c0.enter_context(nc.sbuf_tensor("sl", [128, 1024], I32))
            v8_t = c0.enter_context(nc.sbuf_tensor("v8", [128, 1024], F32))
            nat = c0.enter_context(nc.sbuf_tensor("nat", [128, 2, 2048], BF16))
            stage = c0.enter_context(nc.sbuf_tensor("stage", [128, 2, 16, 128], BF16))
            sq_sb = c0.enter_context(nc.sbuf_tensor("sq1", [128, 2, 512], F32))
            wr = {m: c0.enter_context(nc.sbuf_tensor(f"ring1{m}", [128, 2, 16, 128], BF16)) for m in ("g", "u", "d")}
            g1 = c0.enter_context(nc.sbuf_tensor("g1a", [128, 2, 512], F32))

            with nc.Block() as blk:
                sch = Sched(ctr, sems)
                sch.op("sync", lambda e: e.dma_start(out=xt[:], in_=xT_d.rearrange("(c p) t -> p c t", p=128)),
                       inc="pre", n=16)
                sch.op("sync", lambda e: e.dma_start(out=biases[:], in_=bias_d[:]), inc="pre", n=16)
                sch.op("sync", lambda e: e.dma_start(out=rotm[:], in_=rotm_d[:]), inc="pre", n=16)
                eye_ld = sch.op("sync", lambda e: e.dma_start(out=eyeb[:], in_=eyeb_d[:]), inc="pre", n=16)
                PRE_ALL = sch.val("pre")
                pre_total[0] = PRE_ALL
                sch.op("vector", lambda e: e.memset(ones_c[:], 1.0))
                sch.op("vector", lambda e: e.memset(ones_r[:], 1.0), inc="ve")

                n0_, sd0_ = emit_ln_half(sch, 0, 0, sq_sb)
                ffn1_ln, _ = emit_ln_half(sch, 512, 512, sq_sb, ve_guard=n0_, pe_guard=sd0_)

                dq_done = {}
                tp_done = {}
                pt_evac = {}
                stage_dma = {}

                dq_hist = []

                def emit_dq_chunk(m, oc, gl, qeng):
                    dq_hist.append(gl)
                    slot = (len(dq_hist) - 1) % 2
                    if len(dq_hist) >= 3:
                        sch.wait(qeng, "ve", dq_done[dq_hist[-3]])
                    pksem = "pk%d" % slot
                    sch.op(qeng, lambda e, m=m, oc=oc, slot=slot: e.dma_start(
                        out=pk_t[:, slot, :],
                        in_=pk_d[m].rearrange("(c p) j -> c p j", p=128)[oc]), inc=pksem, n=16)
                    c_ld = sch.op(qeng, lambda e, m=m, oc=oc, slot=slot: e.dma_start(
                        out=sc_t[:, slot, :],
                        in_=sc_d[m].rearrange("(c p) j -> c p j", p=128)[oc]), inc=pksem, n=16)
                    sch.wait("vector", pksem, c_ld)
                    if len(dq_hist) >= 3:
                        sch.wait("vector", "pe", tp_done[dq_hist[-3]])
                    pkap = pk_t[:, slot, :]
                    sc_b = sc_t.rearrange("p s (b one) -> p s b one", one=1)[:, slot].broadcast_to([128, 64, 16])
                    c_dq = 0
                    for par in (0, 1):
                        if par == 0:
                            sch.op("vector", lambda e, pkap=pkap: e.tensor_scalar(v0l[:], pkap, 7, 22, ALU.bitwise_and, ALU.logical_shift_left))
                            sch.op("vector", lambda e, pkap=pkap: e.tensor_scalar(sl_t[:], pkap, 8, 28, ALU.bitwise_and, ALU.logical_shift_left))
                        else:
                            sch.op("vector", lambda e, pkap=pkap: e.tensor_scalar(v0l[:], pkap, 4, None, ALU.logical_shift_right))
                            sch.op("vector", lambda e: e.tensor_scalar(v0l[:], v0l[:], 7, 22, ALU.bitwise_and, ALU.logical_shift_left))
                            sch.op("vector", lambda e, pkap=pkap: e.tensor_scalar(sl_t[:], pkap, 128, 24, ALU.bitwise_and, ALU.logical_shift_left))
                        sch.op("vector", lambda e: e.tensor_scalar(v0l[:], v0l[:], C_EXP, None, ALU.add))
                        v0f = v0l.bitcast(F32)
                        sch.op("vector", lambda e, v0f=v0f: e.tensor_scalar(v8_t[:], v0f[:], 1.0, 0.0, ALU.subtract, ALU.min))
                        sch.op("vector", lambda e, v0f=v0f: e.tensor_add(v8_t[:], v0f[:], v8_t[:]))
                        sch.op("vector", lambda e: e.tensor_tensor(v8_t.bitcast(I32)[:], v8_t.bitcast(I32)[:], sl_t[:], ALU.bitwise_xor))
                        natv = nat.rearrange("p s (b g two) -> p s b g two", two=2, g=16)[:, slot, :, :, par]
                        c_dq = sch.op("vector", lambda e, natv=natv, sc_b=sc_b: e.tensor_tensor(
                            natv, v8_t.rearrange("p (b g) -> p b g", g=16), sc_b, ALU.mult),
                            inc="ve" if par == 1 else None)
                    dq_done[gl] = c_dq
                    sch.wait("tensor", "ve", c_dq)
                    if len(dq_hist) == 1:
                        sch.wait("tensor", "pre", PRE_ALL)
                    for grp in range(2):
                        bank = grp
                        if len(dq_hist) >= 2:
                            sch.wait("tensor", "ac", pt_evac[(dq_hist[-2], grp)])
                        c_tp = 0
                        for kk in range(8):
                            k = grp * 8 + kk
                            c_tp = sch.op("tensor", lambda e, slot=slot, k=k, kk=kk, bank=bank: e.transpose(
                                PT[bank][:, kk * 128:(kk + 1) * 128],
                                nat[:, slot, k * 128:(k + 1) * 128], eyeb[:]),
                                inc="pe" if kk == 7 else None)
                        sch.wait("scalar", "pe", c_tp)
                        if len(dq_hist) >= 3:
                            sch.wait("scalar", "st%d" % slot, stage_dma[dq_hist[-3]])
                        pt_evac[(gl, grp)] = sch.op("scalar", lambda e, slot=slot, grp=grp, bank=bank: e.copy(
                            stage[:, slot, grp * 8:(grp + 1) * 8, :],
                            PT[bank].rearrange("p (g q) -> p g q", q=128)),
                            inc="ac")
                        if grp == 1:
                            tp_done[gl] = c_tp
                    sch.wait("scalar", "ac", pt_evac[(gl, 1)])
                    stage_dma[gl] = sch.op("scalar", lambda e, m=m, oc=oc, slot=slot: e.dma_start(
                        out=w_in[m][oc], in_=stage[:, slot]), inc="st%d" % slot, n=16)

                def emit_cc(m, q):
                    sch.wait("gpsimd", "st0", sch.val("st0"))
                    sch.wait("gpsimd", "st1", sch.val("st1"))
                    rows = 4 * 128 * 16
                    c = sch.op("gpsimd", lambda e, m=m, q=q, rows=rows: e.collective_compute(
                        "AllGather", ALU.bypass,
                        replica_groups=[list(range(NCORES))],
                        ins=[w_in[m].rearrange("a p k q -> (a p k) q")[q * rows:(q + 1) * rows, :]],
                        outs=[w_out[m][q].rearrange("n a p k q -> (n a p k) q")]),
                        inc="cc")
                    assert c == cc_at[(m, q)], (m, q, c, cc_at[(m, q)])

                for m in ("g", "u"):
                    for oc in range(4):
                        emit_dq_chunk(m, oc, (0 if m == "g" else 16) + oc, "sync")
                    emit_cc(m, 0)

                rest = []
                for q in range(1, 4):
                    rest += [("g", oc) for oc in range(4 * q, 4 * q + 4)]
                    rest += [("u", oc) for oc in range(4 * q, 4 * q + 4)]
                rest += [("d", oc) for oc in range(16)]

                CPH = 3

                def dq_hook(gidx):
                    for j in range(CPH * gidx, min(CPH * (gidx + 1), len(rest))):
                        m, oc = rest[j]
                        base = {"g": 0, "u": 16, "d": 32}[m]
                        emit_dq_chunk(m, oc, base + oc, "scalar")
                        if oc % 4 == 3:
                            emit_cc(m, oc // 4)

                emit_ffn_app(sch, sq_sb, wr, g1, first=True, ln_done=ffn1_ln, dq_hook=dq_hook)
                sch.flush(blk)
        cc_all = ctr.get("cc", 0)


        if debug:
            with nc.Block() as blk:
                sch = Sched(ctr, sems)
                sch.op("sync", lambda e: e.dma_start(
                    out=dbg["x1T"].rearrange("(c p) t -> p c t", p=128), in_=xt[:]), inc="ld", n=16)
                sch.op("sync", lambda e: e.dma_start(
                    out=dbg["yT"].rearrange("(c p) t -> p c t", p=128), in_=y[:]), inc="ld", n=16)
                sch.op("sync", lambda e: e.dma_start(out=dbg["wg"][:], in_=w_out["g"][3]), inc="ld", n=16)
                c = sch.op("sync", lambda e: e.dma_start(out=dbg["wd"][:], in_=w_out["d"][3]), inc="ld", n=16)
                sch.wait("sync", "ld", c)
                sch.flush(blk)

        TQ = 256
        with ExitStack() as c2:
            wra = c2.enter_context(nc.sbuf_tensor("wra", [128, 2, 16, 128], BF16))
            cqb = c2.enter_context(nc.sbuf_tensor("cqb", [128, 2, 4, TQ], BF16))
            ckvb = c2.enter_context(nc.sbuf_tensor("ckvb", [128, 2, 4, TQ], BF16))
            qcb = c2.enter_context(nc.sbuf_tensor("qcb", [128, 2, 16, TQ], BF16))
            kcb = c2.enter_context(nc.sbuf_tensor("kcb", [128, 2, 16, TQ], BF16))
            qrb = c2.enter_context(nc.sbuf_tensor("qrb", [128, 2, 8, TQ], BF16))
            krb = c2.enter_context(nc.sbuf_tensor("krb", [128, 2, 8, TQ], BF16))
            nm16 = c2.enter_context(nc.sbuf_tensor("nm16", [128, 16], F32))
            rs16 = c2.enter_context(nc.sbuf_tensor("rs16", [128, 16], F32))
            rs16b = c2.enter_context(nc.sbuf_tensor("rs16b", [128, 16], F32))
            ao = c2.enter_context(nc.sbuf_tensor("ao", [128, 16, TQ], BF16))

            qr0 = shF[:, 0:TQ]
            rtm1 = shF[:, TQ:2 * TQ]
            rtm2 = shF[:, 2 * TQ:3 * TQ]
            sq_base = 3 * TQ
            tmp_off_attn = 5 * TQ
            o_t = sh[:, 4096:6144]
            q_t = sh[:, 6144:9216]
            k_t = sh[:, 9216:12288]
            v_t = sh[:, 12288:14336]
            rope = sh[:, 14336:14848]
            sc256 = shF[:, 7424:7680]
            es256 = shF[:, 7680:7936]
            at256 = sh[:, 15872:16128]

            def y1s(q):
                return q % 2
            q192 = q_t.rearrange("p (i d) -> p i d", d=192)
            k192 = k_t.rearrange("p (i d) -> p i d", d=192)

            proj_specs = [
                ("cq", "Wc", "bc", "y", 16, 4),
                ("ckv", "Wckv", "bckv", "y", 16, 4),
                ("qc", "Wcq", "bcq", "cq", 4, 16),
                ("qr", "Wqr", "bqr", "cq", 4, 8),
                ("kc", "Wck", "bck", "ckv", 4, 16),
                ("kr", "Wkr", "bkr", "y", 16, 8),
                ("v", "Wv", "bv", "ckv", 4, 16),
            ]

            with nc.Block() as blk:
                sch = Sched(ctr, sems)

                wa_hist = []
                bank_free = {}
                tp_ev = {}
                rot_free = [0]
                prev_av = [0]
                ln_chain = [0, 0]
                qst = {}

                def wa_load(fn):
                    slot = len(wa_hist) % 2
                    if len(wa_hist) >= 2:
                        sch.wait("gpsimd", "pe", wa_hist[-2][1])
                    c_w = sch.op("gpsimd", lambda e, fn=fn, slot=slot: fn(e, slot), inc="wa%d" % slot, n=16)
                    ent = [c_w, None]
                    wa_hist.append(ent)
                    return slot, c_w, ent

                def s1(q):
                    t0 = q * TQ
                    ys = y1s(q)
                    st = {"t0": t0, "ys": ys, "dst_ready": {}}
                    qst[q] = st
                    sch.wait("sync", "ve", rot_free[0])
                    c_rl = sch.op("sync", lambda e, t0=t0: e.dma_start(
                        out=rope.rearrange("p (c t) -> p c t", c=2),
                        in_=rope_d.rearrange("p (c t) -> p c t", c=2)[:, :, t0:t0 + TQ]), inc="pre", n=16)
                    sch.wait("vector", "pre", c_rl)
                    sch.wait("scalar", "ve", prev_av[0])
                    ln_guard = max(ln_chain[0], bank_free.get(4, 0), bank_free.get(5, 0)) if bank_free else ln_chain[0]
                    ln_done, stats = emit_ln_half(sch, t0, ys * TQ, None, tw=TQ,
                                                  ve_guard=ln_guard, pe_guard=ln_chain[1],
                                                  sq_ap=lambda slot: shF[:, sq_base + slot * TQ: sq_base + (slot + 1) * TQ],
                                                  tmp_off=tmp_off_attn)
                    ln_chain[0] = ln_done
                    ln_chain[1] = stats
                    st["ln_done"] = ln_done
                    dst_ready = st["dst_ready"]

                    def rhs_of(src, k):
                        if src == "y":
                            return y[:, k, ys * TQ:(ys + 1) * TQ]
                        if src == "cq":
                            return cqb[:, ys, k, :]
                        return ckvb[:, ys, k, :]

                    pb = [0]
                    for (pname, wname, bname, ksrc, kch, mch) in proj_specs:
                        for m in range(mch):
                            slot, c_w, ent = wa_load(
                                lambda e, slot, wname=wname, kch=kch, m=m: e.dma_start(
                                    out=wra[:, slot, 0:kch, :],
                                    in_=aw[wname].rearrange("(k p) n -> p k n", p=128)[:, :, m * 128:(m + 1) * 128]))
                            bank = 2 + (pb[0] % 4)
                            pb[0] += 1
                            sch.wait("tensor", "wa%d" % slot, c_w)
                            if pname == "cq" and m == 0:
                                sch.wait("tensor", "ve", ln_done)
                            if bank in bank_free:
                                sch.wait("tensor", "ve", bank_free[bank])
                            if pname in ("qc", "qr") and m == 0:
                                sch.wait("tensor", "ve", dst_ready[("cq", 3)])
                            if pname in ("kc", "v") and m == 0:
                                sch.wait("tensor", "ve", dst_ready[("ckv", 3)])
                            grp = 0
                            for k in range(kch):
                                grp = sch.op("tensor", lambda e, slot=slot, k=k, ksrc=ksrc, bank=bank: e.matmul(
                                    P[bank][:, 0:TQ], wra[:, slot, k, :], rhs_of(ksrc, k),
                                    start=(k == 0), stop=(k == kch - 1)),
                                    inc="pe" if k == kch - 1 else None)
                            ent[1] = grp
                            sch.wait("vector", "pe", grp)
                            if pname in ("qr", "kr"):
                                c1v = sch.op("vector", lambda e, bank=bank, bname=bname, m=m: e.tensor_scalar(
                                    qr0[:], P[bank][:, 0:TQ], bias_ap(bname, m), None, ALU.add), inc="ve")
                                bank_free[bank] = c1v
                                sch.wait("tensor", "ve", c1v)
                                sch.wait("tensor", "ve", rot_free[0])
                                vm, am = tp_ev.get(0, (0, 0))
                                sch.wait("tensor", "ve", vm)
                                sch.wait("tensor", "ac", am)
                                c_rot = sch.op("tensor", lambda e: e.matmul(
                                    PTf[0][:, 0:TQ], rotm[:], qr0[:], start=True, stop=True), inc="pe")
                                sch.wait("vector", "pe", c_rot)
                                cosap = rope[:, 0:TQ]
                                sinap = rope[:, TQ:2 * TQ]
                                sch.op("vector", lambda e, cosap=cosap: e.tensor_tensor(rtm1[:], qr0[:], cosap, ALU.mult))
                                sch.op("vector", lambda e, sinap=sinap: e.tensor_tensor(rtm2[:], PTf[0][:, 0:TQ], sinap, ALU.mult))
                                dstap = (qrb if pname == "qr" else krb)[:, ys, m, :]
                                c2v = sch.op("vector", lambda e, dstap=dstap: e.tensor_add(dstap, rtm1[:], rtm2[:]), inc="ve")
                                rot_free[0] = c2v
                                dst_ready[(pname, m)] = c2v
                            else:
                                if pname == "v":
                                    dstap = y[:, m, (2 + ys) * TQ:(3 + ys) * TQ]
                                else:
                                    dstap = {"cq": cqb, "ckv": ckvb, "qc": qcb, "kc": kcb}[pname][:, ys, m, :]
                                c1v = sch.op("vector", lambda e, dstap=dstap, bank=bank, bname=bname, m=m: e.tensor_scalar(
                                    dstap, P[bank][:, 0:TQ], bias_ap(bname, m), None, ALU.add), inc="ve")
                                bank_free[bank] = c1v
                                dst_ready[(pname, m)] = c1v
                    st["proj_all"] = sch.val("ve")

                def tgroup(srcs, dst_builder, gi, split4=False):
                    bank = gi % 2
                    vm, am = tp_ev.get(bank, (0, 0))
                    sch.wait("tensor", "ve", vm)
                    sch.wait("tensor", "ac", am)
                    if bank == 0:
                        sch.wait("tensor", "ve", rot_free[0])
                    c_tp = 0
                    for idx, (srcap,) in enumerate(srcs):
                        c_tp = sch.op("tensor", lambda e, srcap=srcap, idx=idx, bank=bank: e.transpose(
                            PT[bank][:, idx * 128:(idx + 1) * 128], srcap, eyeb[:]),
                            inc="pe" if idx == 7 else None)
                    eng = "vector" if gi % 3 != 2 else "scalar"
                    sch.wait(eng, "pe", c_tp)
                    if eng == "scalar" and prev_av[0]:
                        sch.wait(eng, "ve", prev_av[0])
                    if split4:
                        src_ap = PT[bank].rearrange("p (g two d) -> p g two d", two=2, d=64)
                    else:
                        src_ap = PT[bank].rearrange("p (g q) -> p g q", q=128)
                    if eng == "vector":
                        c = sch.op(eng, lambda e, b=dst_builder, s=src_ap: e.tensor_copy(b(), s), inc="ve")
                        tp_ev[bank] = (c, am)
                    else:
                        c = sch.op(eng, lambda e, b=dst_builder, s=src_ap: e.copy(b(), s), inc="ac")
                        tp_ev[bank] = (vm, c)
                    return tp_ev[bank]

                def s2(q):
                    st = qst[q]
                    ys = st["ys"]
                    for sti in range(TQ // 128):
                        s0 = sti * 128
                        sch.wait("tensor", "ve", st["proj_all"])
                        gi = 0
                        marks = []
                        for half in range(2):
                            marks.append(tgroup([(qcb[:, ys, half * 8 + j, s0:s0 + 128],) for j in range(8)],
                                                lambda half=half: q192[:, half * 8:(half + 1) * 8, 0:128], gi))
                            gi += 1
                        marks.append(tgroup([(qrb[:, ys, j, s0:s0 + 128],) for j in range(8)],
                                            lambda: q192[:, :, 128:192].rearrange("p (a two) d -> p a two d", two=2),
                                            gi, split4=True))
                        gi += 1
                        for half in range(2):
                            marks.append(tgroup([(kcb[:, ys, half * 8 + j, s0:s0 + 128],) for j in range(8)],
                                                lambda half=half: k192[:, half * 8:(half + 1) * 8, 0:128], gi))
                            gi += 1
                        marks.append(tgroup([(krb[:, ys, j, s0:s0 + 128],) for j in range(8)],
                                            lambda: k192[:, :, 128:192].rearrange("p (a two) d -> p a two d", two=2),
                                            gi, split4=True))
                        gi += 1
                        for half in range(2):
                            marks.append(tgroup([(y[:, half * 8 + j, (2 + ys) * TQ + s0:(2 + ys) * TQ + s0 + 128],) for j in range(8)],
                                                lambda half=half: v_t.rearrange("p (j d) -> p j d", d=128)[:, half * 8:(half + 1) * 8, :], gi))
                            gi += 1
                        for vm, am in marks:
                            sch.wait("vector", "ve", vm)
                            sch.wait("vector", "ac", am)
                        prod = shF[:, 0:3072].rearrange("p (j d) -> p j d", d=192)
                        c_sc = 0
                        for i in range(H):
                            qi = q192[:, i:i + 1, :].broadcast_to([128, H, 192])
                            sch.op("vector", lambda e, qi=qi, prod=prod: e.tensor_tensor(prod, qi, k192, ALU.mult))
                            c_sc = sch.op("vector", lambda e, i=i, prod=prod: e.tensor_reduce(
                                sc256[:, i * 16:(i + 1) * 16], prod, AX.X, ALU.add),
                                inc="ve" if i == H - 1 else None)
                        sch.wait("vector", "ve", c_sc)
                        sc3 = sc256.rearrange("p (i j) -> p i j", j=16)
                        c_r1 = sch.op("vector", lambda e, sc3=sc3: e.tensor_reduce(nm16[:], sc3, AX.X, ALU.max, negate=True), inc="ve")
                        sch.wait("vector", "ve", c_r1)
                        c_sm = sch.op("vector", lambda e, sc3=sc3: e.tensor_tensor(
                            sc3, sc3,
                            nm16.rearrange("p (i one) -> p i one", one=1).broadcast_to([128, 16, 16]),
                            ALU.add), inc="ve")
                        sch.wait("scalar", "ve", c_sm)
                        c_exp = sch.op("scalar", lambda e: e.activation(es256[:], sc256[:], AF.Exp, scale=SCALER), inc="ac")
                        sch.wait("vector", "ac", c_exp)
                        es3 = es256.rearrange("p (i j) -> p i j", j=16)
                        c_r2 = sch.op("vector", lambda e, es3=es3: e.tensor_reduce(rs16[:], es3, AX.X, ALU.add), inc="ve")
                        sch.wait("vector", "ve", c_r2)
                        c_r3 = sch.op("vector", lambda e: e.reciprocal(rs16b[:], rs16[:]), inc="ve")
                        sch.wait("vector", "ve", c_r3)
                        at3 = at256.rearrange("p (i j) -> p i j", j=16)
                        sch.op("vector", lambda e, es3=es3, at3=at3: e.tensor_tensor(
                            at3, es3,
                            rs16b.rearrange("p (i one) -> p i one", one=1).broadcast_to([128, 16, 16]),
                            ALU.mult))
                        vv = v_t.rearrange("p (j d) -> p d j", d=128)
                        prod2 = shF[:, 0:2048].rearrange("p (d j) -> p d j", j=16)

                        def av_reduce(e, i, prod2):
                            with nc.allow_low_precision(reason="attn@v: fp32 accum, bf16 matmul operand out"):
                                return e.tensor_reduce(o_t[:, i * 128:(i + 1) * 128], prod2, AX.X, ALU.add)

                        c_av = 0
                        for i in range(H):
                            ai = at3[:, i:i + 1, :].broadcast_to([128, 128, 16])
                            sch.op("vector", lambda e, ai=ai, vv=vv, prod2=prod2: e.tensor_tensor(prod2, ai, vv, ALU.mult))
                            c_av = sch.op("vector", lambda e, i=i, prod2=prod2: av_reduce(e, i, prod2),
                                          inc="ve" if i == H - 1 else None)
                        prev_av[0] = c_av
                        sch.wait("tensor", "ve", c_av)
                        for half in range(2):
                            bank = half
                            vm, am = tp_ev.get(bank, (0, 0))
                            sch.wait("tensor", "ve", vm)
                            sch.wait("tensor", "ac", am)
                            if bank == 0:
                                sch.wait("tensor", "ve", rot_free[0])
                            c_tp = 0
                            for idx in range(8):
                                mch = half * 8 + idx
                                c_tp = sch.op("tensor", lambda e, mch=mch, idx=idx, bank=bank: e.transpose(
                                    PT[bank][:, idx * 128:(idx + 1) * 128],
                                    o_t[:, mch * 128:(mch + 1) * 128], eyeb[:]),
                                    inc="pe" if idx == 7 else None)
                            eng = "vector" if half == 0 else "scalar"
                            sch.wait(eng, "pe", c_tp)
                            if eng == "scalar":
                                sch.wait(eng, "ve", c_av)
                            dstb = (lambda half=half, s0=s0: ao[:, half * 8:(half + 1) * 8, s0:s0 + 128])
                            src_ap = PT[bank].rearrange("p (g q) -> p g q", q=128)
                            if eng == "vector":
                                c = sch.op(eng, lambda e, dstb=dstb, s=src_ap: e.tensor_copy(dstb(), s), inc="ve")
                                tp_ev[bank] = (c, am)
                            else:
                                c = sch.op(eng, lambda e, dstb=dstb, s=src_ap: e.copy(dstb(), s), inc="ac")
                                tp_ev[bank] = (vm, c)
                    st["ao_marks"] = (tp_ev[0][0], tp_ev[1][1])

                def s3(q):
                    st = qst[q]
                    t0 = st["t0"]
                    sch.wait("tensor", "ve", st["ao_marks"][0])
                    sch.wait("tensor", "ac", st["ao_marks"][1])
                    for m in range(16):
                        slot, c_w, ent = wa_load(
                            lambda e, slot, m=m: e.dma_start(
                                out=wra[:, slot, :, :],
                                in_=aw["Wo"].rearrange("(k p) n -> p k n", p=128)[:, :, m * 128:(m + 1) * 128]))
                        bank = 2 + (m % 2)
                        sch.wait("tensor", "wa%d" % slot, c_w)
                        if bank in bank_free:
                            sch.wait("tensor", "ve", bank_free[bank])
                        grp = 0
                        for k in range(KCH):
                            grp = sch.op("tensor", lambda e, slot=slot, k=k, bank=bank: e.matmul(
                                P[bank][:, 0:TQ], wra[:, slot, k, :], ao[:, k, :],
                                start=(k == 0), stop=(k == KCH - 1)),
                                inc="pe" if k == KCH - 1 else None)
                        ent[1] = grp
                        sch.wait("vector", "pe", grp)
                        c_r = sch.op("vector", lambda e, m=m, bank=bank, t0=t0: e.scalar_tensor_tensor(
                            xt[:, m, t0:t0 + TQ], P[bank][:, 0:TQ], bias_ap("bo", m),
                            xt[:, m, t0:t0 + TQ], ALU.add, ALU.add), inc="ve")
                        bank_free[bank] = c_r

                s1(0)
                for q in range(4):
                    if q < 3:
                        s1(q + 1)
                    s2(q)
                    s3(q)
                sch.flush(blk)

        if debug:
            with nc.Block() as blk:
                sch = Sched(ctr, sems)
                c = sch.op("sync", lambda e: e.dma_start(
                    out=dbg["x2T"].rearrange("(c p) t -> p c t", p=128), in_=xt[:]), inc="ld", n=16)
                sch.wait("sync", "ld", c)
                sch.flush(blk)

        with ExitStack() as c3:
            sq_sb = c3.enter_context(nc.sbuf_tensor("sq3", [128, 2, 512], F32))
            wr = {m: c3.enter_context(nc.sbuf_tensor(f"ring3{m}", [128, 2, 16, 128], BF16)) for m in ("g", "u", "d")}
            g1 = c3.enter_context(nc.sbuf_tensor("g1b", [128, 2, 512], F32))
            with nc.Block() as blk:
                sch = Sched(ctr, sems)
                emit_ffn_app(sch, sq_sb, wr, g1, first=False)
                sch.flush(blk)

        with ExitStack() as c4:
            ost = c4.enter_context(nc.sbuf_tensor("ost", [128, 2, D], F32))
            eyef = c4.enter_context(nc.sbuf_tensor("eyef_sb", [128, 128], F32))
            with nc.Block() as blk:
                sch = Sched(ctr, sems)
                c_ef = sch.op("sync", lambda e: e.dma_start(out=eyef[:], in_=eyef_d[:]), inc="pre", n=16)
                sch.wait("tensor", "pre", c_ef)
                ost_dma = {}
                pt_free = {}
                for st8 in range(8):
                    slot = st8 % 2
                    last_ev = 0
                    for grp4 in range(4):
                        bank = grp4 % 2
                        if bank in pt_free:
                            sch.wait("tensor", "ac", pt_free[bank])
                        c_tp = 0
                        for idx in range(4):
                            mch = grp4 * 4 + idx
                            c_tp = sch.op("tensor", lambda e, mch=mch, idx=idx, st8=st8, bank=bank: e.transpose(
                                PTf[bank][:, idx * 128:(idx + 1) * 128],
                                xt[:, mch, st8 * 128:(st8 + 1) * 128], eyef[:]),
                                inc="pe" if idx == 3 else None)
                        sch.wait("scalar", "pe", c_tp)
                        if st8 >= 2:
                            sch.wait("scalar", "os%d" % slot, ost_dma[st8 - 2])
                        last_ev = sch.op("scalar", lambda e, grp4=grp4, slot=slot, bank=bank: e.copy(
                            ost[:, slot, grp4 * 512:(grp4 + 1) * 512], PTf[bank][:, :]), inc="ac")
                        pt_free[bank] = last_ev
                    sch.wait("sync", "ac", last_ev)
                    ost_dma[st8] = sch.op("sync", lambda e, st8=st8, slot=slot: e.dma_start(
                        out=out_d[st8 * 128:(st8 + 1) * 128, :], in_=ost[:, slot]), inc="os%d" % slot, n=16)
                sch.wait("sync", "os0", sch.val("os0"))
                sch.wait("sync", "os1", sch.val("os1"))
                sch.flush(blk)

    return nc


_CACHE = {}


def _host_constants():
    j = np.arange(32, dtype=np.float32)
    inv_freq = (1.0 / (10000.0 ** (j / np.float32(32.0)))).astype(np.float32)
    dh = np.arange(128) % 64
    fr = inv_freq[dh % 32]
    R = np.zeros((128, 128), np.float32)
    for h2 in range(2):
        b = h2 * 64
        for dd in range(32):
            R[b + dd + 32, b + dd] = -1.0
            R[b + dd, b + dd + 32] = 1.0
    return fr, R


def kernel(**inputs):
    inp = {k: np.asarray(v) for k, v in inputs.items()}
    h_t = np.asarray(inp["L0_attn_h_t"], np.float32).reshape(S, D)

    fr, R = _host_constants()
    eye_bf = np.eye(128, dtype=ml_dtypes.bfloat16)
    eye_f = np.eye(128, dtype=np.float32)

    def pack_bias(v, chunks):
        return np.ascontiguousarray(np.asarray(v, np.float32).reshape(chunks, 128).T)

    bias_host = np.ascontiguousarray(np.concatenate([
        pack_bias(inp["L0_attn_bc"], 4),
        pack_bias(inp["L0_attn_bcq"], 16),
        pack_bias(inp["L0_attn_bqr"], 8),
        pack_bias(inp["L0_attn_bckv"], 4),
        pack_bias(inp["L0_attn_bck"], 16),
        pack_bias(inp["L0_attn_bkr"], 8),
        pack_bias(inp["L0_attn_bv"], 16),
        pack_bias(inp["L0_attn_bo"], 16),
    ], axis=1))

    if "nc" not in _CACHE:
        _CACHE["nc"] = build_program(debug=False)
    nc = _CACHE["nc"]

    in_maps = []
    for c in range(NCORES):
        t0 = c * T
        tpos = np.arange(t0, t0 + T, dtype=np.float32)
        ang = tpos[None, :] * fr[:, None]
        rope_cs = np.ascontiguousarray(
            np.concatenate([np.cos(ang), np.sin(ang)], axis=1).astype(ml_dtypes.bfloat16))
        m = {
            "xT": np.ascontiguousarray(h_t[t0:t0 + T, :].T),
            "biases": bias_host,
            "rope_cs": rope_cs,
            "rotm": R,
            "eye_bf": eye_bf,
            "eye_f32": eye_f,
            "Wc": np.asarray(inp["L0_attn_Wc"], np.float32),
            "Wcq": np.asarray(inp["L0_attn_Wcq"], np.float32),
            "Wqr": np.asarray(inp["L0_attn_Wqr"], np.float32),
            "Wckv": np.asarray(inp["L0_attn_Wckv"], np.float32),
            "Wck": np.asarray(inp["L0_attn_Wck"], np.float32),
            "Wkr": np.asarray(inp["L0_attn_Wkr"], np.float32),
            "Wv": np.asarray(inp["L0_attn_Wv"], np.float32),
            "Wo": np.asarray(inp["L0_attn_Wo"], np.float32),
            "g_pk": np.ascontiguousarray(inp["L0_ffn_gate_W_packed"][c * 2048:(c + 1) * 2048, :]).astype(np.int32),
            "g_sc": np.ascontiguousarray(inp["L0_ffn_gate_scales"][c * 2048:(c + 1) * 2048, :]).astype(np.float32),
            "u_pk": np.ascontiguousarray(inp["L0_ffn_up_W_packed"][c * 2048:(c + 1) * 2048, :]).astype(np.int32),
            "u_sc": np.ascontiguousarray(inp["L0_ffn_up_scales"][c * 2048:(c + 1) * 2048, :]).astype(np.float32),
            "d_pk": np.ascontiguousarray(inp["L0_ffn_down_W_packed"][:, c * 1024:(c + 1) * 1024]).astype(np.int32),
            "d_sc": np.ascontiguousarray(inp["L0_ffn_down_scales"][:, c * 64:(c + 1) * 64]).astype(np.float32),
        }
        in_maps.append(m)

    res = run_bass_kernel_spmd(nc, in_maps, core_ids=list(range(NCORES)))
    out = np.concatenate([np.asarray(res.results[c]["out"]) for c in range(NCORES)], axis=0)
    return out.reshape(1, S, D).astype(np.float32)


# version 08
# speedup vs baseline: 1.0695x; 1.0300x over previous
import math
import numpy as np
import ml_dtypes
import concourse.bass as bass
import concourse.mybir as mybir
from concourse.bass_utils import run_bass_kernel_spmd
from contextlib import ExitStack

F32 = mybir.dt.float32
BF16 = mybir.dt.bfloat16
I32 = mybir.dt.int32
AF = mybir.ActivationFunctionType
ALU = mybir.AluOpType
AX = mybir.AxisListType

NCORES = 8
S = 8192
T = S // NCORES
D = 2048
DC = 512
DR = 1024
NBLK = 8
KCH = 16
H = 16
SCALER = 1.0 / math.sqrt(192.0)
C_EXP = 126 << 23
EPS = 1e-5

ENGINES = ("sync", "gpsimd", "vector", "scalar", "tensor")


class Sched:

    def __init__(self, ctr, sems):
        self.q = {e: [] for e in ENGINES}
        self.ctr = ctr
        self.sems = sems

    def wait(self, eng, sname, val):
        if val > 0:
            self.q[eng].append(lambda e, s=self.sems[sname], v=val: e.wait_ge(s, v))

    def op(self, eng, fn, inc=None, n=1):
        if inc is None:
            self.q[eng].append(fn)
            return self.ctr.get("_", 0)
        self.ctr[inc] = self.ctr.get(inc, 0) + n
        cnt = self.ctr[inc]
        sem = self.sems[inc]

        def run(e, fn=fn, sem=sem, n=n):
            fn(e).then_inc(sem, n)

        self.q[eng].append(run)
        return cnt

    def val(self, name):
        return self.ctr.get(name, 0)

    def flush(self, blk):
        for ename in ENGINES:
            ops = self.q[ename]
            if not ops:
                continue

            def body(e, ops=ops):
                for f in ops:
                    f(e)

            getattr(blk, ename)(body)
        self.q = {e: [] for e in ENGINES}


def build_program(debug=False):
    nc = bass.Bass("TRN2", target_bir_lowering=False)

    xT_d = nc.dram_tensor("xT", [D, T], F32, kind="ExternalInput")
    out_d = nc.dram_tensor("out", [T, D], F32, kind="ExternalOutput")
    aw = {}
    for name, shp in (("Wc", [D, DC]), ("Wcq", [DC, D]), ("Wqr", [DC, DR]),
                      ("Wckv", [D, DC]), ("Wck", [DC, D]), ("Wkr", [D, DR]),
                      ("Wv", [DC, D]), ("Wo", [D, D])):
        aw[name] = nc.dram_tensor(name, shp, F32, kind="ExternalInput")
    bias_d = nc.dram_tensor("biases", [128, 88], F32, kind="ExternalInput")
    rope_d = nc.dram_tensor("rope_cs", [128, 2 * T], BF16, kind="ExternalInput")
    rotm_d = nc.dram_tensor("rotm", [128, 128], F32, kind="ExternalInput")
    eyeb_d = nc.dram_tensor("eye_bf", [128, 128], BF16, kind="ExternalInput")
    eyef_d = nc.dram_tensor("eye_f32", [128, 128], F32, kind="ExternalInput")
    pk_d = {m: nc.dram_tensor(f"{m}_pk", [2048, 1024], I32, kind="ExternalInput")
            for m in ("g", "u", "d")}
    sc_d = {m: nc.dram_tensor(f"{m}_sc", [2048, 64], F32, kind="ExternalInput")
            for m in ("g", "u", "d")}
    w_in = {m: nc.dram_tensor(f"{m}_in", [16, 128, 16, 128], BF16) for m in ("g", "u", "d")}
    aw_bf = {n: nc.dram_tensor(n + "_bf", list(aw[n].shape), BF16) for n in aw}
    w_out = {m: nc.dram_tensor(f"{m}_out", [4, NBLK, 4, 128, 16, 128], BF16) for m in ("g", "u", "d")}

    dbg = {}
    if debug:
        dbg["x1T"] = nc.dram_tensor("dbg_x1T", [D, T], F32, kind="ExternalOutput")
        dbg["x2T"] = nc.dram_tensor("dbg_x2T", [D, T], F32, kind="ExternalOutput")
        dbg["yT"] = nc.dram_tensor("dbg_yT", [D, T], BF16, kind="ExternalOutput")
        dbg["wg"] = nc.dram_tensor("dbg_wg", [16, 128, 16, 128], BF16, kind="ExternalOutput")
        dbg["wd"] = nc.dram_tensor("dbg_wd", [16, 128, 16, 128], BF16, kind="ExternalOutput")
        dbg["q"] = nc.dram_tensor("dbg_q", [128, 3072], BF16, kind="ExternalOutput")
        dbg["k"] = nc.dram_tensor("dbg_k", [128, 3072], BF16, kind="ExternalOutput")
        dbg["v"] = nc.dram_tensor("dbg_v", [128, 2048], BF16, kind="ExternalOutput")
        dbg["sc"] = nc.dram_tensor("dbg_sc", [128, 256], F32, kind="ExternalOutput")
        dbg["es"] = nc.dram_tensor("dbg_es", [128, 256], F32, kind="ExternalOutput")
        dbg["at"] = nc.dram_tensor("dbg_at", [128, 256], BF16, kind="ExternalOutput")
        dbg["ot"] = nc.dram_tensor("dbg_ot", [128, 2048], BF16, kind="ExternalOutput")
        dbg["rs"] = nc.dram_tensor("dbg_rs", [128, 16], F32, kind="ExternalOutput")
        dbg["rsb"] = nc.dram_tensor("dbg_rsb", [128, 16], F32, kind="ExternalOutput")
        dbg["nm"] = nc.dram_tensor("dbg_nm", [128, 16], F32, kind="ExternalOutput")

    ctr = {}

    with ExitStack() as top:
        def sb(name, shape, dtype):
            return top.enter_context(nc.sbuf_tensor(name, shape, dtype))

        xt = sb("xt", [128, KCH, T], F32)
        y = sb("y", [128, KCH, T], BF16)
        sh = sb("sh", [128, 16384], BF16)
        biases = sb("biases_sb", [128, 88], F32)
        rotm = sb("rotm_sb", [128, 128], F32)
        eyeb = sb("eyeb_sb", [128, 128], BF16)
        ones_c = sb("ones_c", [128, 1], F32)
        ones_r = sb("ones_r", [1, 128], F32)
        rows = sb("rows", [1, 3, 512], F32)

        shF = sh.bitcast(F32)
        inter = sh.rearrange("p (c t) -> p c t", c=KCH)

        P = [top.enter_context(nc.psum_tensor(f"P{i}", [128, 512], F32)) for i in range(6)]
        PT = [top.enter_context(nc.psum_tensor(f"PT{i}", [128, 1024], BF16)) for i in range(2)]
        PTf = [t.bitcast(F32) for t in PT]

        sems = {k: top.enter_context(nc.semaphore("s_" + k))
                for k in ("ld", "gd", "cc", "pe", "ve", "ac", "pre",
                          "pk0", "pk1", "st0", "st1",
                          "wg0", "wg1", "wu0", "wu1", "wd0", "wd1",
                          "wa0", "wa1", "os0", "os1")}

        bias_cols = {}
        off = 0
        for bn, w in (("bc", 4), ("bcq", 16), ("bqr", 8), ("bckv", 4),
                      ("bck", 16), ("bkr", 8), ("bv", 16), ("bo", 16)):
            bias_cols[bn] = off
            off += w

        def bias_ap(bn, chunk):
            c = bias_cols[bn] + chunk
            return biases[:, c:c + 1]

        pre_total = [0]
        cc_at = {("g", 0): 1, ("u", 0): 2}
        _n = 2
        for _q in range(1, 4):
            _n += 1; cc_at[("g", _q)] = _n
            _n += 1; cc_at[("u", _q)] = _n
        for _q in range(4):
            _n += 1; cc_at[("d", _q)] = _n

        def emit_ln_half(sch, t0, y_off, sq_sb, ve_guard=0, pe_guard=0, tw=512,
                         sq_ap=None, tmp_off=3072):
            q_pe = {}
            sch.wait("tensor", "pre", pre_total[0])
            sch.wait("scalar", "pre", pre_total[0])
            sch.wait("vector", "pre", pre_total[0])
            if ve_guard:
                sch.wait("tensor", "ve", ve_guard)
            if pe_guard:
                sch.wait("scalar", "pe", pe_guard)
            for ch in range(KCH):
                slot = ch % 2
                if ch >= 2:
                    sch.wait("scalar", "pe", q_pe[ch - 2])
                sqv = sq_ap(slot) if sq_ap is not None else sq_sb[:, slot, 0:tw]
                c_ac = sch.op("scalar",
                              lambda e, ch=ch, sqv=sqv: e.activation(
                                  sqv, xt[:, ch, t0:t0 + tw], AF.Square),
                              inc="ac")
                sch.wait("tensor", "ac", c_ac)
                sch.op("tensor",
                       lambda e, ch=ch: e.matmul(P[4][0:1, 0:tw], ones_c[:], xt[:, ch, t0:t0 + tw],
                                                 start=(ch == 0), stop=(ch == KCH - 1)))
                q_pe[ch] = sch.op("tensor",
                                  lambda e, ch=ch, sqv=sqv: e.matmul(
                                      P[5][0:1, 0:tw], ones_c[:], sqv,
                                      start=(ch == 0), stop=(ch == KCH - 1)),
                                  inc="pe")
            stats_done = q_pe[KCH - 1]
            sch.wait("vector", "pe", stats_done)
            sch.op("vector", lambda e: e.tensor_scalar(rows[:, 0, 0:tw], P[4][0:1, 0:tw], 1.0 / D, None, ALU.mult))
            sch.op("vector", lambda e: e.tensor_scalar(rows[:, 1, 0:tw], P[5][0:1, 0:tw], 1.0 / D, None, ALU.mult))
            sch.op("vector", lambda e: e.tensor_tensor(rows[:, 2, 0:tw], rows[:, 0, 0:tw], rows[:, 0, 0:tw], ALU.mult))
            sch.op("vector", lambda e: e.tensor_tensor(rows[:, 2, 0:tw], rows[:, 1, 0:tw], rows[:, 2, 0:tw], ALU.subtract))
            c_ve = sch.op("vector", lambda e: e.tensor_scalar(rows[:, 2, 0:tw], rows[:, 2, 0:tw], EPS, None, ALU.add),
                          inc="ve")
            sch.wait("scalar", "ve", c_ve)
            c_ac = sch.op("scalar", lambda e: e.activation(rows[:, 1, 0:tw], rows[:, 2, 0:tw], AF.Sqrt),
                          inc="ac")
            sch.wait("vector", "ac", c_ac)
            c_ve = sch.op("vector", lambda e: e.reciprocal(rows[:, 2, 0:tw], rows[:, 1, 0:tw]), inc="ve")
            sch.wait("tensor", "ve", c_ve)
            sch.op("tensor", lambda e: e.matmul(P[0][:, 0:tw], ones_r[:], rows[:, 0, 0:tw], start=True, stop=True))
            bc_done = sch.op("tensor", lambda e: e.matmul(P[1][:, 0:tw], ones_r[:], rows[:, 2, 0:tw], start=True, stop=True),
                             inc="pe")
            sch.wait("vector", "pe", bc_done)
            tmp = shF[:, tmp_off:tmp_off + tw]
            for ch in range(KCH):
                sch.op("vector", lambda e, ch=ch, tmp=tmp: e.tensor_tensor(
                    tmp, xt[:, ch, t0:t0 + tw], P[0][:, 0:tw], ALU.subtract))
                c_n = sch.op("vector", lambda e, ch=ch, tmp=tmp: e.tensor_tensor(
                    y[:, ch, y_off:y_off + tw], tmp, P[1][:, 0:tw], ALU.mult),
                    inc="ve" if ch == KCH - 1 else None)
            return c_n, stats_done

        def emit_ffn_app(sch, sq_sb, wr, g1, first, ln_done=None, dq_hook=None):
            if ln_done is None:
                n0, sd0 = emit_ln_half(sch, 0, 0, sq_sb)
                ln_done, _ = emit_ln_half(sch, 512, 512, sq_sb, ve_guard=n0, pe_guard=sd0)

            dn_q = "gpsimd" if first else "sync"

            wload = {}
            pe_grp = {}
            gelu_at = {}
            inter_at = {}
            evac_at = {}
            down_pe_done = {}
            first_down = True

            for nb in range(NBLK):
                for mc in range(16):
                    gidx = nb * 16 + mc
                    if dq_hook is not None:
                        dq_hook(gidx)
                    slot = gidx % 2
                    if first and nb == 0:
                        sch.wait("sync", "cc", cc_at[("u", mc // 4)])
                    for m in ("g", "u"):
                        if gidx >= 2:
                            sch.wait("sync", "pe", pe_grp[gidx - 2])
                        wload[(m, gidx)] = sch.op("sync", lambda e, m=m, nb=nb, mc=mc, slot=slot: e.dma_start(
                            out=wr[m][:, slot], in_=w_out[m][mc // 4, nb, mc % 4]), inc="w%s%d" % (m, slot), n=16)
                    for tt in (0, 1):
                        sch.wait("tensor", "wg%d" % slot, wload[("g", gidx)])
                        sch.wait("tensor", "wu%d" % slot, wload[("u", gidx)])
                        if gidx == 0 and tt == 0:
                            sch.wait("tensor", "ve", ln_done)
                        if gidx >= 1:
                            sch.wait("tensor", "ac", gelu_at[(gidx - 1, tt)])
                            sch.wait("tensor", "ve", inter_at[(gidx - 1, tt)])
                        for k in range(KCH):
                            sch.op("tensor", lambda e, slot=slot, k=k, tt=tt: e.matmul(
                                P[tt][:, :], wr["g"][:, slot, k, :], y[:, k, tt * 512:(tt + 1) * 512],
                                start=(k == 0), stop=(k == KCH - 1)))
                        grp_pe = 0
                        for k in range(KCH):
                            grp_pe = sch.op("tensor", lambda e, slot=slot, k=k, tt=tt: e.matmul(
                                P[2 + tt][:, :], wr["u"][:, slot, k, :], y[:, k, tt * 512:(tt + 1) * 512],
                                start=(k == 0), stop=(k == KCH - 1)),
                                inc="pe" if k == KCH - 1 else None)
                        if tt == 1:
                            pe_grp[gidx] = grp_pe
                        sch.wait("scalar", "pe", grp_pe)
                        gelu_at[(gidx, tt)] = sch.op("scalar", lambda e, tt=tt: e.activation(
                            g1[:, tt, :], P[tt][:, :], AF.Gelu_apprx_tanh), inc="ac")
                        sch.wait("vector", "ac", gelu_at[(gidx, tt)])
                        if nb >= 1 and mc == 0 and tt == 0:
                            sch.wait("vector", "pe", down_pe_done[nb - 1])
                        inter_at[(gidx, tt)] = sch.op("vector", lambda e, mc=mc, tt=tt: e.tensor_tensor(
                            inter[:, mc, tt * 512:(tt + 1) * 512], g1[:, tt, :], P[2 + tt][:, :], ALU.mult),
                            inc="ve")
                for hc in range(16):
                    hgidx = nb * 16 + hc
                    slot = hgidx % 2
                    if first and nb == 0:
                        sch.wait(dn_q, "cc", cc_at[("d", hc // 4)])
                    if hgidx >= 2:
                        sch.wait(dn_q, "pe", pe_grp[("d", hgidx - 2)])
                    wload[("d", hgidx)] = sch.op(dn_q, lambda e, nb=nb, hc=hc, slot=slot: e.dma_start(
                        out=wr["d"][:, slot], in_=w_out["d"][hc // 4, nb, hc % 4]), inc="wd%d" % slot, n=16)
                    for tt in (0, 1):
                        g = hgidx * 2 + tt
                        bank = 4 + (g % 2)
                        sch.wait("tensor", "wd%d" % slot, wload[("d", hgidx)])
                        if hc == 0 and tt == 0:
                            sch.wait("tensor", "ve", inter_at[(nb * 16 + 15, 1)])
                        if g >= 2:
                            sch.wait("tensor", "ve", evac_at[g - 2])
                        grp_pe = 0
                        for k in range(KCH):
                            grp_pe = sch.op("tensor", lambda e, slot=slot, k=k, tt=tt, bank=bank: e.matmul(
                                P[bank][:, :], wr["d"][:, slot, k, :], inter[:, k, tt * 512:(tt + 1) * 512],
                                start=(k == 0), stop=(k == KCH - 1)),
                                inc="pe" if k == KCH - 1 else None)
                        if tt == 1:
                            pe_grp[("d", hgidx)] = grp_pe
                        sch.wait("vector", "pe", grp_pe)
                        evac_at[g] = sch.op("vector", lambda e, hc=hc, tt=tt, bank=bank: e.scalar_tensor_tensor(
                            xt[:, hc, tt * 512:(tt + 1) * 512], P[bank][:, :], 0.5,
                            xt[:, hc, tt * 512:(tt + 1) * 512], ALU.mult, ALU.add),
                            inc="ve")
                down_pe_done[nb] = pe_grp[("d", nb * 16 + 15)]

        with ExitStack() as c0:
            pk_t = c0.enter_context(nc.sbuf_tensor("pk", [128, 2, 1024], I32))
            sc_t = c0.enter_context(nc.sbuf_tensor("sct", [128, 2, 64], F32))
            v0l = c0.enter_context(nc.sbuf_tensor("v0l", [128, 1024], I32))
            sl_t = c0.enter_context(nc.sbuf_tensor("sl", [128, 1024], I32))
            v8_t = c0.enter_context(nc.sbuf_tensor("v8", [128, 1024], F32))
            nat = c0.enter_context(nc.sbuf_tensor("nat", [128, 2, 2048], BF16))
            stage = c0.enter_context(nc.sbuf_tensor("stage", [128, 2, 16, 128], BF16))
            sq_sb = c0.enter_context(nc.sbuf_tensor("sq1", [128, 2, 512], F32))
            wr = {m: c0.enter_context(nc.sbuf_tensor(f"ring1{m}", [128, 2, 16, 128], BF16)) for m in ("g", "u", "d")}
            g1 = c0.enter_context(nc.sbuf_tensor("g1a", [128, 2, 512], F32))

            with nc.Block() as blk:
                sch = Sched(ctr, sems)
                sch.op("sync", lambda e: e.dma_start(out=xt[:], in_=xT_d.rearrange("(c p) t -> p c t", p=128)),
                       inc="pre", n=16)
                sch.op("sync", lambda e: e.dma_start(out=biases[:], in_=bias_d[:]), inc="pre", n=16)
                sch.op("sync", lambda e: e.dma_start(out=rotm[:], in_=rotm_d[:]), inc="pre", n=16)
                eye_ld = sch.op("sync", lambda e: e.dma_start(out=eyeb[:], in_=eyeb_d[:]), inc="pre", n=16)
                PRE_ALL = sch.val("pre")
                pre_total[0] = PRE_ALL
                for n in aw:
                    sch.op("gpsimd", lambda e, n=n: e.dma_start(out=aw_bf[n][:], in_=aw[n][:]), inc="gd", n=16)
                AWBF_ALL = sch.val("gd")
                sch.op("vector", lambda e: e.memset(ones_c[:], 1.0))
                sch.op("vector", lambda e: e.memset(ones_r[:], 1.0), inc="ve")

                n0_, sd0_ = emit_ln_half(sch, 0, 0, sq_sb)
                ffn1_ln, _ = emit_ln_half(sch, 512, 512, sq_sb, ve_guard=n0_, pe_guard=sd0_)

                dq_done = {}
                tp_done = {}
                pt_evac = {}
                stage_dma = {}

                dq_hist = []

                def emit_dq_chunk(m, oc, gl, qeng):
                    dq_hist.append(gl)
                    slot = (len(dq_hist) - 1) % 2
                    if len(dq_hist) >= 3:
                        sch.wait(qeng, "ve", dq_done[dq_hist[-3]])
                    pksem = "pk%d" % slot
                    sch.op(qeng, lambda e, m=m, oc=oc, slot=slot: e.dma_start(
                        out=pk_t[:, slot, :],
                        in_=pk_d[m].rearrange("(c p) j -> c p j", p=128)[oc]), inc=pksem, n=16)
                    c_ld = sch.op(qeng, lambda e, m=m, oc=oc, slot=slot: e.dma_start(
                        out=sc_t[:, slot, :],
                        in_=sc_d[m].rearrange("(c p) j -> c p j", p=128)[oc]), inc=pksem, n=16)
                    sch.wait("vector", pksem, c_ld)
                    if len(dq_hist) >= 3:
                        sch.wait("vector", "pe", tp_done[dq_hist[-3]])
                    pkap = pk_t[:, slot, :]
                    sc_b = sc_t.rearrange("p s (b one) -> p s b one", one=1)[:, slot].broadcast_to([128, 64, 16])
                    c_dq = 0
                    for par in (0, 1):
                        if par == 0:
                            sch.op("vector", lambda e, pkap=pkap: e.tensor_scalar(v0l[:], pkap, 7, 22, ALU.bitwise_and, ALU.logical_shift_left))
                            sch.op("vector", lambda e, pkap=pkap: e.tensor_scalar(sl_t[:], pkap, 8, 28, ALU.bitwise_and, ALU.logical_shift_left))
                        else:
                            sch.op("vector", lambda e, pkap=pkap: e.tensor_scalar(v0l[:], pkap, 4, None, ALU.logical_shift_right))
                            sch.op("vector", lambda e: e.tensor_scalar(v0l[:], v0l[:], 7, 22, ALU.bitwise_and, ALU.logical_shift_left))
                            sch.op("vector", lambda e, pkap=pkap: e.tensor_scalar(sl_t[:], pkap, 128, 24, ALU.bitwise_and, ALU.logical_shift_left))
                        sch.op("vector", lambda e: e.tensor_scalar(v0l[:], v0l[:], C_EXP, None, ALU.add))
                        v0f = v0l.bitcast(F32)
                        sch.op("vector", lambda e, v0f=v0f: e.tensor_scalar(v8_t[:], v0f[:], 1.0, 0.0, ALU.subtract, ALU.min))
                        sch.op("vector", lambda e, v0f=v0f: e.tensor_add(v8_t[:], v0f[:], v8_t[:]))
                        sch.op("vector", lambda e: e.tensor_tensor(v8_t.bitcast(I32)[:], v8_t.bitcast(I32)[:], sl_t[:], ALU.bitwise_xor))
                        natv = nat.rearrange("p s (b g two) -> p s b g two", two=2, g=16)[:, slot, :, :, par]
                        c_dq = sch.op("vector", lambda e, natv=natv, sc_b=sc_b: e.tensor_tensor(
                            natv, v8_t.rearrange("p (b g) -> p b g", g=16), sc_b, ALU.mult),
                            inc="ve" if par == 1 else None)
                    dq_done[gl] = c_dq
                    sch.wait("tensor", "ve", c_dq)
                    if len(dq_hist) == 1:
                        sch.wait("tensor", "pre", PRE_ALL)
                    for grp in range(2):
                        bank = grp
                        if len(dq_hist) >= 2:
                            sch.wait("tensor", "ac", pt_evac[(dq_hist[-2], grp)])
                        c_tp = 0
                        for kk in range(8):
                            k = grp * 8 + kk
                            c_tp = sch.op("tensor", lambda e, slot=slot, k=k, kk=kk, bank=bank: e.transpose(
                                PT[bank][:, kk * 128:(kk + 1) * 128],
                                nat[:, slot, k * 128:(k + 1) * 128], eyeb[:]),
                                inc="pe" if kk == 7 else None)
                        sch.wait("scalar", "pe", c_tp)
                        if len(dq_hist) >= 3:
                            sch.wait("scalar", "st%d" % slot, stage_dma[dq_hist[-3]])
                        pt_evac[(gl, grp)] = sch.op("scalar", lambda e, slot=slot, grp=grp, bank=bank: e.copy(
                            stage[:, slot, grp * 8:(grp + 1) * 8, :],
                            PT[bank].rearrange("p (g q) -> p g q", q=128)),
                            inc="ac")
                        if grp == 1:
                            tp_done[gl] = c_tp
                    sch.wait("scalar", "ac", pt_evac[(gl, 1)])
                    stage_dma[gl] = sch.op("scalar", lambda e, m=m, oc=oc, slot=slot: e.dma_start(
                        out=w_in[m][oc], in_=stage[:, slot]), inc="st%d" % slot, n=16)

                def emit_cc(m, q):
                    sch.wait("gpsimd", "st0", sch.val("st0"))
                    sch.wait("gpsimd", "st1", sch.val("st1"))
                    rows = 4 * 128 * 16
                    c = sch.op("gpsimd", lambda e, m=m, q=q, rows=rows: e.collective_compute(
                        "AllGather", ALU.bypass,
                        replica_groups=[list(range(NCORES))],
                        ins=[w_in[m].rearrange("a p k q -> (a p k) q")[q * rows:(q + 1) * rows, :]],
                        outs=[w_out[m][q].rearrange("n a p k q -> (n a p k) q")]),
                        inc="cc")
                    assert c == cc_at[(m, q)], (m, q, c, cc_at[(m, q)])

                for m in ("g", "u"):
                    for oc in range(4):
                        emit_dq_chunk(m, oc, (0 if m == "g" else 16) + oc, "sync")
                    emit_cc(m, 0)

                rest = []
                for q in range(1, 4):
                    rest += [("g", oc) for oc in range(4 * q, 4 * q + 4)]
                    rest += [("u", oc) for oc in range(4 * q, 4 * q + 4)]
                rest += [("d", oc) for oc in range(16)]

                CPH = 3

                def dq_hook(gidx):
                    for j in range(CPH * gidx, min(CPH * (gidx + 1), len(rest))):
                        m, oc = rest[j]
                        base = {"g": 0, "u": 16, "d": 32}[m]
                        emit_dq_chunk(m, oc, base + oc, "scalar")
                        if oc % 4 == 3:
                            emit_cc(m, oc // 4)

                emit_ffn_app(sch, sq_sb, wr, g1, first=True, ln_done=ffn1_ln, dq_hook=dq_hook)
                sch.flush(blk)
        cc_all = ctr.get("cc", 0)


        if debug:
            with nc.Block() as blk:
                sch = Sched(ctr, sems)
                sch.op("sync", lambda e: e.dma_start(
                    out=dbg["x1T"].rearrange("(c p) t -> p c t", p=128), in_=xt[:]), inc="ld", n=16)
                sch.op("sync", lambda e: e.dma_start(
                    out=dbg["yT"].rearrange("(c p) t -> p c t", p=128), in_=y[:]), inc="ld", n=16)
                sch.op("sync", lambda e: e.dma_start(out=dbg["wg"][:], in_=w_out["g"][3]), inc="ld", n=16)
                c = sch.op("sync", lambda e: e.dma_start(out=dbg["wd"][:], in_=w_out["d"][3]), inc="ld", n=16)
                sch.wait("sync", "ld", c)
                sch.flush(blk)

        TQ = 256
        with ExitStack() as c2:
            wra = c2.enter_context(nc.sbuf_tensor("wra", [128, 2, 16, 128], BF16))
            cqb = c2.enter_context(nc.sbuf_tensor("cqb", [128, 2, 4, TQ], BF16))
            ckvb = c2.enter_context(nc.sbuf_tensor("ckvb", [128, 2, 4, TQ], BF16))
            qcb = c2.enter_context(nc.sbuf_tensor("qcb", [128, 2, 16, TQ], BF16))
            kcb = c2.enter_context(nc.sbuf_tensor("kcb", [128, 2, 16, TQ], BF16))
            qrb = c2.enter_context(nc.sbuf_tensor("qrb", [128, 2, 8, TQ], BF16))
            krb = c2.enter_context(nc.sbuf_tensor("krb", [128, 2, 8, TQ], BF16))
            nm16 = c2.enter_context(nc.sbuf_tensor("nm16", [128, 16], F32))
            rs16 = c2.enter_context(nc.sbuf_tensor("rs16", [128, 16], F32))
            rs16b = c2.enter_context(nc.sbuf_tensor("rs16b", [128, 16], F32))
            ao = c2.enter_context(nc.sbuf_tensor("ao", [128, 16, TQ], BF16))

            qr0 = shF[:, 0:TQ]
            rtm1 = shF[:, TQ:2 * TQ]
            rtm2 = shF[:, 2 * TQ:3 * TQ]
            sq_base = 3 * TQ
            tmp_off_attn = 5 * TQ
            o_t = sh[:, 4096:6144]
            q_t = sh[:, 6144:9216]
            k_t = sh[:, 9216:12288]
            v_t = sh[:, 12288:14336]
            rope = sh[:, 14336:14848]
            sc256 = shF[:, 7424:7680]
            es256 = shF[:, 7680:7936]
            at256 = sh[:, 15872:16128]

            def y1s(q):
                return q % 2
            q192 = q_t.rearrange("p (i d) -> p i d", d=192)
            k192 = k_t.rearrange("p (i d) -> p i d", d=192)

            proj_specs = [
                ("cq", "Wc", "bc", "y", 16, 4),
                ("ckv", "Wckv", "bckv", "y", 16, 4),
                ("qc", "Wcq", "bcq", "cq", 4, 16),
                ("qr", "Wqr", "bqr", "cq", 4, 8),
                ("kc", "Wck", "bck", "ckv", 4, 16),
                ("kr", "Wkr", "bkr", "y", 16, 8),
                ("v", "Wv", "bv", "ckv", 4, 16),
            ]

            with nc.Block() as blk:
                sch = Sched(ctr, sems)

                sch.wait("sync", "gd", 8 * 16)
                wa_hist = []
                bank_free = {}
                tp_ev = {}
                rot_free = [0]
                prev_av = [0]
                ln_chain = [0, 0]
                qst = {}

                def wa_load(fn):
                    slot = len(wa_hist) % 2
                    if len(wa_hist) >= 2:
                        sch.wait("sync", "pe", wa_hist[-2][1])
                    c_w = sch.op("sync", lambda e, fn=fn, slot=slot: fn(e, slot), inc="wa%d" % slot, n=16)
                    ent = [c_w, None]
                    wa_hist.append(ent)
                    return slot, c_w, ent

                def s1(q):
                    t0 = q * TQ
                    ys = y1s(q)
                    st = {"t0": t0, "ys": ys, "dst_ready": {}}
                    qst[q] = st
                    sch.wait("sync", "ve", rot_free[0])
                    c_rl = sch.op("sync", lambda e, t0=t0: e.dma_start(
                        out=rope.rearrange("p (c t) -> p c t", c=2),
                        in_=rope_d.rearrange("p (c t) -> p c t", c=2)[:, :, t0:t0 + TQ]), inc="pre", n=16)
                    sch.wait("vector", "pre", c_rl)
                    sch.wait("scalar", "ve", prev_av[0])
                    ln_guard = max(ln_chain[0], bank_free.get(4, 0), bank_free.get(5, 0)) if bank_free else ln_chain[0]
                    ln_done, stats = emit_ln_half(sch, t0, ys * TQ, None, tw=TQ,
                                                  ve_guard=ln_guard, pe_guard=ln_chain[1],
                                                  sq_ap=lambda slot: shF[:, sq_base + slot * TQ: sq_base + (slot + 1) * TQ],
                                                  tmp_off=tmp_off_attn)
                    ln_chain[0] = ln_done
                    ln_chain[1] = stats
                    st["ln_done"] = ln_done
                    dst_ready = st["dst_ready"]

                    def rhs_of(src, k):
                        if src == "y":
                            return y[:, k, ys * TQ:(ys + 1) * TQ]
                        if src == "cq":
                            return cqb[:, ys, k, :]
                        return ckvb[:, ys, k, :]

                    pb = [0]
                    for (pname, wname, bname, ksrc, kch, mch) in proj_specs:
                        for m in range(mch):
                            slot, c_w, ent = wa_load(
                                lambda e, slot, wname=wname, kch=kch, m=m: e.dma_start(
                                    out=wra[:, slot, 0:kch, :],
                                    in_=aw_bf[wname].rearrange("(k p) n -> p k n", p=128)[:, :, m * 128:(m + 1) * 128]))
                            bank = 2 + (pb[0] % 4)
                            pb[0] += 1
                            sch.wait("tensor", "wa%d" % slot, c_w)
                            if pname == "cq" and m == 0:
                                sch.wait("tensor", "ve", ln_done)
                            if bank in bank_free:
                                sch.wait("tensor", "ve", bank_free[bank])
                            if pname in ("qc", "qr") and m == 0:
                                sch.wait("tensor", "ve", dst_ready[("cq", 3)])
                            if pname in ("kc", "v") and m == 0:
                                sch.wait("tensor", "ve", dst_ready[("ckv", 3)])
                            grp = 0
                            for k in range(kch):
                                grp = sch.op("tensor", lambda e, slot=slot, k=k, ksrc=ksrc, bank=bank: e.matmul(
                                    P[bank][:, 0:TQ], wra[:, slot, k, :], rhs_of(ksrc, k),
                                    start=(k == 0), stop=(k == kch - 1)),
                                    inc="pe" if k == kch - 1 else None)
                            ent[1] = grp
                            sch.wait("vector", "pe", grp)
                            if pname in ("qr", "kr"):
                                c1v = sch.op("vector", lambda e, bank=bank, bname=bname, m=m: e.tensor_scalar(
                                    qr0[:], P[bank][:, 0:TQ], bias_ap(bname, m), None, ALU.add), inc="ve")
                                bank_free[bank] = c1v
                                sch.wait("tensor", "ve", c1v)
                                sch.wait("tensor", "ve", rot_free[0])
                                vm, am = tp_ev.get(0, (0, 0))
                                sch.wait("tensor", "ve", vm)
                                sch.wait("tensor", "ac", am)
                                c_rot = sch.op("tensor", lambda e: e.matmul(
                                    PTf[0][:, 0:TQ], rotm[:], qr0[:], start=True, stop=True), inc="pe")
                                sch.wait("vector", "pe", c_rot)
                                cosap = rope[:, 0:TQ]
                                sinap = rope[:, TQ:2 * TQ]
                                sch.op("vector", lambda e, cosap=cosap: e.tensor_tensor(rtm1[:], qr0[:], cosap, ALU.mult))
                                sch.op("vector", lambda e, sinap=sinap: e.tensor_tensor(rtm2[:], PTf[0][:, 0:TQ], sinap, ALU.mult))
                                dstap = (qrb if pname == "qr" else krb)[:, ys, m, :]
                                c2v = sch.op("vector", lambda e, dstap=dstap: e.tensor_add(dstap, rtm1[:], rtm2[:]), inc="ve")
                                rot_free[0] = c2v
                                dst_ready[(pname, m)] = c2v
                            else:
                                if pname == "v":
                                    dstap = y[:, m, (2 + ys) * TQ:(3 + ys) * TQ]
                                else:
                                    dstap = {"cq": cqb, "ckv": ckvb, "qc": qcb, "kc": kcb}[pname][:, ys, m, :]
                                c1v = sch.op("vector", lambda e, dstap=dstap, bank=bank, bname=bname, m=m: e.tensor_scalar(
                                    dstap, P[bank][:, 0:TQ], bias_ap(bname, m), None, ALU.add), inc="ve")
                                bank_free[bank] = c1v
                                dst_ready[(pname, m)] = c1v
                    st["proj_all"] = sch.val("ve")

                def tgroup(srcs, dst_builder, gi, split4=False):
                    bank = gi % 2
                    vm, am = tp_ev.get(bank, (0, 0))
                    sch.wait("tensor", "ve", vm)
                    sch.wait("tensor", "ac", am)
                    if bank == 0:
                        sch.wait("tensor", "ve", rot_free[0])
                    c_tp = 0
                    for idx, (srcap,) in enumerate(srcs):
                        c_tp = sch.op("tensor", lambda e, srcap=srcap, idx=idx, bank=bank: e.transpose(
                            PT[bank][:, idx * 128:(idx + 1) * 128], srcap, eyeb[:]),
                            inc="pe" if idx == 7 else None)
                    eng = "scalar"
                    sch.wait(eng, "pe", c_tp)
                    if prev_av[0]:
                        sch.wait(eng, "ve", prev_av[0])
                    if split4:
                        src_ap = PT[bank].rearrange("p (g two d) -> p g two d", two=2, d=64)
                    else:
                        src_ap = PT[bank].rearrange("p (g q) -> p g q", q=128)
                    if eng == "vector":
                        c = sch.op(eng, lambda e, b=dst_builder, s=src_ap: e.tensor_copy(b(), s), inc="ve")
                        tp_ev[bank] = (c, am)
                    else:
                        c = sch.op(eng, lambda e, b=dst_builder, s=src_ap: e.copy(b(), s), inc="ac")
                        tp_ev[bank] = (vm, c)
                    return tp_ev[bank]

                def s2(q):
                    st = qst[q]
                    ys = st["ys"]
                    for sti in range(TQ // 128):
                        s0 = sti * 128
                        sch.wait("tensor", "ve", st["proj_all"])
                        gi = 0
                        marks = []
                        for half in range(2):
                            marks.append(tgroup([(qcb[:, ys, half * 8 + j, s0:s0 + 128],) for j in range(8)],
                                                lambda half=half: q192[:, half * 8:(half + 1) * 8, 0:128], gi))
                            gi += 1
                        marks.append(tgroup([(qrb[:, ys, j, s0:s0 + 128],) for j in range(8)],
                                            lambda: q192[:, :, 128:192].rearrange("p (a two) d -> p a two d", two=2),
                                            gi, split4=True))
                        gi += 1
                        for half in range(2):
                            marks.append(tgroup([(kcb[:, ys, half * 8 + j, s0:s0 + 128],) for j in range(8)],
                                                lambda half=half: k192[:, half * 8:(half + 1) * 8, 0:128], gi))
                            gi += 1
                        marks.append(tgroup([(krb[:, ys, j, s0:s0 + 128],) for j in range(8)],
                                            lambda: k192[:, :, 128:192].rearrange("p (a two) d -> p a two d", two=2),
                                            gi, split4=True))
                        gi += 1
                        for half in range(2):
                            marks.append(tgroup([(y[:, half * 8 + j, (2 + ys) * TQ + s0:(2 + ys) * TQ + s0 + 128],) for j in range(8)],
                                                lambda half=half: v_t.rearrange("p (j d) -> p j d", d=128)[:, half * 8:(half + 1) * 8, :], gi))
                            gi += 1
                        for vm, am in marks:
                            sch.wait("vector", "ve", vm)
                            sch.wait("vector", "ac", am)
                        prod = sh[:, 0:3072].rearrange("p (j d) -> p j d", d=192)
                        c_sc = 0
                        for i in range(H):
                            qi = q192[:, i:i + 1, :].broadcast_to([128, H, 192])
                            sch.op("vector", lambda e, qi=qi, prod=prod: e.tensor_tensor(prod, qi, k192, ALU.mult))
                            c_sc = sch.op("vector", lambda e, i=i, prod=prod: e.tensor_reduce(
                                sc256[:, i * 16:(i + 1) * 16], prod, AX.X, ALU.add),
                                inc="ve" if i == H - 1 else None)
                        sch.wait("vector", "ve", c_sc)
                        sc3 = sc256.rearrange("p (i j) -> p i j", j=16)
                        c_r1 = sch.op("vector", lambda e, sc3=sc3: e.tensor_reduce(nm16[:], sc3, AX.X, ALU.max, negate=True), inc="ve")
                        sch.wait("vector", "ve", c_r1)
                        c_sm = sch.op("vector", lambda e, sc3=sc3: e.tensor_tensor(
                            sc3, sc3,
                            nm16.rearrange("p (i one) -> p i one", one=1).broadcast_to([128, 16, 16]),
                            ALU.add), inc="ve")
                        sch.wait("scalar", "ve", c_sm)
                        c_exp = sch.op("scalar", lambda e: e.activation(es256[:], sc256[:], AF.Exp, scale=SCALER), inc="ac")
                        sch.wait("vector", "ac", c_exp)
                        es3 = es256.rearrange("p (i j) -> p i j", j=16)
                        c_r2 = sch.op("vector", lambda e, es3=es3: e.tensor_reduce(rs16[:], es3, AX.X, ALU.add), inc="ve")
                        sch.wait("vector", "ve", c_r2)
                        c_r3 = sch.op("vector", lambda e: e.reciprocal(rs16b[:], rs16[:]), inc="ve")
                        sch.wait("vector", "ve", c_r3)
                        at3 = at256.rearrange("p (i j) -> p i j", j=16)
                        sch.op("vector", lambda e, es3=es3, at3=at3: e.tensor_tensor(
                            at3, es3,
                            rs16b.rearrange("p (i one) -> p i one", one=1).broadcast_to([128, 16, 16]),
                            ALU.mult))
                        vv = v_t.rearrange("p (j d) -> p d j", d=128)
                        prod2 = sh[:, 0:2048].rearrange("p (d j) -> p d j", j=16)

                        def av_reduce(e, i, prod2):
                            with nc.allow_low_precision(reason="attn@v: fp32 accum, bf16 matmul operand out"):
                                return e.tensor_reduce(o_t[:, i * 128:(i + 1) * 128], prod2, AX.X, ALU.add)

                        c_av = 0
                        for i in range(H):
                            ai = at3[:, i:i + 1, :].broadcast_to([128, 128, 16])
                            sch.op("vector", lambda e, ai=ai, vv=vv, prod2=prod2: e.tensor_tensor(prod2, ai, vv, ALU.mult))
                            c_av = sch.op("vector", lambda e, i=i, prod2=prod2: av_reduce(e, i, prod2),
                                          inc="ve" if i == H - 1 else None)
                        prev_av[0] = c_av
                        sch.wait("tensor", "ve", c_av)
                        for half in range(2):
                            bank = half
                            vm, am = tp_ev.get(bank, (0, 0))
                            sch.wait("tensor", "ve", vm)
                            sch.wait("tensor", "ac", am)
                            if bank == 0:
                                sch.wait("tensor", "ve", rot_free[0])
                            c_tp = 0
                            for idx in range(8):
                                mch = half * 8 + idx
                                c_tp = sch.op("tensor", lambda e, mch=mch, idx=idx, bank=bank: e.transpose(
                                    PT[bank][:, idx * 128:(idx + 1) * 128],
                                    o_t[:, mch * 128:(mch + 1) * 128], eyeb[:]),
                                    inc="pe" if idx == 7 else None)
                            eng = "scalar"
                            sch.wait(eng, "pe", c_tp)
                            sch.wait(eng, "ve", c_av)
                            dstb = (lambda half=half, s0=s0: ao[:, half * 8:(half + 1) * 8, s0:s0 + 128])
                            src_ap = PT[bank].rearrange("p (g q) -> p g q", q=128)
                            if eng == "vector":
                                c = sch.op(eng, lambda e, dstb=dstb, s=src_ap: e.tensor_copy(dstb(), s), inc="ve")
                                tp_ev[bank] = (c, am)
                            else:
                                c = sch.op(eng, lambda e, dstb=dstb, s=src_ap: e.copy(dstb(), s), inc="ac")
                                tp_ev[bank] = (vm, c)
                    st["ao_marks"] = (tp_ev[0][0], tp_ev[1][1])

                def s3(q):
                    st = qst[q]
                    t0 = st["t0"]
                    sch.wait("tensor", "ve", st["ao_marks"][0])
                    sch.wait("tensor", "ac", st["ao_marks"][1])
                    for m in range(16):
                        slot, c_w, ent = wa_load(
                            lambda e, slot, m=m: e.dma_start(
                                out=wra[:, slot, :, :],
                                in_=aw_bf["Wo"].rearrange("(k p) n -> p k n", p=128)[:, :, m * 128:(m + 1) * 128]))
                        bank = 2 + (m % 2)
                        sch.wait("tensor", "wa%d" % slot, c_w)
                        if bank in bank_free:
                            sch.wait("tensor", "ve", bank_free[bank])
                        grp = 0
                        for k in range(KCH):
                            grp = sch.op("tensor", lambda e, slot=slot, k=k, bank=bank: e.matmul(
                                P[bank][:, 0:TQ], wra[:, slot, k, :], ao[:, k, :],
                                start=(k == 0), stop=(k == KCH - 1)),
                                inc="pe" if k == KCH - 1 else None)
                        ent[1] = grp
                        sch.wait("vector", "pe", grp)
                        c_r = sch.op("vector", lambda e, m=m, bank=bank, t0=t0: e.scalar_tensor_tensor(
                            xt[:, m, t0:t0 + TQ], P[bank][:, 0:TQ], bias_ap("bo", m),
                            xt[:, m, t0:t0 + TQ], ALU.add, ALU.add), inc="ve")
                        bank_free[bank] = c_r

                s1(0)
                for q in range(4):
                    if q < 3:
                        s1(q + 1)
                    s2(q)
                    s3(q)
                sch.flush(blk)

        if debug:
            with nc.Block() as blk:
                sch = Sched(ctr, sems)
                c = sch.op("sync", lambda e: e.dma_start(
                    out=dbg["x2T"].rearrange("(c p) t -> p c t", p=128), in_=xt[:]), inc="ld", n=16)
                sch.wait("sync", "ld", c)
                sch.flush(blk)

        with ExitStack() as c3:
            sq_sb = c3.enter_context(nc.sbuf_tensor("sq3", [128, 2, 512], F32))
            wr = {m: c3.enter_context(nc.sbuf_tensor(f"ring3{m}", [128, 2, 16, 128], BF16)) for m in ("g", "u", "d")}
            g1 = c3.enter_context(nc.sbuf_tensor("g1b", [128, 2, 512], F32))
            with nc.Block() as blk:
                sch = Sched(ctr, sems)
                emit_ffn_app(sch, sq_sb, wr, g1, first=False)
                sch.flush(blk)

        with ExitStack() as c4:
            ost = c4.enter_context(nc.sbuf_tensor("ost", [128, 2, D], F32))
            eyef = c4.enter_context(nc.sbuf_tensor("eyef_sb", [128, 128], F32))
            with nc.Block() as blk:
                sch = Sched(ctr, sems)
                c_ef = sch.op("sync", lambda e: e.dma_start(out=eyef[:], in_=eyef_d[:]), inc="pre", n=16)
                sch.wait("tensor", "pre", c_ef)
                ost_dma = {}
                pt_free = {}
                for st8 in range(8):
                    slot = st8 % 2
                    last_ev = 0
                    for grp4 in range(4):
                        bank = grp4 % 2
                        if bank in pt_free:
                            sch.wait("tensor", "ac", pt_free[bank])
                        c_tp = 0
                        for idx in range(4):
                            mch = grp4 * 4 + idx
                            c_tp = sch.op("tensor", lambda e, mch=mch, idx=idx, st8=st8, bank=bank: e.transpose(
                                PTf[bank][:, idx * 128:(idx + 1) * 128],
                                xt[:, mch, st8 * 128:(st8 + 1) * 128], eyef[:]),
                                inc="pe" if idx == 3 else None)
                        sch.wait("scalar", "pe", c_tp)
                        if st8 >= 2:
                            sch.wait("scalar", "os%d" % slot, ost_dma[st8 - 2])
                        last_ev = sch.op("scalar", lambda e, grp4=grp4, slot=slot, bank=bank: e.copy(
                            ost[:, slot, grp4 * 512:(grp4 + 1) * 512], PTf[bank][:, :]), inc="ac")
                        pt_free[bank] = last_ev
                    sch.wait("sync", "ac", last_ev)
                    ost_dma[st8] = sch.op("sync", lambda e, st8=st8, slot=slot: e.dma_start(
                        out=out_d[st8 * 128:(st8 + 1) * 128, :], in_=ost[:, slot]), inc="os%d" % slot, n=16)
                sch.wait("sync", "os0", sch.val("os0"))
                sch.wait("sync", "os1", sch.val("os1"))
                sch.flush(blk)

    return nc


_CACHE = {}


def _host_constants():
    j = np.arange(32, dtype=np.float32)
    inv_freq = (1.0 / (10000.0 ** (j / np.float32(32.0)))).astype(np.float32)
    dh = np.arange(128) % 64
    fr = inv_freq[dh % 32]
    R = np.zeros((128, 128), np.float32)
    for h2 in range(2):
        b = h2 * 64
        for dd in range(32):
            R[b + dd + 32, b + dd] = -1.0
            R[b + dd, b + dd + 32] = 1.0
    return fr, R


def kernel(**inputs):
    inp = {k: np.asarray(v) for k, v in inputs.items()}
    h_t = np.asarray(inp["L0_attn_h_t"], np.float32).reshape(S, D)

    fr, R = _host_constants()
    eye_bf = np.eye(128, dtype=ml_dtypes.bfloat16)
    eye_f = np.eye(128, dtype=np.float32)

    def pack_bias(v, chunks):
        return np.ascontiguousarray(np.asarray(v, np.float32).reshape(chunks, 128).T)

    bias_host = np.ascontiguousarray(np.concatenate([
        pack_bias(inp["L0_attn_bc"], 4),
        pack_bias(inp["L0_attn_bcq"], 16),
        pack_bias(inp["L0_attn_bqr"], 8),
        pack_bias(inp["L0_attn_bckv"], 4),
        pack_bias(inp["L0_attn_bck"], 16),
        pack_bias(inp["L0_attn_bkr"], 8),
        pack_bias(inp["L0_attn_bv"], 16),
        pack_bias(inp["L0_attn_bo"], 16),
    ], axis=1))

    if "nc" not in _CACHE:
        _CACHE["nc"] = build_program(debug=False)
    nc = _CACHE["nc"]

    in_maps = []
    for c in range(NCORES):
        t0 = c * T
        tpos = np.arange(t0, t0 + T, dtype=np.float32)
        ang = tpos[None, :] * fr[:, None]
        rope_cs = np.ascontiguousarray(
            np.concatenate([np.cos(ang), np.sin(ang)], axis=1).astype(ml_dtypes.bfloat16))
        m = {
            "xT": np.ascontiguousarray(h_t[t0:t0 + T, :].T),
            "biases": bias_host,
            "rope_cs": rope_cs,
            "rotm": R,
            "eye_bf": eye_bf,
            "eye_f32": eye_f,
            "Wc": np.asarray(inp["L0_attn_Wc"], np.float32),
            "Wcq": np.asarray(inp["L0_attn_Wcq"], np.float32),
            "Wqr": np.asarray(inp["L0_attn_Wqr"], np.float32),
            "Wckv": np.asarray(inp["L0_attn_Wckv"], np.float32),
            "Wck": np.asarray(inp["L0_attn_Wck"], np.float32),
            "Wkr": np.asarray(inp["L0_attn_Wkr"], np.float32),
            "Wv": np.asarray(inp["L0_attn_Wv"], np.float32),
            "Wo": np.asarray(inp["L0_attn_Wo"], np.float32),
            "g_pk": np.ascontiguousarray(inp["L0_ffn_gate_W_packed"][c * 2048:(c + 1) * 2048, :]).astype(np.int32),
            "g_sc": np.ascontiguousarray(inp["L0_ffn_gate_scales"][c * 2048:(c + 1) * 2048, :]).astype(np.float32),
            "u_pk": np.ascontiguousarray(inp["L0_ffn_up_W_packed"][c * 2048:(c + 1) * 2048, :]).astype(np.int32),
            "u_sc": np.ascontiguousarray(inp["L0_ffn_up_scales"][c * 2048:(c + 1) * 2048, :]).astype(np.float32),
            "d_pk": np.ascontiguousarray(inp["L0_ffn_down_W_packed"][:, c * 1024:(c + 1) * 1024]).astype(np.int32),
            "d_sc": np.ascontiguousarray(inp["L0_ffn_down_scales"][:, c * 64:(c + 1) * 64]).astype(np.float32),
        }
        in_maps.append(m)

    res = run_bass_kernel_spmd(nc, in_maps, core_ids=list(range(NCORES)))
    out = np.concatenate([np.asarray(res.results[c]["out"]) for c in range(NCORES)], axis=0)
    return out.reshape(1, S, D).astype(np.float32)
